# version 03
# baseline (speedup 1.0000x reference)
import numpy as np
import ml_dtypes
from contextlib import ExitStack

import concourse.bass as bass
import concourse.mybir as mybir
from concourse.alu_op_type import AluOpType as OP
from concourse.bass_utils import run_bass_kernel_spmd

F32 = mybir.dt.float32
F16 = mybir.dt.float16
BF16 = mybir.dt.bfloat16
I32 = mybir.dt.int32
FP8 = mybir.dt.float8e4
AF = mybir.ActivationFunctionType
AX = mybir.AxisListType
BFnp = ml_dtypes.bfloat16

T = 2048
D = 2048
NTC = T // 128
NKC = D // 128
I_FFN = 5632
M_GUP = 2 * I_FFN
DIN, DSTATE, NH, HDIM, CHUNK = 4096, 128, 64, 64, 256
CONVD = DIN + 2 * DSTATE
NCH = T // CHUNK
MPROJ = 2 * (DIN + DSTATE) + NH
EPS = 1e-5

W1R = M_GUP // 8
W2R = D // 8
IPC = D // 8
OPC = DIN // 8
W8C = D // 8

O_W1 = 0
O_W2 = O_W1 + D * W1R
O_IP = O_W2 + I_FFN * W2R
O_OP = O_IP + IPC * MPROJ
O_W8 = O_OP + OPC * D
RANKB_ELEMS = O_W8 + W8C * D


class Q:

    def __init__(self, eng, sem, step=1, extra=None):
        self.eng = eng
        self.sem = sem
        self.step = step
        self.cnt = 0
        self.seen = {}
        self.extra = list(extra or [])
        self.dma = step == 16

    def wait(self, tok):
        if tok is None:
            return
        sem, val = tok
        key = id(sem)
        if self.seen.get(key, 0) >= val:
            return
        self.eng.wait_ge(sem, val)
        self.seen[key] = val

    def waits(self, toks):
        for t in toks:
            self.wait(t)

    def tag(self, inst):
        if self.cnt + self.step > 60000 and self.extra:
            self.sem = self.extra.pop(0)
            self.cnt = 0
        self.cnt += self.step
        inst.then_inc(self.sem, self.step)
        if self.dma:
            self.eng.wait_ge(self.sem, self.cnt)
            self.seen[id(self.sem)] = self.cnt
        return (self.sem, self.cnt)

    def tok(self):
        return (self.sem, self.cnt)


class WSlot:

    def __init__(self, ap):
        self.ap = ap
        self.war = None


class PB:

    def __init__(self, nc, ctx):
        self.nc = nc
        self.ctx = ctx
        sem = lambda n: ctx.enter_context(nc.semaphore(n))
        self.V = Q(nc.vector, sem("sv"))
        self.S = Q(nc.scalar, sem("ss"))
        self.T = Q(nc.tensor, sem("stn"))
        self.G = Q(nc.gpsimd, sem("sgp"))
        self.SY = Q(nc.sync, sem("ssy"))
        self.dG = Q(nc.gpsimd, sem("dg"), 16,
                    extra=[sem("dg2"), sem("dg3")])
        self.dS = Q(nc.sync, sem("dsy"), 16,
                    extra=[sem("dsy2")])
        self.dA = Q(nc.scalar, sem("dact"), 16)
        self.CC = Q(nc.gpsimd, sem("scc"))
        self.pf = [ctx.enter_context(nc.psum_tensor(f"pf{i}", [128, 512], F32))
                   for i in range(6)]
        self.pb = [ctx.enter_context(nc.psum_tensor(f"pb{i}", [128, 1024], BF16))
                   for i in range(2)]
        self.pf_tok = [None] * 6
        self.pb_tok = [None] * 2
        self.pf_i = 0
        self.pb_i = 0

    def sb(self, name, shape, dtype):
        return self.ctx.enter_context(self.nc.sbuf_tensor(name, shape, dtype))

    def next_pf(self):
        i = self.pf_i
        self.pf_i = (self.pf_i + 1) % 6
        return i

    def next_pb(self):
        i = self.pb_i
        self.pb_i = (self.pb_i + 1) % 2
        return i

    def bcast_row(self, dst, n):
        V = self.V
        V.eng.tensor_copy(dst[32:33, 0:n], dst[0:1, 0:n])
        V.eng.tensor_copy(dst[64:65, 0:n], dst[0:1, 0:n])
        t = V.tag(V.eng.tensor_copy(dst[96:97, 0:n], dst[0:1, 0:n]))
        V.wait(t)
        return V.tag(V.eng.stream_shuffle(dst[:, 0:n], dst[:, 0:n], [0] * 32))

    def stream_mm(self, pk_ap, nk, res_fn, src_fn, slots, stream_is_lhsT,
                  pre_toks=(), dq=None):
        TE = self.T
        dq = dq or self.dG
        for tk in pre_toks:
            TE.wait(tk)
        t_mm = None
        for kc in range(nk):
            sl = slots[kc % len(slots)]
            dq.wait(sl.war)
            wt = dq.tag(dq.eng.dma_start(out=sl.ap, in_=src_fn(kc)))
            TE.wait(wt)
            if stream_is_lhsT:
                inst = TE.eng.matmul(pk_ap, sl.ap, res_fn(kc),
                                     start=(kc == 0), stop=(kc == nk - 1))
            else:
                inst = TE.eng.matmul(pk_ap, res_fn(kc), sl.ap,
                                     start=(kc == 0), stop=(kc == nk - 1))
            t_mm = TE.tag(inst)
            sl.war = t_mm
        return t_mm


def build_program(stage):
    nc = bass.Bass("TRN2", target_bir_lowering=False)
    ctx = ExitStack()

    def dram(n, shp, dt, kind=None):
        if kind:
            return nc.dram_tensor(n, shp, dt, kind=kind).ap()
        return nc.dram_tensor(n, shp, dt).ap()

    x_in = dram("x_in", [T, D], F32, "ExternalInput")
    w_in = dram("w_in", [D, 3 * D], F32, "ExternalInput")
    w_outw = dram("w_outw", [D, D], F32, "ExternalInput")
    b_in = dram("b_in", [1, 3 * D], F32, "ExternalInput")
    b_out = dram("b_out", [1, D], F32, "ExternalInput")
    gamma = dram("gamma", [1, D], F32, "ExternalInput")
    beta = dram("beta", [1, D], F32, "ExternalInput")
    cosd = dram("cosd", [T, 1024], BF16, "ExternalInput")
    sind = dram("sind", [T, 1024], BF16, "ExternalInput")
    idn = dram("idn", [128, 128], BF16, "ExternalInput")
    trilm = dram("trilm", [128, 3 * 128], BF16, "ExternalInput")
    smaskd = dram("smaskd", [256, 256], BF16, "ExternalInput")
    w1_rows = dram("w1_rows", [W1R, D], F32, "ExternalInput")
    w2_rows = dram("w2_rows", [W2R, I_FFN], F32, "ExternalInput")
    ip_cols = dram("ip_cols", [MPROJ, IPC], F32, "ExternalInput")
    op_cols = dram("op_cols", [D, OPC], F32, "ExternalInput")
    w8_cols = dram("w8_cols", [D, W8C], FP8, "ExternalInput")
    conv_w = dram("conv_w", [CONVD, 4], F32, "ExternalInput")
    conv_b = dram("conv_b", [1, CONVD], F32, "ExternalInput")
    dt_bias = dram("dt_bias", [1, NH], F32, "ExternalInput")
    a_log = dram("a_log", [1, NH], F32, "ExternalInput")
    d_vec = dram("d_vec", [1, NH], F32, "ExternalInput")
    norm_w = dram("norm_w", [1, DIN], F32, "ExternalInput")
    scal8 = dram("scal8", [1, 2], F32, "ExternalInput")
    masks = dram("masks", [1, 2], F32, "ExternalInput")

    out = dram("out", [T, D], F32, "ExternalOutput")

    cc_a_in = dram("cc_a_in", [1, D], F32)
    cc_a_out = dram("cc_a_out", [1, D], F32)
    cc_h_in = dram("cc_h_in", [128, 34 * 3], F32)
    cc_h_out = dram("cc_h_out", [128, 34 * 3], F32)
    cc_s_in = dram("cc_s_in", [128, NH * HDIM], F32)
    cc_s_out = dram("cc_s_out", [128, NH * HDIM], F32)
    RANKB = RANKB_ELEMS
    cc_w_in = dram("cc_w_in", [1, RANKB], BF16)
    cc_w_out = dram("cc_w_out", [8, RANKB], BF16)
    xbcraw_d = dram("xbcraw_d", [CONVD, T], BF16)
    xconv_d = dram("xconv_d", [DIN, T], BF16)
    res_spill = dram("res_spill", [T, D], BF16)
    x2_d = dram("x2_d", [T, D], BF16)
    x3_d = dram("x3_d", [T, D], BF16)
    cs_d = dram("cs_d", [NCH * 128, NH * HDIM], BF16)
    yd_d = dram("yd_d", [T, NH * HDIM], BF16)

    p = PB(nc, ctx)
    V, S, TE, G, SY = p.V, p.S, p.T, p.G, p.SY
    dG, dS, dA, CC = p.dG, p.dS, p.dA, p.CC

    RES = p.sb("RES", [128, NTC * D], BF16)
    XNF = p.sb("XNF", [128, NKC * T], BF16)
    SCR = p.sb("SCR", [128, 22528], BF16)
    SC2 = p.sb("SC2", [128, 4608], F32)
    WB = p.sb("WB", [128, 4096], BF16)
    COL = p.sb("COL", [128, 80], F32)
    CID = p.sb("CID", [128, 128], BF16)
    CTRI = p.sb("CTRI", [128, 3 * 128], BF16)
    CSM = p.sb("CSM", [128, 2 * 256], BF16)
    DEX = p.sb("DEX", [128, 2176], BF16)

    C_EPS, C_M1, C_R1, C_M2, C_R2 = 0, 1, 3, 4, 6
    C_STAT = 8
    C_SSK, C_SSV, C_RNK, C_RNV, C_RN, C_TMP = 33, 34, 35, 36, 37, 38
    C_ONE = 39
    C_ACOL = 40
    C_BQ = 56

    ws_a = [WSlot(WB[:, 0:512]), WSlot(WB[:, 512:1024])]
    ws_b = [WSlot(WB[:, 1024:1536]), WSlot(WB[:, 1536:2048])]
    ws_c = [WSlot(WB[:, 2048:2560]), WSlot(WB[:, 2560:3072])]
    ws_q = [WSlot(WB[:, 3072:3200]), WSlot(WB[:, 3200:3328])]

    _areas = [(RES, NTC * D), (XNF, NKC * T), (SCR, 22528), (WB, 4096),
              (CID, 128), (CTRI, 3 * 128), (CSM, 2 * 256), (DEX, 2176),
              (SC2, 4608), (COL, 80)]
    _mi = None
    for _arena, _w in _areas:
        for _c0 in range(0, _w, 8192):
            _mi = V.eng.memset(_arena[:, _c0:min(_c0 + 8192, _w)], 0.0)
    tok_init = V.tag(_mi)
    for _q in (V, S, TE, G, SY, dS, dG, dA, CC):
        _q.wait(tok_init)

    ld_id = dS.tag(SY.eng.dma_start(out=CID[:], in_=idn))
    dS.tag(SY.eng.dma_start(out=CTRI[:], in_=trilm))
    dS.tag(SY.eng.dma_start(out=CSM[:, 0:256], in_=smaskd[0:128, :]))
    ld_sm = dS.tag(SY.eng.dma_start(out=CSM[:, 256:512], in_=smaskd[128:256, :]))
    V.eng.memset(COL[:, C_EPS:C_EPS + 1], EPS)
    V.eng.memset(COL[:, C_ONE:C_ONE + 1], 1.0)
    tok_eps = V.tag(V.eng.memset(COL[:, C_TMP:C_TMP + 1], 0.0))
    ld_bq = dS.tag(SY.eng.dma_start(
        out=COL[:, C_BQ:C_BQ + 16],
        in_=b_in.rearrange("o (kc p) -> (o p) kc", p=128)[:, 0:16],
        allow_slow_non_contiguous=True))

    XB = XNF

    def xf32(c0, w):
        return XB[:, c0:c0 + 2 * w].bitcast(F32)

    def xi32(c0, w):
        return XB[:, c0:c0 + 2 * w].bitcast(I32)

    PA0, PA1, PA2, PA3 = 72, 73, 74, 75
    tok_prep_col = V.tag(V.eng.memset(COL[:, 76:77], 0.0))

    def quant_chain(wt_f32, qi_i32, rf_f32, qf_f32, out_bf, n, amax_src=None):
        if amax_src is None:
            t1 = V.tag(V.eng.tensor_reduce(COL[:, PA0:PA0 + 1], wt_f32, AX.X,
                                           OP.max, apply_absolute_value=True))
            V.wait(t1)
            t2 = V.tag(V.eng.tensor_scalar(COL[:, PA1:PA1 + 1], COL[:, PA0:PA0 + 1],
                                           1.0 / 127.0, 1e-8, OP.mult, OP.max))
            V.wait(t2)
            t3 = V.tag(V.eng.reciprocal(COL[:, PA2:PA2 + 1], COL[:, PA1:PA1 + 1]))
            t4 = V.tag(V.eng.tensor_scalar(COL[:, PA3:PA3 + 1], COL[:, PA1:PA1 + 1],
                                           -256.0, None, OP.mult))
            V.waits([t3, t4])
        t5 = V.tag(V.eng.tensor_scalar(qi_i32, wt_f32, COL[:, PA2:PA2 + 1], 256.0,
                                       OP.mult, OP.add))
        V.wait(t5)
        t6 = V.tag(V.eng.scalar_tensor_tensor(rf_f32, wt_f32, COL[:, PA2:PA2 + 1],
                                              qi_i32, OP.mult, OP.subtract))
        V.wait(t6)
        t7 = V.tag(V.eng.scalar_tensor_tensor(qf_f32, rf_f32, -255.5, qi_i32,
                                              OP.is_ge, OP.add))
        V.wait(t7)
        t8 = V.tag(V.eng.tensor_scalar(qf_f32, qf_f32, 129.0, 383.0,
                                       OP.max, OP.min))
        V.wait(t8)
        t9 = V.tag(V.eng.tensor_scalar(out_bf, qf_f32, COL[:, PA1:PA1 + 1],
                                       COL[:, PA3:PA3 + 1], OP.mult, OP.add))
        return t9

    def prep_transpose_store(src_bf, nkt, stage_buf, stage_c0, dst_view):
        V.wait(dS.tok())
        nleft = nkt
        j0 = 0
        while nleft > 0:
            nb = min(8, nleft)
            bi = p.next_pb()
            TE.waits([V.tok(), ld_id, p.pb_tok[bi]])
            for j in range(nb):
                kc = j0 + j
                inst = TE.eng.transpose(p.pb[bi][:, j * 128:(j + 1) * 128],
                                        src_bf[:, kc * 128:(kc + 1) * 128], CID[:])
            t_tr = TE.tag(inst)
            V.wait(t_tr)
            for j in range(nb):
                kc = j0 + j
                inst = V.eng.tensor_copy(
                    stage_buf[:, stage_c0 + kc * 128:stage_c0 + (kc + 1) * 128],
                    p.pb[bi][:, j * 128:(j + 1) * 128])
            p.pb_tok[bi] = V.tag(inst)
            j0 += nb
            nleft -= nb
        SY.wait(V.tok())
        return dS.tag(SY.eng.dma_start(
            out=dst_view,
            in_=stage_buf[:, stage_c0:stage_c0 + nkt * 128].rearrange(
                "p (kc j) -> p kc j", j=128)))

    prep_store_toks = []
    rr1 = cc_w_in[0:1, O_W1:O_W1 + D * W1R].rearrange(
        "o (kc pp m) -> (o pp) kc m", pp=128, m=W1R)
    for rc in range(W1R // 128):
        dS.wait(V.tok())
        t_ld = dS.tag(SY.eng.dma_start(out=xf32(0, 2048),
                                       in_=w1_rows[rc * 128:(rc + 1) * 128, :]))
        V.waits([t_ld, tok_prep_col])
        quant_chain(xf32(0, 2048), xi32(4096, 2048), xf32(8192, 2048),
                    xf32(12288, 2048), XB[:, 16384:18432], 2048)
        t_st = prep_transpose_store(XB[:, 16384:18432], 16, XB, 18432,
                                    rr1[:, :, rc * 128:(rc + 1) * 128])
        prep_store_toks.append(t_st)

    rr2 = cc_w_in[0:1, O_W2:O_W2 + I_FFN * W2R].rearrange(
        "o (ic pp m) -> (o pp) ic m", pp=128, m=W2R)
    for rc in range(W2R // 128):
        dS.wait(V.tok())
        t_ld = dS.tag(SY.eng.dma_start(out=xf32(0, 5632),
                                       in_=w2_rows[rc * 128:(rc + 1) * 128, :]))
        V.wait(t_ld)
        t1 = V.tag(V.eng.tensor_reduce(COL[:, PA0:PA0 + 1], xf32(0, 5632), AX.X,
                                       OP.max, apply_absolute_value=True))
        V.wait(t1)
        t2 = V.tag(V.eng.tensor_scalar(COL[:, PA1:PA1 + 1], COL[:, PA0:PA0 + 1],
                                       1.0 / 127.0, 1e-8, OP.mult, OP.max))
        V.wait(t2)
        t3 = V.tag(V.eng.reciprocal(COL[:, PA2:PA2 + 1], COL[:, PA1:PA1 + 1]))
        t4 = V.tag(V.eng.tensor_scalar(COL[:, PA3:PA3 + 1], COL[:, PA1:PA1 + 1],
                                       -256.0, None, OP.mult))
        V.waits([t3, t4])
        for hh in range(2):
            w0 = hh * 2816
            quant_chain(xf32(0, 5632)[:, w0:w0 + 2816],
                        xi32(11264, 2816), xf32(16896, 2816), xf32(22528, 2816),
                        XB[:, 28160:30976], 2816, amax_src=True)
            t_st = prep_transpose_store(
                XB[:, 28160:30976], 22, SCR, 0,
                rr2[:, hh * 22:(hh + 1) * 22, rc * 128:(rc + 1) * 128])
            prep_store_toks.append(t_st)

    rrip = cc_w_in[0:1, O_IP:O_IP + IPC * MPROJ].rearrange(
        "o (kh pp m) -> (o pp) kh m", pp=128, m=MPROJ)
    for mc in range(67):
        rows = 128 if mc < 66 else MPROJ - 66 * 128
        dG.wait(V.tok())
        dG.wait(TE.tok())
        t_ld = dG.tag(G.eng.dma_start(
            out=XB[0:rows, 0:256], in_=ip_cols[mc * 128:mc * 128 + rows, :]))
        V.wait(t_ld)
        bi = p.next_pb()
        TE.waits([t_ld, ld_id, p.pb_tok[bi], V.tok()])
        for kh in range(2):
            inst = TE.eng.transpose(p.pb[bi][0:128, kh * 128:kh * 128 + rows],
                                    XB[0:rows, kh * 128:(kh + 1) * 128],
                                    CID[0:rows, 0:rows])
        t_tr = TE.tag(inst)
        V.wait(t_tr)
        V.wait(dS.tok())
        for kh in range(2):
            inst = V.eng.tensor_copy(XB[:, 512 + kh * 128:512 + kh * 128 + rows],
                                     p.pb[bi][0:128, kh * 128:kh * 128 + rows])
        p.pb_tok[bi] = V.tag(inst)
        SY.wait(V.tok())
        t_st = dS.tag(SY.eng.dma_start(
            out=rrip[:, :, mc * 128:mc * 128 + rows],
            in_=XB[:, 512:768].rearrange("p (kh j) -> p kh j", j=128)[:, :, 0:rows]))
        prep_store_toks.append(t_st)

    rrop = cc_w_in[0:1, O_OP:O_OP + OPC * D].rearrange(
        "o (kh pp m) -> (o pp) kh m", pp=128, m=D)
    for mc in range(16):
        dG.wait(V.tok())
        dG.wait(TE.tok())
        t_ld = dG.tag(G.eng.dma_start(out=XB[:, 0:512],
                                      in_=op_cols[mc * 128:(mc + 1) * 128, :]))
        V.wait(t_ld)
        bi = p.next_pb()
        TE.waits([t_ld, ld_id, p.pb_tok[bi], V.tok()])
        for kh in range(4):
            inst = TE.eng.transpose(p.pb[bi][:, kh * 128:(kh + 1) * 128],
                                    XB[:, kh * 128:(kh + 1) * 128], CID[:])
        t_tr = TE.tag(inst)
        V.wait(t_tr)
        V.wait(dS.tok())
        for kh in range(4):
            inst = V.eng.tensor_copy(XB[:, 1024 + kh * 128:1024 + (kh + 1) * 128],
                                     p.pb[bi][:, kh * 128:(kh + 1) * 128])
        p.pb_tok[bi] = V.tag(inst)
        SY.wait(V.tok())
        t_st = dS.tag(SY.eng.dma_start(
            out=rrop[:, :, mc * 128:(mc + 1) * 128],
            in_=XB[:, 1024:1536].rearrange("p (kh j) -> p kh j", j=128)))
        prep_store_toks.append(t_st)

    rr8 = cc_w_in[0:1, O_W8:O_W8 + W8C * D].rearrange(
        "o (kh pp m) -> (o pp) kh m", pp=128, m=D)
    s8t = SCR[:, 8192:8320].bitcast(FP8)
    for mc in range(16):
        dS.wait(V.tok())
        dS.wait(TE.tok())
        t_ld = dS.tag(SY.eng.dma_start(out=s8t[:, 0:256],
                                       in_=w8_cols[mc * 128:(mc + 1) * 128, :]))
        V.wait(t_ld)
        t_cv = V.tag(V.eng.tensor_copy(XB[:, 0:256], s8t[:, 0:256]))
        V.wait(t_cv)
        bi = p.next_pb()
        TE.waits([ld_id, p.pb_tok[bi], V.tok()])
        for kh in range(2):
            inst = TE.eng.transpose(p.pb[bi][:, kh * 128:(kh + 1) * 128],
                                    XB[:, kh * 128:(kh + 1) * 128], CID[:])
        t_tr = TE.tag(inst)
        V.wait(t_tr)
        V.wait(dS.tok())
        for kh in range(2):
            inst = V.eng.tensor_copy(XB[:, 512 + kh * 128:512 + (kh + 1) * 128],
                                     p.pb[bi][:, kh * 128:(kh + 1) * 128])
        p.pb_tok[bi] = V.tag(inst)
        SY.wait(V.tok())
        t_st = dS.tag(SY.eng.dma_start(
            out=rr8[:, :, mc * 128:(mc + 1) * 128],
            in_=XB[:, 512:768].rearrange("p (kh j) -> p kh j", j=128)))
        prep_store_toks.append(t_st)

    G.waits(prep_store_toks)
    tok_wblob = CC.tag(G.eng.collective_compute(
        "AllGather", OP.bypass, replica_groups=[list(range(8))],
        ins=[cc_w_in.opt()],
        outs=[cc_w_out.opt()]))
    tok_prep_v = V.tok()
    tok_prep_dS = dS.tok()

    GBG, GBB, TMP1, TMP2, BOUT, BKV, P1S = 0, 2048, 4096, 6144, 8192, 10240, 14336

    dG.wait(tok_prep_v)
    dG.wait(tok_prep_dS)
    dS.wait(tok_prep_v)
    V.wait(tok_prep_dS)
    TE.wait(tok_prep_dS)
    t_g = dG.tag(G.eng.dma_start(out=SCR[0:1, GBG:GBG + D], in_=gamma))
    t_b = dG.tag(G.eng.dma_start(out=SCR[0:1, GBB:GBB + D], in_=beta))
    V.waits([t_g, t_b])
    p.bcast_row(SCR[:, GBG:GBG + D], D)
    tok_gb = p.bcast_row(SCR[:, GBB:GBB + D], D)

    def ln_cols(src_ap, c_m, c_r):
        V.wait(tok_eps)
        for j in range(4):
            t1 = V.tag(V.eng.bn_stats(COL[:, C_STAT + 6 * j:C_STAT + 6 * j + 6],
                                      src_ap[:, 512 * j:512 * (j + 1)]))
        V.wait(t1)
        t2 = V.tag(V.eng.bn_aggr(COL[:, c_m:c_m + 2], COL[:, C_STAT:C_STAT + 24]))
        S.wait(t2)
        t3 = S.tag(S.eng.activation(COL[:, c_r:c_r + 1], COL[:, c_m + 1:c_m + 2],
                                    AF.Ln, bias=COL[:, C_EPS:C_EPS + 1], scale=1.0))
        S.wait(t3)
        return S.tag(S.eng.activation(COL[:, c_r:c_r + 1], COL[:, c_r:c_r + 1],
                                      AF.Exp, scale=-0.5))

    for tc in range(NTC):
        rs = RES[:, tc * D:(tc + 1) * D]
        t_ld = dG.tag(G.eng.dma_start(out=rs, in_=x_in[tc * 128:(tc + 1) * 128, :]))
        V.wait(t_ld)
        t_r1 = ln_cols(rs, C_M1, C_R1)
        xn1o = SCR[:, TMP1:TMP1 + D]
        V.wait(t_r1)
        t_x1o = V.tag(V.eng.tensor_scalar(xn1o, rs, COL[:, C_M1:C_M1 + 1],
                                          COL[:, C_R1:C_R1 + 1], OP.subtract, OP.mult))
        V.wait(t_x1o)
        t_r2 = ln_cols(xn1o, C_M2, C_R2)
        xnp = SCR[:, TMP2:TMP2 + D]
        V.wait(t_r2)
        t_xnp = V.tag(V.eng.tensor_scalar(xnp, xn1o, COL[:, C_M2:C_M2 + 1],
                                          COL[:, C_R2:C_R2 + 1], OP.subtract, OP.mult))
        V.waits([t_xnp, tok_gb])
        t_mg = V.tag(V.eng.tensor_tensor(xnp, xnp, SCR[:, GBG:GBG + D], OP.mult))
        V.wait(t_mg)
        t_xn = V.tag(V.eng.tensor_tensor(xnp, xnp, SCR[:, GBB:GBB + D], OP.add))
        t_xp = V.tag(V.eng.tensor_tensor(rs, rs, xn1o, OP.add))
        TE.waits([t_xn, ld_id])
        for half in range(2):
            bi = p.next_pb()
            TE.wait(p.pb_tok[bi])
            for j in range(8):
                kc = half * 8 + j
                inst = TE.eng.transpose(p.pb[bi][:, j * 128:(j + 1) * 128],
                                        xnp[:, kc * 128:(kc + 1) * 128], CID[:])
            t_tr = TE.tag(inst)
            V.wait(t_tr)
            for j in range(8):
                kc = half * 8 + j
                inst = V.eng.tensor_copy(
                    XNF[:, kc * T + tc * 128: kc * T + (tc + 1) * 128],
                    p.pb[bi][:, j * 128:(j + 1) * 128])
            p.pb_tok[bi] = V.tag(inst)

    tok_xnf = V.tok()
    tok_res0 = V.tok()

    if stage < 1:
        tok_res1 = tok_res0
        fin = ctx.enter_context(nc.semaphore("fin"))
        G.wait(tok_res1)
        n = 0
        for tc in range(NTC):
            G.eng.dma_start(out=out[tc * 128:(tc + 1) * 128, :],
                            in_=RES[:, tc * D:(tc + 1) * D]).then_inc(fin, 16)
            n += 16
        G.eng.wait_ge(fin, n)
        ctx.close()
        return nc

    t_bo = dG.tag(G.eng.dma_start(out=SCR[0:1, BOUT:BOUT + D], in_=b_out))
    V.wait(t_bo)
    p.bcast_row(SCR[:, BOUT:BOUT + D], D)
    t_bkv = dG.tag(G.eng.dma_start(out=SCR[0:1, BKV:BKV + 4096],
                                    in_=b_in[0:1, D:3 * D]))
    V.wait(t_bkv)
    tok_bias = p.bcast_row(SCR[:, BKV:BKV + 4096], 4096)

    KST, VST = P1S, P1S + 2048
    ROTC, ROTS = P1S + 4096, P1S + 5120
    t_z = V.tag(V.eng.memset(SC2[:, 0:D], 0.0))
    V.wait(t_z)

    for tc in range(NTC):
        for half, base in ((0, KST), (1, VST)):
            for mt in range(4):
                bi = p.next_pf()
                pk = p.pf[bi]
                col0 = (1 + half) * D + mt * 512
                t_mm = p.stream_mm(
                    pk[:], NKC,
                    lambda kc: XNF[:, kc * T + tc * 128:kc * T + (tc + 1) * 128],
                    lambda kc, c0=col0: w_in[kc * 128:(kc + 1) * 128, c0:c0 + 512],
                    ws_a, stream_is_lhsT=False,
                    pre_toks=[tok_xnf, p.pf_tok[bi]])
                V.waits([t_mm, tok_bias])
                p.pf_tok[bi] = V.tag(V.eng.tensor_tensor(
                    SCR[:, base + mt * 512:base + (mt + 1) * 512], pk[:],
                    SCR[:, BKV + half * D + mt * 512:BKV + half * D + (mt + 1) * 512],
                    OP.add))
        dS.wait(V.tok())
        t_c = dS.tag(SY.eng.dma_start(out=SCR[:, ROTC:ROTC + 1024],
                                      in_=cosd[tc * 128:(tc + 1) * 128, :]))
        t_s = dS.tag(SY.eng.dma_start(out=SCR[:, ROTS:ROTS + 1024],
                                      in_=sind[tc * 128:(tc + 1) * 128, :]))
        for base, c_ss in ((KST, C_SSK), (VST, C_SSV)):
            kv = SCR[:, base:base + D]
            rot = SCR[:, base:base + 1024]
            S.wait(V.tok())
            t_sq = S.tag(S.eng.activation(SCR[:, TMP1:TMP1 + D], kv, AF.Square,
                                          accum_out=COL[:, c_ss:c_ss + 1]))
            V.waits([t_c, t_s, t_sq])
            tmp = SCR[:, TMP1:TMP1 + 1024]
            cosb = SCR[:, ROTC:ROTC + 1024]
            sinb = SCR[:, ROTS:ROTS + 1024]
            ev_o = SCR[:, TMP2:TMP2 + 1024]
            V.tag(V.eng.tensor_tensor(tmp, rot, cosb, OP.mult))
            V.tag(V.eng.tensor_tensor(ev_o[:, 0:1024:2], rot[:, 1:1024:2],
                                      sinb[:, 0:1024:2], OP.mult))
            t_w = V.tag(V.eng.tensor_tensor(ev_o[:, 1:1024:2], rot[:, 0:1024:2],
                                            sinb[:, 1:1024:2], OP.mult))
            V.wait(t_w)
            V.tag(V.eng.tensor_tensor(tmp[:, 0:1024:2], tmp[:, 0:1024:2],
                                      ev_o[:, 0:1024:2], OP.subtract))
            t_a = V.tag(V.eng.tensor_tensor(tmp[:, 1:1024:2], tmp[:, 1:1024:2],
                                            ev_o[:, 1:1024:2], OP.add))
            V.wait(t_a)
            V.wait(V.tok())
            t_cp = V.tag(V.eng.tensor_copy(rot, tmp))
            V.wait(t_cp)
        S.wait(V.tok())
        S.tag(S.eng.activation(COL[:, C_RNK:C_RNK + 1], COL[:, C_SSK:C_SSK + 1], AF.Ln))
        t_lv = S.tag(S.eng.activation(COL[:, C_RNV:C_RNV + 1], COL[:, C_SSV:C_SSV + 1],
                                      AF.Ln))
        S.wait(t_lv)
        S.tag(S.eng.activation(COL[:, C_RNK:C_RNK + 1], COL[:, C_RNK:C_RNK + 1],
                               AF.Exp, scale=-0.5))
        t_ev = S.tag(S.eng.activation(COL[:, C_RNV:C_RNV + 1], COL[:, C_RNV:C_RNV + 1],
                                      AF.Exp, scale=-0.5))
        V.wait(t_ev)
        t_rn = V.tag(V.eng.tensor_tensor(COL[:, C_RN:C_RN + 1], COL[:, C_RNK:C_RNK + 1],
                                         COL[:, C_RNV:C_RNV + 1], OP.mult))
        V.wait(t_rn)
        t_p = V.tag(V.eng.tensor_tensor(SCR[:, TMP1:TMP1 + D], SCR[:, KST:KST + D],
                                        SCR[:, VST:VST + D], OP.mult))
        V.wait(t_p)
        V.tag(V.eng.scalar_tensor_tensor(SC2[:, 0:D], SCR[:, TMP1:TMP1 + D],
                                         COL[:, C_RN:C_RN + 1], SC2[:, 0:D],
                                         OP.mult, OP.add))
        if stage == 3 and tc == 0:
            fin = ctx.enter_context(nc.semaphore("fin"))
            G.wait(V.tok())
            n = 0
            G.eng.dma_start(out=out[0:128, :], in_=SCR[:, KST:KST + D]).then_inc(fin, 16)
            G.eng.dma_start(out=out[128:256, :], in_=SCR[:, VST:VST + D]).then_inc(fin, 16)
            G.eng.dma_start(out=out[256:384, 0:8], in_=COL[:, C_SSK:C_SSK + 8]).then_inc(fin, 16)
            n = 48
            G.eng.wait_ge(fin, n)
            ctx.close()
            return nc

    tok_apart = V.tok()
    for j in range(4):
        bi = p.next_pf()
        pk = p.pf[bi]
        TE.waits([tok_apart, p.pf_tok[bi], tok_eps])
        t_mm = TE.tag(TE.eng.matmul(pk[0:1, 0:512], COL[:, C_ONE:C_ONE + 1],
                                    SC2[:, j * 512:(j + 1) * 512],
                                    start=True, stop=True))
        V.wait(t_mm)
        p.pf_tok[bi] = V.tag(V.eng.tensor_copy(
            SC2[0:1, 2048 + j * 512:2048 + (j + 1) * 512], pk[0:1, 0:512]))

    t_asum = V.tok()
    SY.wait(t_asum)
    t_aw = dS.tag(SY.eng.dma_start(out=cc_a_in, in_=SC2[0:1, 2048:2048 + D]))
    G.wait(t_aw)
    t_cc = CC.tag(G.eng.collective_compute(
        "AllReduce", OP.add, replica_groups=[[0, 1], [2, 3], [4, 5], [6, 7]],
        ins=[cc_a_in.opt()], outs=[cc_a_out.opt()]))
    G.wait(t_cc)
    t_al = dG.tag(G.eng.dma_start(out=SC2[0:1, 0:D], in_=cc_a_out))
    V.wait(t_al)
    t_af = V.tag(V.eng.tensor_scalar(SC2[0:1, 0:D], SC2[0:1, 0:D], 0.5, 0.5,
                                     OP.mult, OP.add))
    SY.wait(t_af)
    t_aw2 = dS.tag(SY.eng.dma_start(out=cc_a_in, in_=SC2[0:1, 0:D]))
    dS.wait(t_aw2)
    t_ac = dS.tag(SY.eng.dma_start(
        out=COL[:, C_ACOL:C_ACOL + 16],
        in_=cc_a_in.rearrange("o (kc p) -> (o p) kc", p=128),
        allow_slow_non_contiguous=True))

    if stage == 2:
        fin = ctx.enter_context(nc.semaphore("fin"))
        SY.wait(t_ac)
        SY.wait((t_af[0], t_af[1]))
        n = 0
        SY.eng.dma_start(out=out[0:1, :], in_=SC2[0:1, 0:D]).then_inc(fin, 16)
        n += 16
        SY.eng.dma_start(out=out[4:132, 0:16], in_=COL[:, C_ACOL:C_ACOL + 16]).then_inc(fin, 16)
        n += 16
        SY.eng.dma_start(out=out[256:257, :], in_=SC2[0:1, 2048:2048 + D]).then_inc(fin, 16)
        n += 16
        SY.eng.wait_ge(fin, n)
        ctx.close()
        return nc

    AQF = P1S
    for tt_i in range(4):
        t0 = tt_i * 512
        for mc in range(NKC):
            bi = p.next_pf()
            pk = p.pf[bi]
            t_mm = p.stream_mm(
                pk[:, 0:512], NKC,
                lambda kc: XNF[:, kc * T + t0:kc * T + t0 + 512],
                lambda kc, m=mc: w_in[kc * 128:(kc + 1) * 128, m * 128:(m + 1) * 128],
                ws_q, stream_is_lhsT=True,
                pre_toks=[tok_xnf, p.pf_tok[bi]])
            V.waits([t_mm, t_ac, ld_bq])
            p.pf_tok[bi] = V.tag(V.eng.tensor_scalar(
                SCR[:, AQF + mc * 512:AQF + (mc + 1) * 512], pk[:, 0:512],
                COL[:, C_BQ + mc:C_BQ + mc + 1], COL[:, C_ACOL + mc:C_ACOL + mc + 1],
                OP.add, OP.mult))
        tok_aq = V.tok()
        for stc in range(4):
            tc = tt_i * 4 + stc
            for ot in range(4):
                bi = p.next_pf()
                pk = p.pf[bi]
                t_mm = p.stream_mm(
                    pk[:], NKC,
                    lambda kc, s=stc: SCR[:, AQF + kc * 512 + s * 128:
                                          AQF + kc * 512 + (s + 1) * 128],
                    lambda kc, o=ot: w_outw[kc * 128:(kc + 1) * 128,
                                            o * 512:(o + 1) * 512],
                    ws_b, stream_is_lhsT=False,
                    pre_toks=[tok_aq, p.pf_tok[bi]])
                rsl = RES[:, tc * D + ot * 512:tc * D + (ot + 1) * 512]
                V.waits([t_mm, tok_res0, V.tok()])
                t1 = V.tag(V.eng.tensor_tensor(SCR[:, TMP1:TMP1 + 512], pk[:],
                                               SCR[:, BOUT + ot * 512:BOUT + (ot + 1) * 512],
                                               OP.add))
                V.wait(t1)
                t2 = V.tag(V.eng.tensor_tensor(rsl, rsl, SCR[:, TMP1:TMP1 + 512],
                                               OP.add))
                p.pf_tok[bi] = t2

    tok_res1 = V.tok()

    if stage == 1:
        fin = ctx.enter_context(nc.semaphore("fin"))
        G.wait(tok_res1)
        n = 0
        for tc in range(NTC):
            G.eng.dma_start(out=out[tc * 128:(tc + 1) * 128, :],
                            in_=RES[:, tc * D:(tc + 1) * D]).then_inc(fin, 16)
            n += 16
        G.eng.wait_ge(fin, n)
        ctx.close()
        return nc

    RW1 = [cc_w_out[r:r + 1, O_W1:O_W1 + D * W1R].rearrange(
        "o (kc pp m) -> (o pp) kc m", pp=128, m=W1R) for r in range(8)]
    RW2 = [cc_w_out[r:r + 1, O_W2:O_W2 + I_FFN * W2R].rearrange(
        "o (ic pp m) -> (o pp) ic m", pp=128, m=W2R) for r in range(8)]

    def sf32(c0, w):
        return SCR[:, c0:c0 + 2 * w].bitcast(F32)

    def si32v(c0, w):
        return SCR[:, c0:c0 + 2 * w].bitcast(I32)


    V.wait(TE.tok())
    S.wait(TE.tok())

    for tc in range(NTC):
        rs_ = RES[:, tc * D:(tc + 1) * D]
        t_r = ln_cols(rs_, C_M1, C_R1)
        xn2f = sf32(0, 2048)
        V.wait(t_r)
        t_x = V.tag(V.eng.tensor_scalar(xn2f, rs_, COL[:, C_M1:C_M1 + 1],
                                        COL[:, C_R1:C_R1 + 1], OP.subtract, OP.mult))
        V.wait(t_x)
        quant_chain(xn2f, si32v(4096, 2048), sf32(8192, 2048), sf32(12288, 2048),
                    SCR[:, 16384:18432], 2048)
        if stage == 6 and tc == 0:
            fin = ctx.enter_context(nc.semaphore("fin"))
            G.wait(V.tok())
            G.eng.dma_start(out=out[0:128, :], in_=SCR[:, 16384:18432]).then_inc(fin, 16)
            G.eng.dma_start(out=out[128:256, :], in_=xn2f).then_inc(fin, 16)
            G.eng.dma_start(out=out[256:384, 0:80], in_=COL[:]).then_inc(fin, 16)
            G.eng.wait_ge(fin, 48)
            ctx.close()
            return nc
        for half in range(2):
            bi = p.next_pb()
            TE.waits([V.tok(), p.pb_tok[bi]])
            for j in range(8):
                kc = half * 8 + j
                inst = TE.eng.transpose(p.pb[bi][:, j * 128:(j + 1) * 128],
                                        SCR[:, 16384 + kc * 128:16384 + (kc + 1) * 128],
                                        CID[:])
            t_tr = TE.tag(inst)
            V.wait(t_tr)
            for j in range(8):
                kc = half * 8 + j
                inst = V.eng.tensor_copy(
                    XNF[:, kc * T + tc * 128:kc * T + (tc + 1) * 128],
                    p.pb[bi][:, j * 128:(j + 1) * 128])
            p.pb_tok[bi] = V.tag(inst)
    tok_xqf = V.tok()

    SY.wait(tok_xqf)
    for tc in range(NTC):
        t_sp = dS.tag(SY.eng.dma_start(out=res_spill[tc * 128:(tc + 1) * 128, :],
                                       in_=RES[:, tc * D:(tc + 1) * D]))
    tok_spill = t_sp

    if stage == 5:
        fin = ctx.enter_context(nc.semaphore("fin"))
        G.wait(tok_xqf)
        G.wait(CC.tok())
        n = 0
        G.eng.dma_start(out=out[0:128, :], in_=XNF[:, 0:T]).then_inc(fin, 16)
        n += 16
        t_bl = dG.tag(G.eng.dma_start(
            out=SCR[:, 0:1408],
            in_=cc_w_out[0:1, O_W1:O_W1 + 128 * W1R].rearrange(
                "o (pp m) -> (o pp) m", pp=128)))
        G.wait(t_bl)
        G.eng.dma_start(out=out[128:256, 0:1408], in_=SCR[:, 0:1408]).then_inc(fin, 16)
        n += 16
        t_b2 = dG.tag(G.eng.dma_start(
            out=SCR[:, 2048:2304],
            in_=cc_w_out[3:4, O_W2:O_W2 + 128 * W2R].rearrange(
                "o (pp m) -> (o pp) m", pp=128)))
        G.wait(t_b2)
        G.eng.dma_start(out=out[256:384, 0:256], in_=SCR[:, 2048:2304]).then_inc(fin, 16)
        n += 16
        G.eng.wait_ge(fin, n)
        ctx.close()
        return nc

    HT = RES

    def htf32(c0, w):
        return HT[:, c0:c0 + 2 * w].bitcast(F32)

    def hti32(c0, w):
        return HT[:, c0:c0 + 2 * w].bitcast(I32)

    w1slab = [WSlot(WB[:, 0:2048]), WSlot(WB[:, 2048:4096])]
    slab3 = lambda sl: sl.ap.rearrange("p (kc m) -> p kc m", m=512)

    def mm4_w1(bks, tt, m0, nk=16):
        for b in range(4):
            TE.wait(p.pf_tok[bks[b]])
        last = None
        for g in range(nk // 4):
            sl = w1slab[g % 2]
            dS.wait(sl.war)
            spans = []
            ma = m0
            while ma < m0 + 512:
                r = ma // W1R
                mb = min(m0 + 512, (r + 1) * W1R)
                spans.append((r, ma, mb))
                ma = mb
            toks = []
            for (r, ma, mb) in spans:
                toks.append(dS.tag(SY.eng.dma_start(
                    out=slab3(sl)[:, :, ma - m0:mb - m0],
                    in_=RW1[r][:, 4 * g:4 * g + 4, ma - r * W1R:mb - r * W1R])))
            TE.waits(toks)
            for j in range(4):
                kc = 4 * g + j
                for b in range(4):
                    tc = tt * 4 + b
                    inst = TE.eng.matmul(
                        p.pf[bks[b]][:],
                        XNF[:, kc * T + tc * 128:kc * T + (tc + 1) * 128],
                        slab3(sl)[:, j, :],
                        start=(kc == 0), stop=(kc == nk - 1))
                last = TE.tag(inst)
            sl.war = last
        return last

    sg_war = [None] * 4
    for tt in range(4):
        for mtp in range(11):
            bks = [p.next_pf() for _ in range(4)]
            t_mm = mm4_w1(bks, tt, mtp * 512)
            S.wait(t_mm)
            for b in range(4):
                S.waits([p.pf_tok[bks[b]], tok_spill, sg_war[b]])
                t_sg = S.tag(S.eng.activation(
                    HT[:, 22528 + b * 512:22528 + (b + 1) * 512],
                    p.pf[bks[b]][:], AF.Silu))
                p.pf_tok[bks[b]] = t_sg
            t_mm = mm4_w1(bks, tt, I_FFN + mtp * 512)
            V.wait(t_mm)
            for b in range(4):
                V.waits([p.pf_tok[bks[b]], tok_spill])
                t_h = V.tag(V.eng.tensor_tensor(
                    HT[:, b * I_FFN + mtp * 512:b * I_FFN + (mtp + 1) * 512],
                    HT[:, 22528 + b * 512:22528 + (b + 1) * 512],
                    p.pf[bks[b]][:], OP.mult))
                p.pf_tok[bks[b]] = t_h
                sg_war[b] = t_h
        V.wait(TE.tok())
        V.wait(V.tok())
        for b in range(4):
            hrow = HT[:, b * I_FFN:(b + 1) * I_FFN]
            t1 = V.tag(V.eng.tensor_reduce(COL[:, 76:77], hrow, AX.X,
                                           OP.max, apply_absolute_value=True))
            V.wait(t1)
            t2 = V.tag(V.eng.tensor_scalar(COL[:, 77:78], COL[:, 76:77],
                                           1.0 / 127.0, 1e-8, OP.mult, OP.max))
            V.wait(t2)
            t3 = V.tag(V.eng.reciprocal(COL[:, 78:79], COL[:, 77:78]))
            t4 = V.tag(V.eng.tensor_scalar(COL[:, 79:80], COL[:, 77:78],
                                           -256.0, None, OP.mult))
            V.waits([t3, t4])
            for pc in range(6):
                w0 = pc * 1024
                w = min(1024, I_FFN - w0)
                hpc = hrow[:, w0:w0 + w]
                qi = hti32(24576, w)
                rf = htf32(26624, w)
                qf = htf32(28672, w)
                dq = HT[:, 30720:30720 + w]
                t5 = V.tag(V.eng.tensor_scalar(qi, hpc, COL[:, 78:79], 256.0,
                                               OP.mult, OP.add))
                V.wait(t5)
                t6 = V.tag(V.eng.scalar_tensor_tensor(rf, hpc, COL[:, 78:79],
                                                      qi, OP.mult, OP.subtract))
                V.wait(t6)
                t7 = V.tag(V.eng.scalar_tensor_tensor(qf, rf, -255.5, qi,
                                                      OP.is_ge, OP.add))
                V.wait(t7)
                t8 = V.tag(V.eng.tensor_scalar(qf, qf, 129.0, 383.0,
                                               OP.max, OP.min))
                V.wait(t8)
                t9 = V.tag(V.eng.tensor_scalar(dq, qf, COL[:, 77:78],
                                               COL[:, 79:80], OP.mult, OP.add))
                nkt = w // 128
                bi = p.next_pb()
                TE.waits([t9, p.pb_tok[bi]])
                for j in range(nkt):
                    inst = TE.eng.transpose(p.pb[bi][:, j * 128:(j + 1) * 128],
                                            dq[:, j * 128:(j + 1) * 128], CID[:])
                t_tr = TE.tag(inst)
                V.wait(t_tr)
                for j in range(nkt):
                    ic = pc * 8 + j
                    inst = V.eng.tensor_copy(
                        SCR[:, ic * 512 + b * 128:ic * 512 + (b + 1) * 128],
                        p.pb[bi][:, j * 128:(j + 1) * 128])
                p.pb_tok[bi] = V.tag(inst)
        tok_hq = V.tok()

        SY.wait(V.tok())
        SY.wait(S.tok())
        x1toks = []
        for b in range(4):
            tc = tt * 4 + b
            x1toks.append(dS.tag(SY.eng.dma_start(
                out=HT[:, 22528 + b * 2048:22528 + (b + 1) * 2048],
                in_=res_spill[tc * 128:(tc + 1) * 128, :])))

        w2slab = w1slab
        for ot in range(4):
            o0 = ot * 512
            bks = [p.next_pf() for _ in range(4)]
            last = None
            for g in range(11):
                sl = w2slab[g % 2]
                dS.wait(sl.war)
                toks = []
                for half in range(2):
                    r = (o0 + half * 256) // W2R
                    toks.append(dS.tag(SY.eng.dma_start(
                        out=slab3(sl)[:, :, half * 256:(half + 1) * 256],
                        in_=RW2[r][:, 4 * g:4 * g + 4, :])))
                TE.waits(toks)
                TE.wait(tok_hq)
                for b in range(4):
                    TE.wait(p.pf_tok[bks[b]])
                for j in range(4):
                    ic = 4 * g + j
                    for b in range(4):
                        inst = TE.eng.matmul(
                            p.pf[bks[b]][:],
                            SCR[:, ic * 512 + b * 128:ic * 512 + (b + 1) * 128],
                            slab3(sl)[:, j, :],
                            start=(ic == 0), stop=(ic == 43))
                    last = TE.tag(inst)
                sl.war = last
            for b in range(4):
                V.waits([last, x1toks[b]])
                xb_ = HT[:, 22528 + b * 2048 + o0:22528 + b * 2048 + o0 + 512]
                t_e = V.tag(V.eng.tensor_tensor(xb_, xb_, p.pf[bks[b]][:], OP.add))
                p.pf_tok[bks[b]] = t_e
        SY.wait(V.tok())
        for b in range(4):
            tc = tt * 4 + b
            t_x2 = dS.tag(SY.eng.dma_start(
                out=x2_d[tc * 128:(tc + 1) * 128, :],
                in_=HT[:, 22528 + b * 2048:22528 + (b + 1) * 2048]))
    tok_x2 = t_x2


    if stage == 4:
        fin = ctx.enter_context(nc.semaphore("fin"))
        G.wait(tok_x2)
        n = 0
        for tc in range(NTC):
            G.eng.dma_start(out=HT[:, 0:D],
                            in_=x2_d[tc * 128:(tc + 1) * 128, :]).then_inc(fin, 16)
            n += 16
            G.eng.wait_ge(fin, n)
            G.eng.dma_start(out=out[tc * 128:(tc + 1) * 128, :],
                            in_=HT[:, 0:D]).then_inc(fin, 16)
            n += 16
            G.eng.wait_ge(fin, n)
        ctx.close()
        return nc

    RIP = [cc_w_out[r:r + 1, O_IP:O_IP + IPC * MPROJ].rearrange(
        "o (kh pp m) -> (o pp) kh m", pp=128, m=MPROJ) for r in range(8)]
    ROP = [cc_w_out[r:r + 1, O_OP:O_OP + OPC * D].rearrange(
        "o (kh pp m) -> (o pp) kh m", pp=128, m=D) for r in range(8)]
    RW8 = [cc_w_out[r:r + 1, O_W8:O_W8 + W8C * D].rearrange(
        "o (kh pp m) -> (o pp) kh m", pp=128, m=D) for r in range(8)]

    C_EM, C_HS = 68, 69
    t_mk = dS.tag(SY.eng.dma_start(out=COL[0:1, C_EM:C_EM + 2], in_=masks))
    V.wait(t_mk)
    V.eng.tensor_copy(COL[32:33, C_EM:C_EM + 2], COL[0:1, C_EM:C_EM + 2])
    V.eng.tensor_copy(COL[64:65, C_EM:C_EM + 2], COL[0:1, C_EM:C_EM + 2])
    t_c3 = V.tag(V.eng.tensor_copy(COL[96:97, C_EM:C_EM + 2], COL[0:1, C_EM:C_EM + 2]))
    V.wait(t_c3)
    tok_mk = V.tag(V.eng.stream_shuffle(COL[:, C_EM:C_EM + 2],
                                        COL[:, C_EM:C_EM + 2], [0] * 32))

    V.wait(TE.tok())
    for tc in range(NTC):
        t_ld = dS.tag(SY.eng.dma_start(out=SCR[:, 0:2048],
                                       in_=x2_d[tc * 128:(tc + 1) * 128, :]))
        V.wait(t_ld)
        t_r = ln_cols(SCR[:, 0:2048], C_M1, C_R1)
        V.wait(t_r)
        t_x = V.tag(V.eng.tensor_scalar(SCR[:, 2048:4096], SCR[:, 0:2048],
                                        COL[:, C_M1:C_M1 + 1], COL[:, C_R1:C_R1 + 1],
                                        OP.subtract, OP.mult))
        for half in range(2):
            bi = p.next_pb()
            TE.waits([t_x, p.pb_tok[bi]])
            for j in range(8):
                kc = half * 8 + j
                inst = TE.eng.transpose(p.pb[bi][:, j * 128:(j + 1) * 128],
                                        SCR[:, 2048 + kc * 128:2048 + (kc + 1) * 128],
                                        CID[:])
            t_tr = TE.tag(inst)
            V.wait(t_tr)
            for j in range(8):
                kc = half * 8 + j
                inst = V.eng.tensor_copy(
                    XNF[:, kc * T + tc * 128:kc * T + (tc + 1) * 128],
                    p.pb[bi][:, j * 128:(j + 1) * 128])
            p.pb_tok[bi] = V.tag(inst)
        dS.wait(V.tok())
    tok_xn3 = V.tok()

    ipslab = [WSlot(WB[:, 0:512]), WSlot(WB[:, 512:1024])]

    def ipt_slab(g, m0, w):
        sl = ipslab[g % 2]
        dS.wait(sl.war)
        toks = []
        for rr in range(2):
            kc0 = 4 * g + 2 * rr
            r = kc0 // 2
            toks.append(dS.tag(SY.eng.dma_start(
                out=sl.ap[:, 0:4 * w].rearrange("p (kc m) -> p kc m", m=w)[
                    :, 2 * rr:2 * rr + 2, :],
                in_=RIP[r][:, :, m0:m0 + w])))
        return sl, toks

    HSTG = 1024
    for mc in range(35):
        rows = 128 if mc < 34 else 64
        m0 = DIN + mc * 128
        bks = [p.next_pf() for _ in range(4)]
        for b in range(4):
            TE.wait(p.pf_tok[bks[b]])
        last = None
        for g in range(4):
            sl, toks = ipt_slab(g, m0, rows)
            TE.waits(toks + [tok_xn3])
            for j in range(4):
                kc = 4 * g + j
                for b in range(4):
                    inst = TE.eng.matmul(
                        p.pf[bks[b]][0:rows, :],
                        sl.ap[:, 0:4 * rows].rearrange("p (kc m) -> p kc m",
                                                       m=rows)[:, j, :],
                        XNF[:, kc * T + b * 512:kc * T + (b + 1) * 512],
                        start=(kc == 0), stop=(kc == 15))
                last = TE.tag(inst)
            sl.war = last
        if mc < 34:
            for b in range(4):
                V.waits([last, p.pf_tok[bks[b]]])
                t_e = V.tag(V.eng.tensor_copy(SCR[:, 0:512], p.pf[bks[b]][:]))
                if b == 3:
                    V.wait(t_e)
                    t_h = V.tag(V.eng.tensor_copy(
                        SCR[:, HSTG + mc * 3:HSTG + (mc + 1) * 3],
                        SCR[:, 509:512]))
                    t_e = t_h
                SY.wait(t_e)
                t_st = dS.tag(SY.eng.dma_start(
                    out=xbcraw_d[mc * 128:(mc + 1) * 128, b * 512:(b + 1) * 512],
                    in_=SCR[:, 0:512]))
                dS.wait(V.tok())
                p.pf_tok[bks[b]] = t_e
                V.wait(t_st)
        else:
            for b in range(4):
                V.waits([last, p.pf_tok[bks[b]]])
                t_e = V.tag(V.eng.tensor_copy(
                    SC2[0:64, b * 512:(b + 1) * 512], p.pf[bks[b]][0:64, :]))
                p.pf_tok[bks[b]] = t_e
    tok_xbc = dS.tok()

    V.waits([tok_mk])
    t_hm = V.tag(V.eng.tensor_scalar(SC2[0:128, 2048:2150], SCR[:, HSTG:HSTG + 102],
                                     COL[:, C_EM:C_EM + 1], None, OP.mult))
    SY.wait(t_hm)
    t_hw = dS.tag(SY.eng.dma_start(out=cc_h_in, in_=SC2[:, 2048:2150]))
    G.wait(t_hw)
    t_hc = CC.tag(G.eng.collective_compute(
        "AllReduce", OP.add, replica_groups=[[0, 1], [2, 3], [4, 5], [6, 7]],
        ins=[cc_h_in.opt()], outs=[cc_h_out.opt()]))
    G.wait(t_hc)
    t_hl = dG.tag(G.eng.dma_start(out=SC2[:, 2048:2150], in_=cc_h_out))
    V.wait(t_hl)
    t_hs = V.tag(V.eng.tensor_scalar(SCR[:, HSTG:HSTG + 102], SC2[:, 2048:2150],
                                     COL[:, C_HS:C_HS + 1], None, OP.mult))

    if stage == 61:
        fin = ctx.enter_context(nc.semaphore("fin"))
        G.waits([tok_xbc, V.tok()])
        n = 0
        for i in range(16):
            G.eng.dma_start(out=SCR[:, 0:2048],
                            in_=xbcraw_d[i * 128:(i + 1) * 128, 0:2048]).then_inc(fin, 16)
            n += 16
            G.eng.wait_ge(fin, n)
            G.eng.dma_start(out=out[i * 128:(i + 1) * 128, 0:2048],
                            in_=SCR[:, 0:2048]).then_inc(fin, 16)
            n += 16
            G.eng.wait_ge(fin, n)
        G.eng.dma_start(out=out[0:64, 0:2048],
                        in_=SC2[0:64, 0:2048]).then_inc(fin, 16)
        n += 16
        G.eng.wait_ge(fin, n)
        ctx.close()
        return nc

    CPAD = 1280
    CST = 14336
    BCM, CCM = 16384, 18432
    for cc34 in range(34):
        dS.wait(V.tok())
        t_ci = dS.tag(SY.eng.dma_start(
            out=SCR[:, CPAD + 3:CPAD + 3 + 2048],
            in_=xbcraw_d[cc34 * 128:(cc34 + 1) * 128, :]))
        dA.wait(S.tok())
        dA.wait(V.tok())
        t_cwc = dA.tag(S.eng.dma_start(out=COL[:, 12:16],
                                       in_=conv_w[cc34 * 128:(cc34 + 1) * 128, :]))
        t_cbc = dA.tag(S.eng.dma_start(
            out=COL[:, 16:17],
            in_=conv_b[0:1, cc34 * 128:(cc34 + 1) * 128].rearrange("o p -> p o"),
            allow_slow_non_contiguous=True))
        V.waits([t_ci, t_cwc, t_hs])
        V.wait(S.tok())
        t_hc2 = V.tag(V.eng.tensor_copy(SCR[:, CPAD:CPAD + 3],
                                        SCR[:, HSTG + cc34 * 3:HSTG + (cc34 + 1) * 3]))
        V.wait(t_hc2)
        ACC = SC2[:, 2560:4608]
        t_a0 = V.tag(V.eng.tensor_scalar(ACC, SCR[:, CPAD:CPAD + 2048],
                                         COL[:, 12:13], None, OP.mult))
        V.wait(t_a0)
        for i in (1, 2, 3):
            t_ai = V.tag(V.eng.scalar_tensor_tensor(
                ACC, SCR[:, CPAD + i:CPAD + i + 2048], COL[:, 12 + i:13 + i], ACC,
                OP.mult, OP.add))
            V.wait(t_ai)
        S.waits([t_ai, t_cbc])
        if cc34 < 32:
            t_sl = S.tag(S.eng.activation(SCR[:, CST:CST + 2048], ACC, AF.Silu,
                                          bias=COL[:, 16:17]))
            SY.wait(t_sl)
            t_so = dS.tag(SY.eng.dma_start(
                out=xconv_d[cc34 * 128:(cc34 + 1) * 128, :], in_=SCR[:, CST:CST + 2048]))
            S.wait(t_so)
        elif cc34 == 32:
            t_sl = S.tag(S.eng.activation(SCR[:, BCM:BCM + 2048], ACC, AF.Silu,
                                          bias=COL[:, 16:17]))
        else:
            t_sl = S.tag(S.eng.activation(SCR[:, CCM:CCM + 2048], ACC, AF.Silu,
                                          bias=COL[:, 16:17]))
    tok_conv = S.tok()

    if stage == 615:
        fin = ctx.enter_context(nc.semaphore("fin"))
        G.waits([V.tok(), S.tok(), TE.tok(), dS.tok(), dA.tok(), dG.tok()])
        G.eng.dma_start(out=out[0:128, 0:128], in_=CID[:]).then_inc(fin, 16)
        G.eng.wait_ge(fin, 16)
        ctx.close()
        return nc

    t_al2 = dA.tag(S.eng.dma_start(out=COL[0:64, 5:6],
                                   in_=a_log.rearrange("o p -> p o"),
                                   allow_slow_non_contiguous=True))
    t_db = dA.tag(S.eng.dma_start(out=COL[0:64, 6:7],
                                  in_=dt_bias.rearrange("o p -> p o"),
                                  allow_slow_non_contiguous=True))
    t_dv = dA.tag(S.eng.dma_start(out=COL[0:64, 7:8],
                                  in_=d_vec.rearrange("o p -> p o"),
                                  allow_slow_non_contiguous=True))
    S.waits([t_al2, t_db, t_dv, V.tok()])
    t_am = S.tag(S.eng.activation(COL[0:64, 5:6], COL[0:64, 5:6], AF.Exp))
    V.wait(t_am)
    t_an = V.tag(V.eng.tensor_scalar(COL[0:64, 5:6], COL[0:64, 5:6], -1.0, None,
                                     OP.mult))
    DTF = 2560
    dtr = SC2[0:64, 0:2048]
    S.wait(V.tok())
    t_s1 = S.tag(S.eng.activation(SC2[0:64, DTF:DTF + 2048], dtr, AF.Abs,
                                  bias=COL[0:64, 6:7]))
    S.wait(t_s1)
    t_s2 = S.tag(S.eng.activation(SC2[0:64, DTF:DTF + 2048],
                                  SC2[0:64, DTF:DTF + 2048], AF.Exp, scale=-1.0))
    S.wait(t_s2)
    t_s3 = S.tag(S.eng.activation(SC2[0:64, DTF:DTF + 2048],
                                  SC2[0:64, DTF:DTF + 2048], AF.Ln,
                                  bias=COL[0:64, C_ONE:C_ONE + 1]))
    t_s4 = S.tag(S.eng.activation(SC2[0:64, 0:2048], dtr, AF.Relu,
                                  bias=COL[0:64, 6:7]))
    V.waits([t_s3, t_s4])
    t_dt = V.tag(V.eng.tensor_tensor(SC2[0:64, DTF:DTF + 2048],
                                     SC2[0:64, DTF:DTF + 2048],
                                     SC2[0:64, 0:2048], OP.add))
    ADTC = 4096
    V.waits([t_dt, t_an])
    t_adt = V.tag(V.eng.tensor_scalar(SCR[0:64, ADTC:ADTC + 2048],
                                      SC2[0:64, DTF:DTF + 2048],
                                      COL[0:64, 5:6], None, OP.mult))
    V.wait(t_adt)
    for c in range(NCH):
        t_sc = V.tag(V.eng.tensor_tensor_scan(
            SC2[0:64, c * 256:(c + 1) * 256],
            SCR[0:64, ADTC + c * 256:ADTC + (c + 1) * 256],
            SCR[0:64, ADTC + c * 256:ADTC + (c + 1) * 256],
            0.0, OP.add, OP.bypass))
    tok_acscm = V.tok()

    if stage == 616:
        fin = ctx.enter_context(nc.semaphore("fin"))
        G.waits([V.tok(), S.tok(), TE.tok(), dS.tok(), dA.tok(), dG.tok()])
        G.eng.dma_start(out=out[0:128, 0:128], in_=CID[:]).then_inc(fin, 16)
        G.eng.wait_ge(fin, 16)
        ctx.close()
        return nc

    ADTT = 6144
    DTTM = 0
    DTB = 8192
    S.wait(V.tok())
    t_dtb = S.tag(S.eng.activation(SCR[0:64, DTB:DTB + 2048],
                                   SC2[0:64, DTF:DTF + 2048], AF.Identity))
    V.wait(t_dtb)
    for tc in range(NTC):
        bi = p.next_pb()
        TE.waits([V.tok(), t_dtb, p.pb_tok[bi]])
        i1 = TE.eng.transpose(p.pb[bi][:, 0:64],
                              SCR[0:64, ADTC + tc * 128:ADTC + (tc + 1) * 128],
                              CID[0:64, 0:64])
        i2 = TE.eng.transpose(p.pb[bi][:, 64:128],
                              SCR[0:64, DTB + tc * 128:DTB + (tc + 1) * 128],
                              CID[0:64, 0:64])
        t_tr = TE.tag(i2)
        V.wait(t_tr)
        V.eng.tensor_copy(SCR[:, ADTT + tc * 64:ADTT + (tc + 1) * 64],
                          p.pb[bi][:, 0:64])
        inst = V.eng.tensor_copy(SCR[:, DTTM + tc * 64:DTTM + (tc + 1) * 64],
                                 p.pb[bi][:, 64:128])
        p.pb_tok[bi] = V.tag(inst)
    tok_dtt = V.tok()

    if stage == 617:
        fin = ctx.enter_context(nc.semaphore("fin"))
        G.waits([V.tok(), S.tok(), TE.tok(), dS.tok(), dA.tok(), dG.tok()])
        G.eng.dma_start(out=out[0:128, 0:128], in_=CID[:]).then_inc(fin, 16)
        G.eng.wait_ge(fin, 16)
        ctx.close()
        return nc

    if stage == 621:
        fin = ctx.enter_context(nc.semaphore("fin"))
        G.waits([V.tok(), S.tok(), TE.tok(), dS.tok(), dG.tok(), dA.tok()])
        G.eng.dma_start(out=out[0:128, 0:1024],
                        in_=SCR[:, 0:1024]).then_inc(fin, 16)
        G.eng.wait_ge(fin, 16)
        ctx.close()
        return nc

    ACSTM = 2048
    EXPA, DECAYC = 20480, 3072
    TRI_I = CTRI[:, 0:128]
    TRI_S = CTRI[:, 128:256]
    TRI_O = CTRI[:, 256:384]
    V.wait(tok_dtt)
    t_z2 = V.tag(V.eng.memset(DEX[:, 576:832], 0.0))
    for c in range(NCH):
        adt0 = SCR[:, ADTT + (2 * c) * 64:ADTT + (2 * c + 1) * 64]
        adt1 = SCR[:, ADTT + (2 * c + 1) * 64:ADTT + (2 * c + 2) * 64]
        for lh in range(2):
            tcb = 2 * c + lh
            bi = p.next_pf()
            pk = p.pf[bi]
            TE.waits([tok_dtt, p.pf_tok[bi]])
            if lh == 0:
                i1 = TE.eng.matmul(pk[:, 0:64], TRI_I, adt0, start=True, stop=True)
            else:
                TE.eng.matmul(pk[:, 0:64], TRI_O, adt0, start=True, stop=False)
                i1 = TE.eng.matmul(pk[:, 0:64], TRI_I, adt1, start=False, stop=True)
            if lh == 0:
                TE.eng.matmul(pk[:, 64:128], TRI_S, adt0, start=True, stop=False)
                i2 = TE.eng.matmul(pk[:, 64:128], TRI_O, adt1, start=False, stop=True)
            else:
                i2 = TE.eng.matmul(pk[:, 64:128], TRI_S, adt1, start=True, stop=True)
            t_mm = TE.tag(i2)
            V.wait(t_mm)
            t_ac = V.tag(V.eng.tensor_copy(
                SC2[:, ACSTM + tcb * 64:ACSTM + (tcb + 1) * 64], pk[:, 0:64]))
            S.waits([t_mm, t_ac])
            t_e1 = S.tag(S.eng.activation(
                SCR[:, EXPA + tcb * 64:EXPA + (tcb + 1) * 64], pk[:, 0:64], AF.Exp))
            t_e2 = S.tag(S.eng.activation(
                SCR[:, DECAYC + tcb * 64:DECAYC + (tcb + 1) * 64], pk[:, 64:128],
                AF.Exp))
            p.pf_tok[bi] = t_e2
            V.waits([t_e1, t_e2])
        if stage == 6171 and c == 0:
            fin = ctx.enter_context(nc.semaphore("fin"))
            G.waits([V.tok(), S.tok(), TE.tok(), dS.tok(), dA.tok(), dG.tok()])
            G.eng.dma_start(out=out[0:128, 0:128], in_=CID[:]).then_inc(fin, 16)
            G.eng.wait_ge(fin, 16)
            ctx.close()
            return nc
        bi = p.next_pf()
        pk = p.pf[bi]
        TE.wait(p.pf_tok[bi])
        TE.eng.matmul(pk[0:1, 0:64], TRI_O[:, 0:1], adt0, start=True, stop=False)
        i3 = TE.eng.matmul(pk[0:1, 0:64], TRI_O[:, 0:1], adt1, start=False, stop=True)
        t_mm = TE.tag(i3)
        S.wait(t_mm)
        t_db = S.tag(S.eng.activation(DEX[0:1, c * 64:(c + 1) * 64], pk[0:1, 0:64],
                                      AF.Exp))
        p.pf_tok[bi] = t_db
        V.wait(t_db)
        p.bcast_row(DEX[:, c * 64:(c + 1) * 64], 64)
        if stage == 6172 and c == 1:
            fin = ctx.enter_context(nc.semaphore("fin"))
            G.waits([V.tok(), S.tok(), TE.tok(), dS.tok(), dA.tok(), dG.tok()])
            G.eng.dma_start(out=out[0:128, 0:640], in_=DEX[:, 0:640]).then_inc(fin, 16)
            G.eng.dma_start(out=out[128:256, 0:1024],
                            in_=SC2[:, ACSTM:ACSTM + 1024]).then_inc(fin, 16)
            G.eng.wait_ge(fin, 32)
            ctx.close()
            return nc

    if stage == 618:
        fin = ctx.enter_context(nc.semaphore("fin"))
        G.waits([V.tok(), S.tok(), TE.tok(), dS.tok(), dG.tok(), dA.tok()])
        G.eng.dma_start(out=out[0:128, 0:640], in_=DEX[:, 0:640]).then_inc(fin, 16)
        G.eng.dma_start(out=out[128:256, 0:1024],
                        in_=SC2[:, ACSTM:ACSTM + 1024]).then_inc(fin, 16)
        G.eng.dma_start(out=out[256:384, 0:1024],
                        in_=SCR[:, DECAYC:DECAYC + 1024]).then_inc(fin, 16)
        G.eng.dma_start(out=out[384:512, 0:1024],
                        in_=SCR[:, EXPA:EXPA + 1024]).then_inc(fin, 16)
        G.eng.dma_start(out=out[512:640, 0:2048],
                        in_=SCR[:, ADTT:ADTT + 2048]).then_inc(fin, 16)
        G.eng.wait_ge(fin, 80)
        ctx.close()
        return nc

    if stage == 622:
        fin = ctx.enter_context(nc.semaphore("fin"))
        G.waits([V.tok(), S.tok(), TE.tok(), dS.tok(), dG.tok(), dA.tok()])
        G.eng.dma_start(out=out[0:128, 0:1024],
                        in_=SCR[:, 0:1024]).then_inc(fin, 16)
        G.eng.wait_ge(fin, 16)
        ctx.close()
        return nc

    t_dr = dG.tag(G.eng.dma_start(out=DEX[0:1, 512:576], in_=d_vec))
    V.wait(t_dr)
    tok_db = p.bcast_row(DEX[:, 512:576], 64)

    if stage == 62:
        fin = ctx.enter_context(nc.semaphore("fin"))
        G.waits([V.tok(), S.tok(), dS.tok(), dG.tok(), dA.tok(), TE.tok()])
        n = 0
        for i in range(8):
            G.eng.dma_start(out=SCR[:, 0:2048],
                            in_=xconv_d[i * 128:(i + 1) * 128, :]).then_inc(fin, 16)
            n += 16
            G.eng.wait_ge(fin, n)
            G.eng.dma_start(out=out[i * 128:(i + 1) * 128, :],
                            in_=SCR[:, 0:2048]).then_inc(fin, 16)
            n += 16
            G.eng.wait_ge(fin, n)
        G.eng.dma_start(out=out[10 * 128:11 * 128, :],
                        in_=SCR[:, BCM:BCM + 2048]).then_inc(fin, 16)
        G.eng.dma_start(out=out[11 * 128:12 * 128, :],
                        in_=SCR[:, CCM:CCM + 2048]).then_inc(fin, 16)
        G.eng.dma_start(out=out[12 * 128:13 * 128, 0:1024],
                        in_=SC2[:, ACSTM:ACSTM + 1024]).then_inc(fin, 16)
        G.eng.dma_start(out=out[13 * 128:14 * 128, 0:1024],
                        in_=SCR[:, EXPA:EXPA + 1024]).then_inc(fin, 16)
        G.eng.dma_start(out=out[14 * 128:15 * 128, 0:1024],
                        in_=SCR[:, DECAYC:DECAYC + 1024]).then_inc(fin, 16)
        G.eng.dma_start(out=out[15 * 128:16 * 128, 0:640],
                        in_=DEX[:, 0:640]).then_inc(fin, 16)
        G.eng.dma_start(out=out[8 * 128:8 * 128 + 64, 0:2048],
                        in_=SC2[0:64, 0:2048]).then_inc(fin, 16)
        n += 7 * 16
        G.eng.wait_ge(fin, n)
        ctx.close()
        return nc

    XCMA, XTMA, XDTA, XDDA = 0, 8192, 16384, 24576
    BTM = 1024
    CBMA = 1280
    ACSB = 4096
    MPF = 8192
    MB0, MB1 = 12288, 14336
    MXA = 20480

    V.wait(tok_conv)
    V.wait(tok_acscm)
    tok_ydprev = None
    for c in range(NCH):
        V.wait(TE.tok())
        V.wait(tok_ydprev)
        dS.wait(dS.tok())
        dS.waits([V.tok(), TE.tok()])
        t_x = dS.tag(SY.eng.dma_start(
            out=RES[:, XCMA:XCMA + 8192].rearrange("p (xc j) -> p xc j", j=256),
            in_=xconv_d[:, c * 256:(c + 1) * 256].rearrange(
                "(xc pp) j -> pp xc j", pp=128)))
        for xc in range(32):
            bi = p.next_pb()
            TE.waits([t_x, p.pb_tok[bi], ld_id])
            i1 = TE.eng.transpose(p.pb[bi][:, 0:128],
                                  RES[:, XCMA + xc * 256:XCMA + xc * 256 + 128],
                                  CID[:])
            i2 = TE.eng.transpose(p.pb[bi][:, 128:256],
                                  RES[:, XCMA + xc * 256 + 128:XCMA + (xc + 1) * 256],
                                  CID[:])
            t_tr = TE.tag(i2)
            V.wait(t_tr)
            V.eng.tensor_copy(RES[:, XTMA + xc * 128:XTMA + (xc + 1) * 128],
                              p.pb[bi][:, 0:128])
            inst = V.eng.tensor_copy(
                RES[:, XTMA + 4096 + xc * 128:XTMA + 4096 + (xc + 1) * 128],
                p.pb[bi][:, 128:256])
            p.pb_tok[bi] = V.tag(inst)
        V.wait(V.tok())
        for lh in range(2):
            tcb = 2 * c + lh
            dtex = SCR[:, DTTM + tcb * 64:DTTM + (tcb + 1) * 64].unsqueeze(
                2).broadcast_to([128, 64, 64])
            dcex = SCR[:, DECAYC + tcb * 64:DECAYC + (tcb + 1) * 64].unsqueeze(
                2).broadcast_to([128, 64, 64])
            t_1 = V.tag(V.eng.tensor_tensor(
                RES[:, XDTA + lh * 4096:XDTA + (lh + 1) * 4096],
                RES[:, XTMA + lh * 4096:XTMA + (lh + 1) * 4096], dtex, OP.mult))
            V.wait(t_1)
            t_2 = V.tag(V.eng.tensor_tensor(
                RES[:, XDDA + lh * 4096:XDDA + (lh + 1) * 4096],
                RES[:, XDTA + lh * 4096:XDTA + (lh + 1) * 4096], dcex, OP.mult))
            V.wait(t_2)
        bi = p.next_pb()
        TE.waits([V.tok(), p.pb_tok[bi], tok_conv])
        i1 = TE.eng.transpose(p.pb[bi][:, 0:128],
                              SCR[:, BCM + c * 256:BCM + c * 256 + 128], CID[:])
        i2 = TE.eng.transpose(p.pb[bi][:, 128:256],
                              SCR[:, BCM + c * 256 + 128:BCM + (c + 1) * 256], CID[:])
        t_tr = TE.tag(i2)
        V.wait(t_tr)
        inst = V.eng.tensor_copy(SCR[:, BTM:BTM + 256], p.pb[bi][:, 0:256])
        p.pb_tok[bi] = V.tag(inst)
        for sh in range(2):
            bi = p.next_pf()
            pk = p.pf[bi]
            TE.waits([tok_conv, p.pf_tok[bi]])
            t_mm = TE.tag(TE.eng.matmul(
                pk[:, 0:256], SCR[:, BCM + c * 256 + sh * 128:BCM + c * 256 + (sh + 1) * 128],
                SCR[:, CCM + c * 256:CCM + (c + 1) * 256], start=True, stop=True))
            V.waits([t_mm, ld_sm])
            p.pf_tok[bi] = V.tag(V.eng.tensor_tensor(
                SCR[:, CBMA + sh * 256:CBMA + (sh + 1) * 256], pk[:, 0:256],
                CSM[:, sh * 256:(sh + 1) * 256], OP.mult))
        if stage == 635 and c == 1:
            fin = ctx.enter_context(nc.semaphore("fin"))
            G.waits([V.tok(), S.tok(), TE.tok(), dS.tok(), dG.tok()])
            G.eng.dma_start(out=out[0:128, 0:2048],
                            in_=SCR[:, BCM:BCM + 2048]).then_inc(fin, 16)
            G.eng.dma_start(out=out[128:256, 0:2048],
                            in_=SCR[:, CCM:CCM + 2048]).then_inc(fin, 16)
            G.eng.dma_start(out=out[256:384, 0:512],
                            in_=SCR[:, CBMA:CBMA + 512]).then_inc(fin, 16)
            G.eng.dma_start(out=out[384:512, 0:256],
                            in_=SCR[:, BTM:BTM + 256]).then_inc(fin, 16)
            G.eng.wait_ge(fin, 64)
            ctx.close()
            return nc
        for hg in range(8):
            dG.wait(V.tok())
            t_fl = dG.tag(G.eng.dma_start(
                out=SCR[0:1, ACSB:ACSB + 4096].bitcast(F32),
                in_=SC2[hg * 8:hg * 8 + 8, c * 256:(c + 1) * 256]))
            V.wait(t_fl)
            acsb_f = SCR[:, ACSB:ACSB + 4096].bitcast(F32)
            V.eng.tensor_copy(acsb_f[32:33, :], acsb_f[0:1, :])
            V.eng.tensor_copy(acsb_f[64:65, :], acsb_f[0:1, :])
            t_c3 = V.tag(V.eng.tensor_copy(acsb_f[96:97, :], acsb_f[0:1, :]))
            V.wait(t_c3)
            t_bc = V.tag(V.eng.stream_shuffle(acsb_f[:, :], acsb_f[:, :], [0] * 32))
            V.wait(t_bc)
            for sh in range(2):
                tcb = 2 * c + sh
                V.wait(S.tok())
                for h in range(8):
                    t_st2 = V.tag(V.eng.scalar_tensor_tensor(
                        SCR[:, MPF:MPF + 4096].bitcast(F32)[:, h * 256:(h + 1) * 256],
                        acsb_f[:, h * 256:(h + 1) * 256],
                        SC2[:, ACSTM + tcb * 64 + hg * 8 + h:ACSTM + tcb * 64 + hg * 8 + h + 1],
                        DEX[:, 576:832], OP.subtract, OP.min))
                S.waits([t_st2, TE.tok()])
                mb = SCR[:, (MB0 if sh == 0 else MB1):(MB0 if sh == 0 else MB1) + 2048]
                t_ex = S.tag(S.eng.activation(mb,
                                              SCR[:, MPF:MPF + 4096].bitcast(F32),
                                              AF.Exp))
                V.wait(t_ex)
                cbex = SCR[:, CBMA + sh * 256:CBMA + (sh + 1) * 256].unsqueeze(
                    1).broadcast_to([128, 8, 256])
                t_mb = V.tag(V.eng.tensor_tensor(mb, mb, cbex, OP.mult))
                V.wait(t_mb)
            if stage == 634 and c == 1 and hg == 1:
                fin = ctx.enter_context(nc.semaphore("fin"))
                G.waits([V.tok(), S.tok(), TE.tok(), dS.tok(), dG.tok()])
                G.eng.dma_start(out=out[0:128, 0:2048],
                                in_=SCR[0:128, ACSB:ACSB + 4096].bitcast(F32)).then_inc(fin, 16)
                G.eng.dma_start(out=out[128:256, 0:2048],
                                in_=SCR[:, MB0:MB0 + 2048]).then_inc(fin, 16)
                G.eng.dma_start(out=out[256:384, 0:2048],
                                in_=SCR[:, MB1:MB1 + 2048]).then_inc(fin, 16)
                G.eng.dma_start(out=out[384:512, 0:2048],
                                in_=SCR[:, MPF:MPF + 4096].bitcast(F32)).then_inc(fin, 16)
                G.eng.dma_start(out=out[512:640, 0:256],
                                in_=DEX[:, 576:832]).then_inc(fin, 16)
                G.eng.dma_start(out=out[640:768, 0:512],
                                in_=SCR[:, CBMA:CBMA + 512]).then_inc(fin, 16)
                G.eng.dma_start(out=out[768:896, 0:1024],
                                in_=SC2[:, 2048:3072]).then_inc(fin, 16)
                G.eng.wait_ge(fin, 112)
                ctx.close()
                return nc
            if stage == 632 and c == 1 and hg == 0:
                fin = ctx.enter_context(nc.semaphore("fin"))
                G.waits([V.tok(), S.tok(), TE.tok(), dS.tok(), dG.tok()])
                G.eng.dma_start(out=out[0:128, 0:2048],
                                in_=SCR[0:128, ACSB:ACSB + 4096].bitcast(F32)).then_inc(fin, 16)
                G.eng.dma_start(out=out[128:256, 0:2048],
                                in_=SCR[:, MB0:MB0 + 2048]).then_inc(fin, 16)
                G.eng.dma_start(out=out[256:384, 0:2048],
                                in_=SCR[:, MB1:MB1 + 2048]).then_inc(fin, 16)
                G.eng.dma_start(out=out[384:512, 0:2048],
                                in_=RES[:, XDTA:XDTA + 2048]).then_inc(fin, 16)
                G.eng.dma_start(out=out[512:640, 0:2048],
                                in_=RES[:, XTMA:XTMA + 2048]).then_inc(fin, 16)
                G.eng.dma_start(out=out[640:768, 0:2048],
                                in_=SCR[:, MPF:MPF + 4096].bitcast(F32)).then_inc(fin, 16)
                G.eng.wait_ge(fin, 96)
                ctx.close()
                return nc
            if stage == 631 and c == 0 and hg == 0:
                fin = ctx.enter_context(nc.semaphore("fin"))
                G.waits([V.tok(), S.tok(), TE.tok(), dS.tok(), dG.tok()])
                G.eng.dma_start(out=out[0:128, 0:2048],
                                in_=SCR[0:128, ACSB:ACSB + 4096].bitcast(F32)).then_inc(fin, 16)
                G.eng.dma_start(out=out[128:256, 0:2048],
                                in_=SCR[:, MB0:MB0 + 2048]).then_inc(fin, 16)
                G.eng.dma_start(out=out[256:384, 0:2048],
                                in_=SCR[:, MB1:MB1 + 2048]).then_inc(fin, 16)
                G.eng.dma_start(out=out[384:512, 0:512],
                                in_=SCR[:, CBMA:CBMA + 512]).then_inc(fin, 16)
                G.eng.dma_start(out=out[512:640, 0:2048],
                                in_=RES[:, XDTA:XDTA + 2048]).then_inc(fin, 16)
                G.eng.dma_start(out=out[640:768, 0:2048],
                                in_=RES[:, XTMA:XTMA + 2048]).then_inc(fin, 16)
                G.eng.dma_start(out=out[768:896, 0:2048],
                                in_=RES[:, XDTA:XDTA + 2048]).then_inc(fin, 16)
                G.eng.dma_start(out=out[896:1024, 0:1024],
                                in_=SCR[:, DTTM:DTTM + 1024]).then_inc(fin, 16)
                G.eng.dma_start(out=out[1024:1152, 0:2048],
                                in_=RES[:, XDDA:XDDA + 2048]).then_inc(fin, 16)
                G.eng.wait_ge(fin, 144)
                ctx.close()
                return nc
            bi_d0 = p.next_pf()
            bi_d1 = p.next_pf()
            bi_cs = p.next_pf()
            TE.waits([V.tok(), p.pf_tok[bi_d0], p.pf_tok[bi_d1], p.pf_tok[bi_cs]])
            for h in range(8):
                hh = hg * 8 + h
                for lh, bi_d in ((0, bi_d0), (1, bi_d1)):
                    TE.eng.matmul(
                        p.pf[bi_d][:, h * 64:(h + 1) * 64],
                        SCR[:, MB0 + h * 256 + lh * 128:MB0 + h * 256 + (lh + 1) * 128],
                        RES[:, XDTA + hh * 64:XDTA + (hh + 1) * 64],
                        start=True, stop=False)
                    TE.eng.matmul(
                        p.pf[bi_d][:, h * 64:(h + 1) * 64],
                        SCR[:, MB1 + h * 256 + lh * 128:MB1 + h * 256 + (lh + 1) * 128],
                        RES[:, XDTA + 4096 + hh * 64:XDTA + 4096 + (hh + 1) * 64],
                        start=False, stop=True)
                TE.eng.matmul(p.pf[bi_cs][:, h * 64:(h + 1) * 64],
                              SCR[:, BTM:BTM + 128],
                              RES[:, XDDA + hh * 64:XDDA + (hh + 1) * 64],
                              start=True, stop=False)
                i_cs = TE.eng.matmul(p.pf[bi_cs][:, h * 64:(h + 1) * 64],
                                     SCR[:, BTM + 128:BTM + 256],
                                     RES[:, XDDA + 4096 + hh * 64:XDDA + 4096 + (hh + 1) * 64],
                                     start=False, stop=True)
            t_mm = TE.tag(i_cs)
            for lh, bi_d in ((0, bi_d0), (1, bi_d1)):
                dbex = DEX[:, 512 + hg * 8:512 + (hg + 1) * 8].unsqueeze(
                    2).broadcast_to([128, 8, 64])
                ysl = RES[:, XCMA + lh * 4096 + hg * 512:XCMA + lh * 4096 + (hg + 1) * 512]
                V.waits([t_mm, tok_db])
                t_y1 = V.tag(V.eng.tensor_tensor(
                    ysl, RES[:, XTMA + lh * 4096 + hg * 512:XTMA + lh * 4096 + (hg + 1) * 512],
                    dbex, OP.mult))
                V.wait(t_y1)
                t_y2 = V.tag(V.eng.tensor_tensor(ysl, ysl, p.pf[bi_d][:], OP.add))
                p.pf_tok[bi_d] = t_y2
            V.wait(t_mm)
            t_cse = V.tag(V.eng.tensor_copy(DEX[:, 832:1344], p.pf[bi_cs][:]))
            p.pf_tok[bi_cs] = t_cse
            SY.wait(t_cse)
            t_css = dS.tag(SY.eng.dma_start(
                out=cs_d[c * 128:(c + 1) * 128, hg * 512:(hg + 1) * 512],
                in_=DEX[:, 832:1344]))
            V.wait(t_css)
        if stage == 633 and c == 1:
            fin = ctx.enter_context(nc.semaphore("fin"))
            G.waits([V.tok(), S.tok(), TE.tok(), dS.tok(), dG.tok()])
            G.eng.dma_start(out=out[0:128, 0:2048],
                            in_=RES[:, XCMA:XCMA + 2048]).then_inc(fin, 16)
            G.eng.dma_start(out=out[128:256, 0:2048],
                            in_=RES[:, XCMA + 4096:XCMA + 4096 + 2048]).then_inc(fin, 16)
            G.eng.dma_start(out=out[256:384, 0:2048],
                            in_=SCR[:, MB0:MB0 + 2048]).then_inc(fin, 16)
            G.eng.dma_start(out=out[384:512, 0:2048],
                            in_=RES[:, XDTA:XDTA + 2048]).then_inc(fin, 16)
            G.eng.wait_ge(fin, 64)
            ctx.close()
            return nc
        SY.wait(V.tok())
        for lh in range(2):
            t_yd = dS.tag(SY.eng.dma_start(
                out=yd_d[c * 256 + lh * 128:c * 256 + (lh + 1) * 128, :],
                in_=RES[:, XCMA + lh * 4096:XCMA + (lh + 1) * 4096]))
        tok_ydprev = t_yd
    tok_yd = dS.tok()

    if stage == 63:
        fin = ctx.enter_context(nc.semaphore("fin"))
        G.waits([tok_yd, V.tok()])
        n = 0
        for i in range(8):
            G.eng.dma_start(out=SCR[:, 0:4096],
                            in_=yd_d[i * 256:i * 256 + 128, :]).then_inc(fin, 16)
            n += 16
            G.eng.wait_ge(fin, n)
            G.eng.dma_start(out=out[i * 128:(i + 1) * 128, 0:2048],
                            in_=SCR[:, 0:2048].bitcast(F32)[:, 0:2048] if False
                            else SCR[:, 0:4096].bitcast(F32)[:, 0:2048]).then_inc(fin, 16)
            n += 16
            G.eng.wait_ge(fin, n)
        for i in range(8):
            G.eng.dma_start(out=SCR[:, 0:4096],
                            in_=cs_d[i * 128:(i + 1) * 128, :]).then_inc(fin, 16)
            n += 16
            G.eng.wait_ge(fin, n)
            G.eng.dma_start(out=out[(8 + i) * 128:(9 + i) * 128, 0:2048],
                            in_=SCR[:, 0:4096].bitcast(F32)[:, 0:2048]).then_inc(fin, 16)
            n += 16
            G.eng.wait_ge(fin, n)
        G.eng.wait_ge(fin, n)
        ctx.close()
        return nc

    PREV = 4096
    CSL = 12288
    PREVB = 0
    V.wait(dS.tok())
    t_pz = V.tag(V.eng.memset(SCR[:, PREV:PREV + 8192].bitcast(F32), 0.0))
    V.wait(t_pz)
    prevf = SCR[:, PREV:PREV + 8192].bitcast(F32)
    for c in range(NCH):
        t_cl = dS.tag(SY.eng.dma_start(out=SCR[:, CSL:CSL + 4096],
                                       in_=cs_d[c * 128:(c + 1) * 128, :]))
        dcex = DEX[:, c * 64:(c + 1) * 64].unsqueeze(2).broadcast_to([128, 64, 64])
        t_s1 = V.tag(V.eng.tensor_tensor(prevf, prevf, dcex, OP.mult))
        V.waits([t_s1, t_cl])
        t_s2 = V.tag(V.eng.tensor_tensor(prevf, prevf, SCR[:, CSL:CSL + 4096],
                                         OP.add))
        V.wait(t_s2)
        dS.wait(t_s2)
    t_ms = V.tag(V.eng.tensor_scalar(prevf, prevf, COL[:, C_EM:C_EM + 1], None,
                                     OP.mult))
    SY.wait(t_ms)
    t_sw = dS.tag(SY.eng.dma_start(out=cc_s_in, in_=prevf))
    G.wait(t_sw)
    t_sc = CC.tag(G.eng.collective_compute(
        "AllReduce", OP.add, replica_groups=[[0, 1], [2, 3], [4, 5], [6, 7]],
        ins=[cc_s_in.opt()], outs=[cc_s_out.opt()]))
    G.wait(t_sc)
    t_sl2 = dG.tag(G.eng.dma_start(out=prevf, in_=cc_s_out))
    V.wait(t_sl2)
    t_su = V.tag(V.eng.tensor_scalar(prevf, prevf, COL[:, C_HS:C_HS + 1], None,
                                     OP.mult))
    V.wait(t_su)

    if stage == 64:
        fin = ctx.enter_context(nc.semaphore("fin"))
        G.waits([V.tok(), S.tok(), dS.tok(), dG.tok()])
        G.eng.dma_start(out=out[0:128, 0:2048],
                        in_=prevf[:, 0:2048]).then_inc(fin, 16)
        G.eng.dma_start(out=out[128:256, 0:2048],
                        in_=prevf[:, 2048:4096]).then_inc(fin, 16)
        G.eng.dma_start(out=out[256:384, 0:512],
                        in_=DEX[:, 0:512]).then_inc(fin, 16)
        n = 48
        for i in range(8):
            G.eng.dma_start(out=SCR[:, 0:4096],
                            in_=cs_d[i * 128:(i + 1) * 128, :]).then_inc(fin, 16)
            n += 16
            G.eng.wait_ge(fin, n)
            G.eng.dma_start(out=out[(3 + i) * 128:(4 + i) * 128, 0:2048],
                            in_=SCR[:, 0:4096].bitcast(F32)[:, 0:2048]).then_inc(fin, 16)
            n += 16
            G.eng.wait_ge(fin, n)
        G.eng.wait_ge(fin, n)
        ctx.close()
        return nc

    YGA, YNA, YNFA, SZA, YCA, NWA = 0, 8192, 12288, 16384, 24576, 28672
    t_nw = dG.tag(G.eng.dma_start(out=RES[0:1, NWA:NWA + 4096], in_=norm_w))
    V.wait(t_nw)
    tok_nw = p.bcast_row(RES[:, NWA:NWA + 4096], 4096)

    zslab = [WSlot(WB[:, 0:2048]), WSlot(WB[:, 2048:4096])]
    for sl in zslab:
        sl.war = TE.tok()

    for c in range(NCH):
        V.wait(TE.tok())
        t_pb = V.tag(V.eng.tensor_copy(SCR[:, PREVB:PREVB + 4096], prevf))
        V.wait(t_pb)
        t_cl = dS.tag(SY.eng.dma_start(out=SCR[:, CSL:CSL + 4096],
                                       in_=cs_d[c * 128:(c + 1) * 128, :]))
        dcex = DEX[:, c * 64:(c + 1) * 64].unsqueeze(2).broadcast_to([128, 64, 64])
        t_s1 = V.tag(V.eng.tensor_tensor(prevf, prevf, dcex, OP.mult))
        V.waits([t_s1, t_cl])
        t_s2 = V.tag(V.eng.tensor_tensor(prevf, prevf, SCR[:, CSL:CSL + 4096],
                                         OP.add))
        S.wait(V.tok())
        for zm in range(8):
            bks = [p.next_pf() for _ in range(2)]
            for b in range(2):
                TE.wait(p.pf_tok[bks[b]])
            last = None
            for g in range(4):
                sl = zslab[g % 2]
                dS.wait(sl.war)
                toksl = []
                for rr in range(2):
                    kc0 = 4 * g + 2 * rr
                    r = kc0 // 2
                    toksl.append(dS.tag(SY.eng.dma_start(
                        out=sl.ap.rearrange("p (kc m) -> p kc m", m=512)[
                            :, 2 * rr:2 * rr + 2, :],
                        in_=RIP[r][:, :, zm * 512:(zm + 1) * 512])))
                TE.waits(toksl + [tok_xn3])
                for j in range(4):
                    kc = 4 * g + j
                    for lh in range(2):
                        tcb = 2 * c + lh
                        inst = TE.eng.matmul(
                            p.pf[bks[lh]][:],
                            XNF[:, kc * T + tcb * 128:kc * T + (tcb + 1) * 128],
                            sl.ap.rearrange("p (kc m) -> p kc m", m=512)[:, j, :],
                            start=(kc == 0), stop=(kc == 15))
                    last = TE.tag(inst)
                sl.war = last
            for lh in range(2):
                S.waits([last, p.pf_tok[bks[lh]]])
                t_sz = S.tag(S.eng.activation(
                    RES[:, SZA + lh * 4096 + zm * 512:SZA + lh * 4096 + (zm + 1) * 512],
                    p.pf[bks[lh]][:], AF.Silu))
                p.pf_tok[bks[lh]] = t_sz
        for lh in range(2):
            tcb = 2 * c + lh
            ysl = RES[:, YCA:YCA + 4096]
            dS.wait(S.tok())
            t_ydl = dS.tag(SY.eng.dma_start(
                out=ysl, in_=yd_d[c * 256 + lh * 128:c * 256 + (lh + 1) * 128, :]))
            for hg in range(8):
                bi = p.next_pf()
                pk = p.pf[bi]
                TE.waits([V.tok(), p.pf_tok[bi]])
                for h in range(8):
                    hh = hg * 8 + h
                    i_yo = TE.eng.matmul(
                        pk[:, h * 64:(h + 1) * 64],
                        SCR[:, CCM + c * 256 + lh * 128:CCM + c * 256 + (lh + 1) * 128],
                        SCR[:, PREVB + hh * 64:PREVB + (hh + 1) * 64],
                        start=True, stop=True)
                t_mm = TE.tag(i_yo)
                eaex = SCR[:, EXPA + tcb * 64 + hg * 8:EXPA + tcb * 64 + (hg + 1) * 8].\
                    unsqueeze(2).broadcast_to([128, 8, 64])
                V.waits([t_mm, t_ydl])
                t_e1 = V.tag(V.eng.tensor_tensor(DEX[:, 1344:1856], pk[:], eaex,
                                                 OP.mult))
                V.wait(t_e1)
                t_e2 = V.tag(V.eng.tensor_tensor(
                    ysl[:, hg * 512:(hg + 1) * 512], ysl[:, hg * 512:(hg + 1) * 512],
                    DEX[:, 1344:1856], OP.add))
                p.pf_tok[bi] = t_e2
            if stage == 65 and c == 0 and lh == 0:
                fin = ctx.enter_context(nc.semaphore("fin"))
                G.waits([V.tok(), S.tok(), TE.tok(), dS.tok(), dG.tok()])
                G.eng.dma_start(out=out[0:128, 0:2048],
                                in_=ysl[:, 0:2048]).then_inc(fin, 16)
                G.eng.dma_start(out=out[128:256, 0:2048],
                                in_=ysl[:, 2048:4096]).then_inc(fin, 16)
                G.eng.dma_start(out=out[256:384, 0:2048],
                                in_=RES[:, SZA:SZA + 2048]).then_inc(fin, 16)
                G.eng.dma_start(out=out[384:512, 0:2048],
                                in_=RES[:, SZA + 2048:SZA + 4096]).then_inc(fin, 16)
                G.eng.dma_start(out=out[512:640, 0:1024],
                                in_=SCR[:, EXPA:EXPA + 1024]).then_inc(fin, 16)
                G.eng.dma_start(out=out[640:768, 0:2048],
                                in_=SCR[:, PREVB:PREVB + 2048]).then_inc(fin, 16)
                G.eng.wait_ge(fin, 96)
                ctx.close()
                return nc
            ygf = RES[:, YGA:YGA + 8192].bitcast(F32)
            V.wait(S.tok())
            V.wait(V.tok())
            t_yg = V.tag(V.eng.tensor_tensor(
                ygf, ysl, RES[:, SZA + lh * 4096:SZA + (lh + 1) * 4096], OP.mult))
            S.wait(t_yg)
            t_sq = S.tag(S.eng.activation(ysl, ygf, AF.Square,
                                          accum_out=COL[:, 10:11]))
            S.wait(t_sq)
            t_l = S.tag(S.eng.activation(COL[:, 11:12], COL[:, 10:11], AF.Ln,
                                         bias=COL[:, C_EPS:C_EPS + 1],
                                         scale=1.0 / DIN))
            S.wait(t_l)
            t_r = S.tag(S.eng.activation(COL[:, 11:12], COL[:, 11:12], AF.Exp,
                                         scale=-0.5))
            V.waits([t_r, tok_nw])
            t_n1 = V.tag(V.eng.tensor_scalar(ygf, ygf, COL[:, 11:12], None,
                                             OP.mult))
            V.wait(t_n1)
            t_n2 = V.tag(V.eng.tensor_tensor(RES[:, YNA:YNA + 4096], ygf,
                                             RES[:, NWA:NWA + 4096], OP.mult))
            for xc in range(16):
                bi = p.next_pb()
                TE.waits([t_n2, p.pb_tok[bi]])
                i1 = TE.eng.transpose(p.pb[bi][:, 0:128],
                                      RES[:, YNA + (2 * xc) * 128:YNA + (2 * xc + 1) * 128],
                                      CID[:])
                i2 = TE.eng.transpose(p.pb[bi][:, 128:256],
                                      RES[:, YNA + (2 * xc + 1) * 128:YNA + (2 * xc + 2) * 128],
                                      CID[:])
                t_tr = TE.tag(i2)
                V.wait(t_tr)
                inst = V.eng.tensor_copy(RES[:, YNFA + xc * 256:YNFA + (xc + 1) * 256],
                                         p.pb[bi][:, 0:256])
                p.pb_tok[bi] = V.tag(inst)
            tok_ynf = V.tok()
            dS.wait(tok_ynf)
            t_x2l = dS.tag(SY.eng.dma_start(
                out=RES[:, YNA:YNA + 2048],
                in_=x2_d[c * 256 + lh * 128:c * 256 + (lh + 1) * 128, :]))
            for ot in range(4):
                bi = p.next_pf()
                pk = p.pf[bi]
                TE.waits([tok_ynf, p.pf_tok[bi]])
                last = None
                for g in range(8):
                    sl = zslab[g % 2]
                    dS.wait(sl.war)
                    r = g
                    t_w = dS.tag(SY.eng.dma_start(
                        out=sl.ap.rearrange("p (kc m) -> p kc m", m=512),
                        in_=ROP[r][:, :, ot * 512:(ot + 1) * 512]))
                    TE.wait(t_w)
                    for j in range(4):
                        ic = 4 * g + j
                        inst = TE.eng.matmul(
                            pk[:], RES[:, YNFA + ic * 128:YNFA + (ic + 1) * 128],
                            sl.ap.rearrange("p (kc m) -> p kc m", m=512)[:, j, :],
                            start=(ic == 0), stop=(ic == 31))
                    last = TE.tag(inst)
                    sl.war = last
                V.waits([last, t_x2l])
                t_ev = V.tag(V.eng.tensor_tensor(
                    RES[:, YNA + ot * 512:YNA + (ot + 1) * 512],
                    RES[:, YNA + ot * 512:YNA + (ot + 1) * 512], pk[:], OP.add))
                p.pf_tok[bi] = t_ev
            SY.wait(V.tok())
            t_x3 = dS.tag(SY.eng.dma_start(
                out=x3_d[c * 256 + lh * 128:c * 256 + (lh + 1) * 128, :],
                in_=RES[:, YNA:YNA + 2048]))
            V.wait(t_x3)
    tok_x3 = dS.tok()

    if stage == 7:
        fin = ctx.enter_context(nc.semaphore("fin"))
        G.wait(tok_x3)
        n = 0
        for tc in range(NTC):
            G.eng.dma_start(out=SCR[:, 0:D],
                            in_=x3_d[tc * 128:(tc + 1) * 128, :]).then_inc(fin, 16)
            n += 16
            G.eng.wait_ge(fin, n)
            G.eng.dma_start(out=out[tc * 128:(tc + 1) * 128, :],
                            in_=SCR[:, 0:D]).then_inc(fin, 16)
            n += 16
            G.eng.wait_ge(fin, n)
        ctx.close()
        return nc

    dS.waits([V.tok(), S.tok(), TE.tok()])
    S.wait(V.tok())
    V.wait(S.tok())
    C_RS8, C_SWS = 33, 34
    t_s8 = dS.tag(SY.eng.dma_start(out=COL[0:1, C_RS8:C_RS8 + 2], in_=scal8))
    V.wait(t_s8)
    V.eng.tensor_copy(COL[32:33, C_RS8:C_RS8 + 2], COL[0:1, C_RS8:C_RS8 + 2])
    V.eng.tensor_copy(COL[64:65, C_RS8:C_RS8 + 2], COL[0:1, C_RS8:C_RS8 + 2])
    t_cc3 = V.tag(V.eng.tensor_copy(COL[96:97, C_RS8:C_RS8 + 2],
                                    COL[0:1, C_RS8:C_RS8 + 2]))
    V.wait(t_cc3)
    t_sh8 = V.tag(V.eng.stream_shuffle(COL[:, C_RS8:C_RS8 + 2],
                                       COL[:, C_RS8:C_RS8 + 2], [0] * 32))
    V.wait(t_sh8)
    t_sws = V.tag(V.eng.tensor_tensor(COL[:, C_SWS:C_SWS + 1],
                                      COL[:, C_RS8:C_RS8 + 1],
                                      COL[:, C_SWS:C_SWS + 1], OP.mult))
    V.wait(t_sws)
    t_rs8 = V.tag(V.eng.reciprocal(COL[:, C_RS8:C_RS8 + 1], COL[:, C_RS8:C_RS8 + 1]))
    V.wait(t_rs8)

    V.wait(TE.tok())
    for tc in range(NTC):
        t_ld = dS.tag(SY.eng.dma_start(out=SCR[:, 0:2048],
                                       in_=x3_d[tc * 128:(tc + 1) * 128, :]))
        V.wait(t_ld)
        t_r = ln_cols(SCR[:, 0:2048], C_M1, C_R1)
        V.wait(t_r)
        t_x = V.tag(V.eng.tensor_scalar(sf32(4096, 2048), SCR[:, 0:2048],
                                        COL[:, C_M1:C_M1 + 1], COL[:, C_R1:C_R1 + 1],
                                        OP.subtract, OP.mult))
        V.wait(t_x)
        t_u = V.tag(V.eng.tensor_scalar(sf32(8192, 2048), sf32(4096, 2048),
                                        COL[:, C_RS8:C_RS8 + 1], 240.0,
                                        OP.mult, OP.min))
        V.wait(t_u)
        t_q8 = V.tag(V.eng.tensor_scalar(SCR[:, 12288:13312].bitcast(FP8),
                                         sf32(8192, 2048), -240.0, None, OP.max))
        V.wait(t_q8)
        t_dec = V.tag(V.eng.tensor_copy(SCR[:, 16384:18432],
                                        SCR[:, 12288:13312].bitcast(FP8)))
        for half in range(2):
            bi = p.next_pb()
            TE.waits([t_dec, p.pb_tok[bi]])
            for j in range(8):
                kc = half * 8 + j
                inst = TE.eng.transpose(p.pb[bi][:, j * 128:(j + 1) * 128],
                                        SCR[:, 16384 + kc * 128:16384 + (kc + 1) * 128],
                                        CID[:])
            t_tr = TE.tag(inst)
            V.wait(t_tr)
            for j in range(8):
                kc = half * 8 + j
                inst = V.eng.tensor_copy(
                    XNF[:, kc * T + tc * 128:kc * T + (tc + 1) * 128],
                    p.pb[bi][:, j * 128:(j + 1) * 128])
            p.pb_tok[bi] = V.tag(inst)
        dS.wait(V.tok())
    tok_xq8 = V.tok()

    w8slab = [WSlot(WB[:, 0:2048]), WSlot(WB[:, 2048:4096])]
    for sl in w8slab:
        sl.war = TE.tok()
    for tt in range(4):
        SY.wait(V.tok())
        x3toks = []
        for b in range(4):
            tc = tt * 4 + b
            x3toks.append(dS.tag(SY.eng.dma_start(
                out=SCR[:, b * 2048:(b + 1) * 2048],
                in_=x3_d[tc * 128:(tc + 1) * 128, :])))
        for ot in range(4):
            bks = [p.next_pf() for _ in range(4)]
            for b in range(4):
                TE.wait(p.pf_tok[bks[b]])
            last = None
            for g in range(4):
                sl = w8slab[g % 2]
                dS.wait(sl.war)
                toks = []
                for rr in range(2):
                    kc0 = 4 * g + 2 * rr
                    r = kc0 // 2
                    toks.append(dS.tag(SY.eng.dma_start(
                        out=sl.ap.rearrange("p (kc m) -> p kc m", m=512)[
                            :, 2 * rr:2 * rr + 2, :],
                        in_=RW8[r][:, :, ot * 512:(ot + 1) * 512])))
                TE.waits(toks + [tok_xq8])
                for j in range(4):
                    ic = 4 * g + j
                    for b in range(4):
                        tc = tt * 4 + b
                        inst = TE.eng.matmul(
                            p.pf[bks[b]][:],
                            XNF[:, ic * T + tc * 128:ic * T + (tc + 1) * 128],
                            sl.ap.rearrange("p (kc m) -> p kc m", m=512)[:, j, :],
                            start=(ic == 0), stop=(ic == 15))
                    last = TE.tag(inst)
                sl.war = last
            for b in range(4):
                tc = tt * 4 + b
                V.waits([last, x3toks[b], p.pf_tok[bks[b]]])
                t_e = V.tag(V.eng.scalar_tensor_tensor(
                    SC2[:, 0:512],
                    p.pf[bks[b]][:], COL[:, C_SWS:C_SWS + 1],
                    SCR[:, b * 2048 + ot * 512:b * 2048 + (ot + 1) * 512],
                    OP.mult, OP.add))
                p.pf_tok[bks[b]] = t_e
                SY.wait(t_e)
                t_st = dS.tag(SY.eng.dma_start(
                    out=out[tc * 128:(tc + 1) * 128, ot * 512:(ot + 1) * 512],
                    in_=SC2[:, 0:512]))
                V.wait(t_st)
    SY.wait(dS.tok())
    G.wait(dS.tok())

    ctx.close()
    return nc



def _rot_tables(tok0):
    rot = D // 2
    inv = 1.0 / (10000.0 ** (np.arange(0, rot, 2, dtype=np.float32) / rot))
    fr = (np.arange(tok0, tok0 + T, dtype=np.float32)[:, None] * inv[None, :])
    fr = np.repeat(fr, 2, axis=-1)
    return np.cos(fr).astype(BFnp), np.sin(fr).astype(BFnp)


def _consts():
    idn = np.eye(128, dtype=np.float32)
    tri = np.zeros((128, 3 * 128), np.float32)
    r = np.arange(128)
    tri[:, 0:128] = (r[:, None] <= r[None, :])
    tri[:, 128:256] = (r[:, None] > r[None, :])
    tri[:, 256:384] = 1.0
    smask = (np.arange(256)[:, None] <= np.arange(256)[None, :]).astype(np.float32)
    return idn.astype(BFnp), tri.astype(BFnp), smask.astype(BFnp)


_NC_CACHE = {}


def kernel(L0_attn_X, L0_attn_W_in, L0_attn_b_in, L0_attn_W_out, L0_attn_b_out,
           L0_attn_gamma, L0_attn_beta, L0_ffn_w_gate_up, L0_ffn_w_down,
           L1_attn_in_proj_w, L1_attn_conv_w, L1_attn_conv_b, L1_attn_dt_bias,
           L1_attn_A_log, L1_attn_D, L1_attn_norm_w, L1_attn_out_proj_w,
           L1_ffn_weight, L1_ffn_weight_scale, L1_ffn_input_scale, _stage=99):
    idn, tri, smask = _consts()
    w8v = np.ascontiguousarray(L1_ffn_weight).view(ml_dtypes.float8_e4m3)
    in_maps = []
    for c in range(8):
        b, h = c // 2, c % 2
        cosb, sinb = _rot_tables(h * T)
        m = {
            "x_in": np.ascontiguousarray(np.asarray(L0_attn_X)[b, h * T:(h + 1) * T, :]),
            "w_in": np.ascontiguousarray(L0_attn_W_in),
            "w_outw": np.ascontiguousarray(L0_attn_W_out),
            "b_in": np.ascontiguousarray(L0_attn_b_in)[None, :],
            "b_out": np.ascontiguousarray(L0_attn_b_out)[None, :],
            "gamma": np.ascontiguousarray(L0_attn_gamma)[None, :],
            "beta": np.ascontiguousarray(L0_attn_beta)[None, :],
            "cosd": cosb, "sind": sinb,
            "idn": idn, "trilm": tri, "smaskd": smask,
            "w1_rows": np.ascontiguousarray(L0_ffn_w_gate_up[c * W1R:(c + 1) * W1R, :]),
            "w2_rows": np.ascontiguousarray(L0_ffn_w_down[c * W2R:(c + 1) * W2R, :]),
            "ip_cols": np.ascontiguousarray(L1_attn_in_proj_w[:, c * IPC:(c + 1) * IPC]),
            "op_cols": np.ascontiguousarray(L1_attn_out_proj_w[:, c * OPC:(c + 1) * OPC]),
            "w8_cols": np.ascontiguousarray(w8v[:, c * W8C:(c + 1) * W8C]),
            "conv_w": np.ascontiguousarray(L1_attn_conv_w),
            "conv_b": np.ascontiguousarray(L1_attn_conv_b)[None, :],
            "dt_bias": np.ascontiguousarray(L1_attn_dt_bias)[None, :],
            "a_log": np.ascontiguousarray(L1_attn_A_log)[None, :],
            "d_vec": np.ascontiguousarray(L1_attn_D)[None, :],
            "norm_w": np.ascontiguousarray(L1_attn_norm_w)[None, :],
            "scal8": np.array([[float(np.asarray(L1_ffn_input_scale,
                                                 dtype=np.float32)),
                                float(np.asarray(L1_ffn_weight_scale))]], np.float32),
            "masks": np.array([[1.0 - h, float(h)]], np.float32),
        }
        in_maps.append(m)
    nc = build_program(_stage)
    res = run_bass_kernel_spmd(nc, in_maps, core_ids=list(range(8)))
    outp = np.zeros((4, 4096, D), np.float32)
    for c in range(8):
        b, h = c // 2, c % 2
        outp[b, h * T:(h + 1) * T, :] = res.results[c]["out"]
    return outp


# version 04 (final)
# speedup vs baseline: 1.2649x; 1.2649x over previous
import numpy as np
import ml_dtypes
from contextlib import ExitStack

import concourse.bass as bass
import concourse.mybir as mybir
from concourse.alu_op_type import AluOpType as OP
from concourse.bass_utils import run_bass_kernel_spmd

F32 = mybir.dt.float32
F16 = mybir.dt.float16
BF16 = mybir.dt.bfloat16
I32 = mybir.dt.int32
FP8 = mybir.dt.float8e4
AF = mybir.ActivationFunctionType
AX = mybir.AxisListType
BFnp = ml_dtypes.bfloat16

T = 2048
D = 2048
NTC = T // 128
NKC = D // 128
I_FFN = 5632
M_GUP = 2 * I_FFN
DIN, DSTATE, NH, HDIM, CHUNK = 4096, 128, 64, 64, 256
CONVD = DIN + 2 * DSTATE
NCH = T // CHUNK
MPROJ = 2 * (DIN + DSTATE) + NH
EPS = 1e-5

W1R = M_GUP // 8
W2R = D // 8
IPC = D // 8
OPC = DIN // 8
W8C = D // 8

O_W1 = 0
O_W2 = O_W1 + D * W1R
O_IP = O_W2 + I_FFN * W2R
O_OP = O_IP + IPC * MPROJ
O_W8 = O_OP + OPC * D
RANKB_ELEMS = O_W8 + W8C * D


class Q:

    def __init__(self, eng, sem, step=1, extra=None):
        self.eng = eng
        self.sem = sem
        self.step = step
        self.cnt = 0
        self.seen = {}
        self.extra = list(extra or [])
        self.dma = step == 16

    def wait(self, tok):
        if tok is None:
            return
        sem, val = tok
        key = id(sem)
        if self.seen.get(key, 0) >= val:
            return
        self.eng.wait_ge(sem, val)
        self.seen[key] = val

    def waits(self, toks):
        for t in toks:
            self.wait(t)

    def tag(self, inst):
        if self.cnt + self.step > 60000 and self.extra:
            self.sem = self.extra.pop(0)
            self.cnt = 0
        self.cnt += self.step
        inst.then_inc(self.sem, self.step)
        if self.dma:
            self.eng.wait_ge(self.sem, self.cnt)
            self.seen[id(self.sem)] = self.cnt
        return (self.sem, self.cnt)

    def tok(self):
        return (self.sem, self.cnt)


class WSlot:

    def __init__(self, ap):
        self.ap = ap
        self.war = None
        self.sem = None
        self.cnt = 0


class PB:

    def __init__(self, nc, ctx):
        self.nc = nc
        self.ctx = ctx
        sem = lambda n: ctx.enter_context(nc.semaphore(n))
        self.V = Q(nc.vector, sem("sv"))
        self.S = Q(nc.scalar, sem("ss"))
        self.T = Q(nc.tensor, sem("stn"))
        self.G = Q(nc.gpsimd, sem("sgp"))
        self.SY = Q(nc.sync, sem("ssy"))
        self.dG = Q(nc.gpsimd, sem("dg"), 16,
                    extra=[sem("dg2"), sem("dg3")])
        self.dS = Q(nc.sync, sem("dsy"), 16,
                    extra=[sem("dsy2")])
        self.dA = Q(nc.scalar, sem("dact"), 16)
        self.CC = Q(nc.gpsimd, sem("scc"))
        self.pf = [ctx.enter_context(nc.psum_tensor(f"pf{i}", [128, 512], F32))
                   for i in range(6)]
        self.pb = [ctx.enter_context(nc.psum_tensor(f"pb{i}", [128, 1024], BF16))
                   for i in range(2)]
        self.pf_tok = [None] * 6
        self.pb_tok = [None] * 2
        self.pf_i = 0
        self.pb_i = 0

    def sb(self, name, shape, dtype):
        return self.ctx.enter_context(self.nc.sbuf_tensor(name, shape, dtype))

    def next_pf(self):
        i = self.pf_i
        self.pf_i = (self.pf_i + 1) % 6
        return i

    def next_pb(self):
        i = self.pb_i
        self.pb_i = (self.pb_i + 1) % 2
        return i

    def bcast_row(self, dst, n):
        V = self.V
        V.eng.tensor_copy(dst[32:33, 0:n], dst[0:1, 0:n])
        V.eng.tensor_copy(dst[64:65, 0:n], dst[0:1, 0:n])
        t = V.tag(V.eng.tensor_copy(dst[96:97, 0:n], dst[0:1, 0:n]))
        V.wait(t)
        return V.tag(V.eng.stream_shuffle(dst[:, 0:n], dst[:, 0:n], [0] * 32))

    def slot_dma(self, dq, sl, out, in_, **kw):
        if sl.sem is None:
            self._wsid = getattr(self, "_wsid", 0) + 1
            sl.sem = self.ctx.enter_context(self.nc.semaphore("ws%d" % self._wsid))
        if sl.cnt:
            dq.eng.wait_ge(sl.sem, sl.cnt)
        inst = dq.eng.dma_start(out=out, in_=in_, **kw)
        sl.cnt += 16
        inst.then_inc(sl.sem, 16)
        return (sl.sem, sl.cnt)

    def stream_mm(self, pk_ap, nk, res_fn, src_fn, slots, stream_is_lhsT,
                  pre_toks=(), dq=None):
        TE = self.T
        dq = dq or self.dG
        for tk in pre_toks:
            TE.wait(tk)
        t_mm = None
        for kc in range(nk):
            sl = slots[kc % len(slots)]
            dq.wait(sl.war)
            wt = self.slot_dma(dq, sl, sl.ap, src_fn(kc))
            TE.wait(wt)
            if stream_is_lhsT:
                inst = TE.eng.matmul(pk_ap, sl.ap, res_fn(kc),
                                     start=(kc == 0), stop=(kc == nk - 1))
            else:
                inst = TE.eng.matmul(pk_ap, res_fn(kc), sl.ap,
                                     start=(kc == 0), stop=(kc == nk - 1))
            t_mm = TE.tag(inst)
            sl.war = t_mm
        return t_mm


def build_program(stage):
    nc = bass.Bass("TRN2", target_bir_lowering=False)
    ctx = ExitStack()

    def dram(n, shp, dt, kind=None):
        if kind:
            return nc.dram_tensor(n, shp, dt, kind=kind).ap()
        return nc.dram_tensor(n, shp, dt).ap()

    x_in = dram("x_in", [T, D], F32, "ExternalInput")
    w_in = dram("w_in", [D, 3 * D], F32, "ExternalInput")
    w_outw = dram("w_outw", [D, D], F32, "ExternalInput")
    b_in = dram("b_in", [1, 3 * D], F32, "ExternalInput")
    b_out = dram("b_out", [1, D], F32, "ExternalInput")
    gamma = dram("gamma", [1, D], F32, "ExternalInput")
    beta = dram("beta", [1, D], F32, "ExternalInput")
    cosd = dram("cosd", [T, 1024], BF16, "ExternalInput")
    sind = dram("sind", [T, 1024], BF16, "ExternalInput")
    idn = dram("idn", [128, 128], BF16, "ExternalInput")
    trilm = dram("trilm", [128, 3 * 128], BF16, "ExternalInput")
    smaskd = dram("smaskd", [256, 256], BF16, "ExternalInput")
    w1_rows = dram("w1_rows", [W1R, D], F32, "ExternalInput")
    w2_rows = dram("w2_rows", [W2R, I_FFN], F32, "ExternalInput")
    ip_cols = dram("ip_cols", [MPROJ, IPC], F32, "ExternalInput")
    op_cols = dram("op_cols", [D, OPC], F32, "ExternalInput")
    w8_cols = dram("w8_cols", [D, W8C], FP8, "ExternalInput")
    conv_w = dram("conv_w", [CONVD, 4], F32, "ExternalInput")
    conv_b = dram("conv_b", [1, CONVD], F32, "ExternalInput")
    dt_bias = dram("dt_bias", [1, NH], F32, "ExternalInput")
    a_log = dram("a_log", [1, NH], F32, "ExternalInput")
    d_vec = dram("d_vec", [1, NH], F32, "ExternalInput")
    norm_w = dram("norm_w", [1, DIN], F32, "ExternalInput")
    scal8 = dram("scal8", [1, 2], F32, "ExternalInput")
    masks = dram("masks", [1, 2], F32, "ExternalInput")

    out = dram("out", [T, D], F32, "ExternalOutput")

    cc_a_in = dram("cc_a_in", [1, D], F32)
    cc_a_out = dram("cc_a_out", [1, D], F32)
    cc_h_in = dram("cc_h_in", [128, 34 * 3], F32)
    cc_h_out = dram("cc_h_out", [128, 34 * 3], F32)
    cc_s_in = dram("cc_s_in", [128, NH * HDIM], F32)
    cc_s_out = dram("cc_s_out", [128, NH * HDIM], F32)
    RANKB = RANKB_ELEMS
    cc_w_in = dram("cc_w_in", [1, RANKB], BF16)
    cc_w_out = dram("cc_w_out", [8, RANKB], BF16)
    xbcraw_d = dram("xbcraw_d", [CONVD, T], BF16)
    xconv_d = dram("xconv_d", [DIN, T], BF16)
    res_spill = dram("res_spill", [T, D], BF16)
    x2_d = dram("x2_d", [T, D], BF16)
    x3_d = dram("x3_d", [T, D], BF16)
    cs_d = dram("cs_d", [NCH * 128, NH * HDIM], BF16)
    yd_d = dram("yd_d", [T, NH * HDIM], BF16)

    p = PB(nc, ctx)
    V, S, TE, G, SY = p.V, p.S, p.T, p.G, p.SY
    dG, dS, dA, CC = p.dG, p.dS, p.dA, p.CC

    RES = p.sb("RES", [128, NTC * D], BF16)
    XNF = p.sb("XNF", [128, NKC * T], BF16)
    SCR = p.sb("SCR", [128, 22528], BF16)
    SC2 = p.sb("SC2", [128, 4608], F32)
    WB = p.sb("WB", [128, 4096], BF16)
    COL = p.sb("COL", [128, 80], F32)
    CID = p.sb("CID", [128, 128], BF16)
    CTRI = p.sb("CTRI", [128, 3 * 128], BF16)
    CSM = p.sb("CSM", [128, 2 * 256], BF16)
    DEX = p.sb("DEX", [128, 2176], BF16)

    C_EPS, C_M1, C_R1, C_M2, C_R2 = 0, 1, 3, 4, 6
    C_STAT = 8
    C_SSK, C_SSV, C_RNK, C_RNV, C_RN, C_TMP = 33, 34, 35, 36, 37, 38
    C_ONE = 39
    C_ACOL = 40
    C_BQ = 56

    ws_a = [WSlot(WB[:, 0:512]), WSlot(WB[:, 512:1024])]
    ws_b = [WSlot(WB[:, 1024:1536]), WSlot(WB[:, 1536:2048])]
    ws_c = [WSlot(WB[:, 2048:2560]), WSlot(WB[:, 2560:3072])]
    ws_q = [WSlot(WB[:, 3072:3200]), WSlot(WB[:, 3200:3328])]

    _areas = [(RES, NTC * D), (XNF, NKC * T), (SCR, 22528), (WB, 4096),
              (CID, 128), (CTRI, 3 * 128), (CSM, 2 * 256), (DEX, 2176),
              (SC2, 4608), (COL, 80)]
    _mi = None
    for _arena, _w in _areas:
        for _c0 in range(0, _w, 8192):
            _mi = V.eng.memset(_arena[:, _c0:min(_c0 + 8192, _w)], 0.0)
    tok_init = V.tag(_mi)
    for _q in (V, S, TE, G, SY, dS, dG, dA, CC):
        _q.wait(tok_init)

    ld_id = dS.tag(SY.eng.dma_start(out=CID[:], in_=idn))
    dS.tag(SY.eng.dma_start(out=CTRI[:], in_=trilm))
    dS.tag(SY.eng.dma_start(out=CSM[:, 0:256], in_=smaskd[0:128, :]))
    ld_sm = dS.tag(SY.eng.dma_start(out=CSM[:, 256:512], in_=smaskd[128:256, :]))
    V.eng.memset(COL[:, C_EPS:C_EPS + 1], EPS)
    V.eng.memset(COL[:, C_ONE:C_ONE + 1], 1.0)
    tok_eps = V.tag(V.eng.memset(COL[:, C_TMP:C_TMP + 1], 0.0))
    ld_bq = dS.tag(SY.eng.dma_start(
        out=COL[:, C_BQ:C_BQ + 16],
        in_=b_in.rearrange("o (kc p) -> (o p) kc", p=128)[:, 0:16],
        allow_slow_non_contiguous=True))

    XB = XNF

    def xf32(c0, w):
        return XB[:, c0:c0 + 2 * w].bitcast(F32)

    def xi32(c0, w):
        return XB[:, c0:c0 + 2 * w].bitcast(I32)

    PA0, PA1, PA2, PA3 = 72, 73, 74, 75
    tok_prep_col = V.tag(V.eng.memset(COL[:, 76:77], 0.0))

    def quant_chain(wt_f32, qi_i32, rf_f32, qf_f32, out_bf, n, amax_src=None):
        V.wait(dA.tok())
        """Quantize wt (fp32 [?,n]) -> dequantized bf16 out. Uses COL PA0..3.
        amax_src: if given, per-row amax was precomputed into COL[PA1]=s,
        COL[PA2]=rs, COL[PA3]=-256*s already."""
        if amax_src is None:
            t1 = V.tag(V.eng.tensor_reduce(COL[:, PA0:PA0 + 1], wt_f32, AX.X,
                                           OP.max, apply_absolute_value=True))
            V.wait(t1)
            t2 = V.tag(V.eng.tensor_scalar(COL[:, PA1:PA1 + 1], COL[:, PA0:PA0 + 1],
                                           1.0 / 127.0, 1e-8, OP.mult, OP.max))
            V.wait(t2)
            t3 = V.tag(V.eng.reciprocal(COL[:, PA2:PA2 + 1], COL[:, PA1:PA1 + 1]))
            t4 = V.tag(V.eng.tensor_scalar(COL[:, PA3:PA3 + 1], COL[:, PA1:PA1 + 1],
                                           -256.0, None, OP.mult))
            V.waits([t3, t4])
        t5 = V.tag(V.eng.tensor_scalar(qi_i32, wt_f32, COL[:, PA2:PA2 + 1], 256.0,
                                       OP.mult, OP.add))
        V.wait(t5)
        t6 = V.tag(V.eng.scalar_tensor_tensor(rf_f32, wt_f32, COL[:, PA2:PA2 + 1],
                                              qi_i32, OP.mult, OP.subtract))
        V.wait(t6)
        t7 = V.tag(V.eng.scalar_tensor_tensor(qf_f32, rf_f32, -255.5, qi_i32,
                                              OP.is_ge, OP.add))
        V.wait(t7)
        t8 = V.tag(V.eng.tensor_scalar(qf_f32, qf_f32, 129.0, 383.0,
                                       OP.max, OP.min))
        V.wait(t8)
        t9 = V.tag(V.eng.tensor_scalar(out_bf, qf_f32, COL[:, PA1:PA1 + 1],
                                       COL[:, PA3:PA3 + 1], OP.mult, OP.add))
        return t9

    def prep_transpose_store(src_bf, nkt, stage_buf, stage_c0, dst_view):
        V.wait(dA.tok())
        nleft = nkt
        j0 = 0
        while nleft > 0:
            nb = min(8, nleft)
            bi = p.next_pb()
            TE.waits([V.tok(), ld_id, p.pb_tok[bi]])
            for j in range(nb):
                kc = j0 + j
                inst = TE.eng.transpose(p.pb[bi][:, j * 128:(j + 1) * 128],
                                        src_bf[:, kc * 128:(kc + 1) * 128], CID[:])
            t_tr = TE.tag(inst)
            V.wait(t_tr)
            for j in range(nb):
                kc = j0 + j
                inst = V.eng.tensor_copy(
                    stage_buf[:, stage_c0 + kc * 128:stage_c0 + (kc + 1) * 128],
                    p.pb[bi][:, j * 128:(j + 1) * 128])
            p.pb_tok[bi] = V.tag(inst)
            j0 += nb
            nleft -= nb
        S.wait(V.tok())
        return dA.tag(S.eng.dma_start(
            out=dst_view,
            in_=stage_buf[:, stage_c0:stage_c0 + nkt * 128].rearrange(
                "p (kc j) -> p kc j", j=128)))

    prep_store_toks = []
    rr1 = cc_w_in[0:1, O_W1:O_W1 + D * W1R].rearrange(
        "o (kc pp m) -> (o pp) kc m", pp=128, m=W1R)
    for rc in range(W1R // 128):
        dS.wait(V.tok())
        t_ld = dS.tag(SY.eng.dma_start(out=xf32(0, 2048),
                                       in_=w1_rows[rc * 128:(rc + 1) * 128, :]))
        V.waits([t_ld, tok_prep_col])
        quant_chain(xf32(0, 2048), xi32(4096, 2048), xf32(8192, 2048),
                    xf32(12288, 2048), XB[:, 16384:18432], 2048)
        t_st = prep_transpose_store(XB[:, 16384:18432], 16, XB, 18432,
                                    rr1[:, :, rc * 128:(rc + 1) * 128])
        prep_store_toks.append(t_st)

    rr2 = cc_w_in[0:1, O_W2:O_W2 + I_FFN * W2R].rearrange(
        "o (ic pp m) -> (o pp) ic m", pp=128, m=W2R)
    for rc in range(W2R // 128):
        dS.wait(V.tok())
        t_ld = dS.tag(SY.eng.dma_start(out=xf32(0, 5632),
                                       in_=w2_rows[rc * 128:(rc + 1) * 128, :]))
        V.wait(t_ld)
        t1 = V.tag(V.eng.tensor_reduce(COL[:, PA0:PA0 + 1], xf32(0, 5632), AX.X,
                                       OP.max, apply_absolute_value=True))
        V.wait(t1)
        t2 = V.tag(V.eng.tensor_scalar(COL[:, PA1:PA1 + 1], COL[:, PA0:PA0 + 1],
                                       1.0 / 127.0, 1e-8, OP.mult, OP.max))
        V.wait(t2)
        t3 = V.tag(V.eng.reciprocal(COL[:, PA2:PA2 + 1], COL[:, PA1:PA1 + 1]))
        t4 = V.tag(V.eng.tensor_scalar(COL[:, PA3:PA3 + 1], COL[:, PA1:PA1 + 1],
                                       -256.0, None, OP.mult))
        V.waits([t3, t4])
        for hh in range(2):
            w0 = hh * 2816
            quant_chain(xf32(0, 5632)[:, w0:w0 + 2816],
                        xi32(11264, 2816), xf32(16896, 2816), xf32(22528, 2816),
                        XB[:, 28160:30976], 2816, amax_src=True)
            t_st = prep_transpose_store(
                XB[:, 28160:30976], 22, SCR, 0,
                rr2[:, hh * 22:(hh + 1) * 22, rc * 128:(rc + 1) * 128])
            prep_store_toks.append(t_st)

    rrip = cc_w_in[0:1, O_IP:O_IP + IPC * MPROJ].rearrange(
        "o (kh pp m) -> (o pp) kh m", pp=128, m=MPROJ)
    for mc in range(67):
        rows = 128 if mc < 66 else MPROJ - 66 * 128
        dG.wait(V.tok())
        dG.wait(TE.tok())
        t_ld = dG.tag(G.eng.dma_start(
            out=XB[0:rows, 0:256], in_=ip_cols[mc * 128:mc * 128 + rows, :]))
        V.wait(t_ld)
        bi = p.next_pb()
        TE.waits([t_ld, ld_id, p.pb_tok[bi], V.tok()])
        for kh in range(2):
            inst = TE.eng.transpose(p.pb[bi][0:128, kh * 128:kh * 128 + rows],
                                    XB[0:rows, kh * 128:(kh + 1) * 128],
                                    CID[0:rows, 0:rows])
        t_tr = TE.tag(inst)
        V.wait(t_tr)
        V.wait(dA.tok())
        for kh in range(2):
            inst = V.eng.tensor_copy(XB[:, 512 + kh * 128:512 + kh * 128 + rows],
                                     p.pb[bi][0:128, kh * 128:kh * 128 + rows])
        p.pb_tok[bi] = V.tag(inst)
        S.wait(V.tok())
        t_st = dA.tag(S.eng.dma_start(
            out=rrip[:, :, mc * 128:mc * 128 + rows],
            in_=XB[:, 512:768].rearrange("p (kh j) -> p kh j", j=128)[:, :, 0:rows]))
        prep_store_toks.append(t_st)

    rrop = cc_w_in[0:1, O_OP:O_OP + OPC * D].rearrange(
        "o (kh pp m) -> (o pp) kh m", pp=128, m=D)
    for mc in range(16):
        dG.wait(V.tok())
        dG.wait(TE.tok())
        t_ld = dG.tag(G.eng.dma_start(out=XB[:, 0:512],
                                      in_=op_cols[mc * 128:(mc + 1) * 128, :]))
        V.wait(t_ld)
        bi = p.next_pb()
        TE.waits([t_ld, ld_id, p.pb_tok[bi], V.tok()])
        for kh in range(4):
            inst = TE.eng.transpose(p.pb[bi][:, kh * 128:(kh + 1) * 128],
                                    XB[:, kh * 128:(kh + 1) * 128], CID[:])
        t_tr = TE.tag(inst)
        V.wait(t_tr)
        V.wait(dA.tok())
        for kh in range(4):
            inst = V.eng.tensor_copy(XB[:, 1024 + kh * 128:1024 + (kh + 1) * 128],
                                     p.pb[bi][:, kh * 128:(kh + 1) * 128])
        p.pb_tok[bi] = V.tag(inst)
        S.wait(V.tok())
        t_st = dA.tag(S.eng.dma_start(
            out=rrop[:, :, mc * 128:(mc + 1) * 128],
            in_=XB[:, 1024:1536].rearrange("p (kh j) -> p kh j", j=128)))
        prep_store_toks.append(t_st)

    rr8 = cc_w_in[0:1, O_W8:O_W8 + W8C * D].rearrange(
        "o (kh pp m) -> (o pp) kh m", pp=128, m=D)
    s8t = SCR[:, 8192:8320].bitcast(FP8)
    for mc in range(16):
        dS.wait(V.tok())
        dS.wait(TE.tok())
        t_ld = dS.tag(SY.eng.dma_start(out=s8t[:, 0:256],
                                       in_=w8_cols[mc * 128:(mc + 1) * 128, :]))
        V.wait(t_ld)
        t_cv = V.tag(V.eng.tensor_copy(XB[:, 0:256], s8t[:, 0:256]))
        V.wait(t_cv)
        bi = p.next_pb()
        TE.waits([ld_id, p.pb_tok[bi], V.tok()])
        for kh in range(2):
            inst = TE.eng.transpose(p.pb[bi][:, kh * 128:(kh + 1) * 128],
                                    XB[:, kh * 128:(kh + 1) * 128], CID[:])
        t_tr = TE.tag(inst)
        V.wait(t_tr)
        V.wait(dA.tok())
        for kh in range(2):
            inst = V.eng.tensor_copy(XB[:, 512 + kh * 128:512 + (kh + 1) * 128],
                                     p.pb[bi][:, kh * 128:(kh + 1) * 128])
        p.pb_tok[bi] = V.tag(inst)
        S.wait(V.tok())
        t_st = dA.tag(S.eng.dma_start(
            out=rr8[:, :, mc * 128:(mc + 1) * 128],
            in_=XB[:, 512:768].rearrange("p (kh j) -> p kh j", j=128)))
        prep_store_toks.append(t_st)

    G.waits(prep_store_toks)
    tok_wblob = CC.tag(G.eng.collective_compute(
        "AllGather", OP.bypass, replica_groups=[list(range(8))],
        ins=[cc_w_in.opt()],
        outs=[cc_w_out.opt()]))
    tok_prep_v = V.tok()
    tok_prep_dS = dS.tok()

    GBG, GBB, TMP1, TMP2, BOUT, BKV, P1S = 0, 2048, 4096, 6144, 8192, 10240, 14336

    dG.wait(tok_prep_v)
    dG.wait(tok_prep_dS)
    dS.wait(tok_prep_v)
    V.wait(tok_prep_dS)
    TE.wait(tok_prep_dS)
    t_g = dG.tag(G.eng.dma_start(out=SCR[0:1, GBG:GBG + D], in_=gamma))
    t_b = dG.tag(G.eng.dma_start(out=SCR[0:1, GBB:GBB + D], in_=beta))
    V.waits([t_g, t_b])
    p.bcast_row(SCR[:, GBG:GBG + D], D)
    tok_gb = p.bcast_row(SCR[:, GBB:GBB + D], D)

    def ln_cols(src_ap, c_m, c_r):
        V.wait(tok_eps)
        for j in range(4):
            t1 = V.tag(V.eng.bn_stats(COL[:, C_STAT + 6 * j:C_STAT + 6 * j + 6],
                                      src_ap[:, 512 * j:512 * (j + 1)]))
        V.wait(t1)
        t2 = V.tag(V.eng.bn_aggr(COL[:, c_m:c_m + 2], COL[:, C_STAT:C_STAT + 24]))
        S.wait(t2)
        t3 = S.tag(S.eng.activation(COL[:, c_r:c_r + 1], COL[:, c_m + 1:c_m + 2],
                                    AF.Ln, bias=COL[:, C_EPS:C_EPS + 1], scale=1.0))
        S.wait(t3)
        return S.tag(S.eng.activation(COL[:, c_r:c_r + 1], COL[:, c_r:c_r + 1],
                                      AF.Exp, scale=-0.5))

    for tc in range(NTC):
        rs = RES[:, tc * D:(tc + 1) * D]
        t_ld = dG.tag(G.eng.dma_start(out=rs, in_=x_in[tc * 128:(tc + 1) * 128, :]))
        V.wait(t_ld)
        t_r1 = ln_cols(rs, C_M1, C_R1)
        xn1o = SCR[:, TMP1:TMP1 + D]
        V.wait(t_r1)
        t_x1o = V.tag(V.eng.tensor_scalar(xn1o, rs, COL[:, C_M1:C_M1 + 1],
                                          COL[:, C_R1:C_R1 + 1], OP.subtract, OP.mult))
        V.wait(t_x1o)
        t_r2 = ln_cols(xn1o, C_M2, C_R2)
        xnp = SCR[:, TMP2:TMP2 + D]
        V.wait(t_r2)
        t_xnp = V.tag(V.eng.tensor_scalar(xnp, xn1o, COL[:, C_M2:C_M2 + 1],
                                          COL[:, C_R2:C_R2 + 1], OP.subtract, OP.mult))
        V.waits([t_xnp, tok_gb])
        t_mg = V.tag(V.eng.tensor_tensor(xnp, xnp, SCR[:, GBG:GBG + D], OP.mult))
        V.wait(t_mg)
        t_xn = V.tag(V.eng.tensor_tensor(xnp, xnp, SCR[:, GBB:GBB + D], OP.add))
        t_xp = V.tag(V.eng.tensor_tensor(rs, rs, xn1o, OP.add))
        TE.waits([t_xn, ld_id])
        for half in range(2):
            bi = p.next_pb()
            TE.wait(p.pb_tok[bi])
            for j in range(8):
                kc = half * 8 + j
                inst = TE.eng.transpose(p.pb[bi][:, j * 128:(j + 1) * 128],
                                        xnp[:, kc * 128:(kc + 1) * 128], CID[:])
            t_tr = TE.tag(inst)
            V.wait(t_tr)
            for j in range(8):
                kc = half * 8 + j
                inst = V.eng.tensor_copy(
                    XNF[:, kc * T + tc * 128: kc * T + (tc + 1) * 128],
                    p.pb[bi][:, j * 128:(j + 1) * 128])
            p.pb_tok[bi] = V.tag(inst)

    tok_xnf = V.tok()
    tok_res0 = V.tok()

    if stage < 1:
        tok_res1 = tok_res0
        fin = ctx.enter_context(nc.semaphore("fin"))
        G.wait(tok_res1)
        n = 0
        for tc in range(NTC):
            G.eng.dma_start(out=out[tc * 128:(tc + 1) * 128, :],
                            in_=RES[:, tc * D:(tc + 1) * D]).then_inc(fin, 16)
            n += 16
        G.eng.wait_ge(fin, n)
        ctx.close()
        return nc

    t_bo = dG.tag(G.eng.dma_start(out=SCR[0:1, BOUT:BOUT + D], in_=b_out))
    V.wait(t_bo)
    p.bcast_row(SCR[:, BOUT:BOUT + D], D)
    t_bkv = dG.tag(G.eng.dma_start(out=SCR[0:1, BKV:BKV + 4096],
                                    in_=b_in[0:1, D:3 * D]))
    V.wait(t_bkv)
    tok_bias = p.bcast_row(SCR[:, BKV:BKV + 4096], 4096)

    KST, VST = P1S, P1S + 2048
    ROTC, ROTS = P1S + 4096, P1S + 5120
    t_z = V.tag(V.eng.memset(SC2[:, 0:D], 0.0))
    V.wait(t_z)

    for tc in range(NTC):
        for half, base in ((0, KST), (1, VST)):
            for mt in range(4):
                bi = p.next_pf()
                pk = p.pf[bi]
                col0 = (1 + half) * D + mt * 512
                t_mm = p.stream_mm(
                    pk[:], NKC,
                    lambda kc: XNF[:, kc * T + tc * 128:kc * T + (tc + 1) * 128],
                    lambda kc, c0=col0: w_in[kc * 128:(kc + 1) * 128, c0:c0 + 512],
                    ws_a, stream_is_lhsT=False,
                    pre_toks=[tok_xnf, p.pf_tok[bi]])
                V.waits([t_mm, tok_bias])
                p.pf_tok[bi] = V.tag(V.eng.tensor_tensor(
                    SCR[:, base + mt * 512:base + (mt + 1) * 512], pk[:],
                    SCR[:, BKV + half * D + mt * 512:BKV + half * D + (mt + 1) * 512],
                    OP.add))
        dS.wait(V.tok())
        t_c = dS.tag(SY.eng.dma_start(out=SCR[:, ROTC:ROTC + 1024],
                                      in_=cosd[tc * 128:(tc + 1) * 128, :]))
        t_s = dS.tag(SY.eng.dma_start(out=SCR[:, ROTS:ROTS + 1024],
                                      in_=sind[tc * 128:(tc + 1) * 128, :]))
        for base, c_ss in ((KST, C_SSK), (VST, C_SSV)):
            kv = SCR[:, base:base + D]
            rot = SCR[:, base:base + 1024]
            S.wait(V.tok())
            t_sq = S.tag(S.eng.activation(SCR[:, TMP1:TMP1 + D], kv, AF.Square,
                                          accum_out=COL[:, c_ss:c_ss + 1]))
            V.waits([t_c, t_s, t_sq])
            tmp = SCR[:, TMP1:TMP1 + 1024]
            cosb = SCR[:, ROTC:ROTC + 1024]
            sinb = SCR[:, ROTS:ROTS + 1024]
            ev_o = SCR[:, TMP2:TMP2 + 1024]
            V.tag(V.eng.tensor_tensor(tmp, rot, cosb, OP.mult))
            V.tag(V.eng.tensor_tensor(ev_o[:, 0:1024:2], rot[:, 1:1024:2],
                                      sinb[:, 0:1024:2], OP.mult))
            t_w = V.tag(V.eng.tensor_tensor(ev_o[:, 1:1024:2], rot[:, 0:1024:2],
                                            sinb[:, 1:1024:2], OP.mult))
            V.wait(t_w)
            V.tag(V.eng.tensor_tensor(tmp[:, 0:1024:2], tmp[:, 0:1024:2],
                                      ev_o[:, 0:1024:2], OP.subtract))
            t_a = V.tag(V.eng.tensor_tensor(tmp[:, 1:1024:2], tmp[:, 1:1024:2],
                                            ev_o[:, 1:1024:2], OP.add))
            V.wait(t_a)
            V.wait(V.tok())
            t_cp = V.tag(V.eng.tensor_copy(rot, tmp))
            V.wait(t_cp)
        S.wait(V.tok())
        S.tag(S.eng.activation(COL[:, C_RNK:C_RNK + 1], COL[:, C_SSK:C_SSK + 1], AF.Ln))
        t_lv = S.tag(S.eng.activation(COL[:, C_RNV:C_RNV + 1], COL[:, C_SSV:C_SSV + 1],
                                      AF.Ln))
        S.wait(t_lv)
        S.tag(S.eng.activation(COL[:, C_RNK:C_RNK + 1], COL[:, C_RNK:C_RNK + 1],
                               AF.Exp, scale=-0.5))
        t_ev = S.tag(S.eng.activation(COL[:, C_RNV:C_RNV + 1], COL[:, C_RNV:C_RNV + 1],
                                      AF.Exp, scale=-0.5))
        V.wait(t_ev)
        t_rn = V.tag(V.eng.tensor_tensor(COL[:, C_RN:C_RN + 1], COL[:, C_RNK:C_RNK + 1],
                                         COL[:, C_RNV:C_RNV + 1], OP.mult))
        V.wait(t_rn)
        t_p = V.tag(V.eng.tensor_tensor(SCR[:, TMP1:TMP1 + D], SCR[:, KST:KST + D],
                                        SCR[:, VST:VST + D], OP.mult))
        V.wait(t_p)
        V.tag(V.eng.scalar_tensor_tensor(SC2[:, 0:D], SCR[:, TMP1:TMP1 + D],
                                         COL[:, C_RN:C_RN + 1], SC2[:, 0:D],
                                         OP.mult, OP.add))
        if stage == 3 and tc == 0:
            fin = ctx.enter_context(nc.semaphore("fin"))
            G.wait(V.tok())
            n = 0
            G.eng.dma_start(out=out[0:128, :], in_=SCR[:, KST:KST + D]).then_inc(fin, 16)
            G.eng.dma_start(out=out[128:256, :], in_=SCR[:, VST:VST + D]).then_inc(fin, 16)
            G.eng.dma_start(out=out[256:384, 0:8], in_=COL[:, C_SSK:C_SSK + 8]).then_inc(fin, 16)
            n = 48
            G.eng.wait_ge(fin, n)
            ctx.close()
            return nc

    tok_apart = V.tok()
    for j in range(4):
        bi = p.next_pf()
        pk = p.pf[bi]
        TE.waits([tok_apart, p.pf_tok[bi], tok_eps])
        t_mm = TE.tag(TE.eng.matmul(pk[0:1, 0:512], COL[:, C_ONE:C_ONE + 1],
                                    SC2[:, j * 512:(j + 1) * 512],
                                    start=True, stop=True))
        V.wait(t_mm)
        p.pf_tok[bi] = V.tag(V.eng.tensor_copy(
            SC2[0:1, 2048 + j * 512:2048 + (j + 1) * 512], pk[0:1, 0:512]))

    t_asum = V.tok()
    SY.wait(t_asum)
    t_aw = dS.tag(SY.eng.dma_start(out=cc_a_in, in_=SC2[0:1, 2048:2048 + D]))
    G.wait(t_aw)
    t_cc = CC.tag(G.eng.collective_compute(
        "AllReduce", OP.add, replica_groups=[[0, 1], [2, 3], [4, 5], [6, 7]],
        ins=[cc_a_in.opt()], outs=[cc_a_out.opt()]))
    G.wait(t_cc)
    t_al = dG.tag(G.eng.dma_start(out=SC2[0:1, 0:D], in_=cc_a_out))
    V.wait(t_al)
    t_af = V.tag(V.eng.tensor_scalar(SC2[0:1, 0:D], SC2[0:1, 0:D], 0.5, 0.5,
                                     OP.mult, OP.add))
    SY.wait(t_af)
    t_aw2 = dS.tag(SY.eng.dma_start(out=cc_a_in, in_=SC2[0:1, 0:D]))
    dS.wait(t_aw2)
    t_ac = dS.tag(SY.eng.dma_start(
        out=COL[:, C_ACOL:C_ACOL + 16],
        in_=cc_a_in.rearrange("o (kc p) -> (o p) kc", p=128),
        allow_slow_non_contiguous=True))

    if stage == 2:
        fin = ctx.enter_context(nc.semaphore("fin"))
        SY.wait(t_ac)
        SY.wait((t_af[0], t_af[1]))
        n = 0
        SY.eng.dma_start(out=out[0:1, :], in_=SC2[0:1, 0:D]).then_inc(fin, 16)
        n += 16
        SY.eng.dma_start(out=out[4:132, 0:16], in_=COL[:, C_ACOL:C_ACOL + 16]).then_inc(fin, 16)
        n += 16
        SY.eng.dma_start(out=out[256:257, :], in_=SC2[0:1, 2048:2048 + D]).then_inc(fin, 16)
        n += 16
        SY.eng.wait_ge(fin, n)
        ctx.close()
        return nc

    AQF = P1S
    for tt_i in range(4):
        t0 = tt_i * 512
        for mc in range(NKC):
            bi = p.next_pf()
            pk = p.pf[bi]
            t_mm = p.stream_mm(
                pk[:, 0:512], NKC,
                lambda kc: XNF[:, kc * T + t0:kc * T + t0 + 512],
                lambda kc, m=mc: w_in[kc * 128:(kc + 1) * 128, m * 128:(m + 1) * 128],
                ws_q, stream_is_lhsT=True,
                pre_toks=[tok_xnf, p.pf_tok[bi]])
            V.waits([t_mm, t_ac, ld_bq])
            p.pf_tok[bi] = V.tag(V.eng.tensor_scalar(
                SCR[:, AQF + mc * 512:AQF + (mc + 1) * 512], pk[:, 0:512],
                COL[:, C_BQ + mc:C_BQ + mc + 1], COL[:, C_ACOL + mc:C_ACOL + mc + 1],
                OP.add, OP.mult))
        tok_aq = V.tok()
        for stc in range(4):
            tc = tt_i * 4 + stc
            for ot in range(4):
                bi = p.next_pf()
                pk = p.pf[bi]
                t_mm = p.stream_mm(
                    pk[:], NKC,
                    lambda kc, s=stc: SCR[:, AQF + kc * 512 + s * 128:
                                          AQF + kc * 512 + (s + 1) * 128],
                    lambda kc, o=ot: w_outw[kc * 128:(kc + 1) * 128,
                                            o * 512:(o + 1) * 512],
                    ws_b, stream_is_lhsT=False,
                    pre_toks=[tok_aq, p.pf_tok[bi]])
                rsl = RES[:, tc * D + ot * 512:tc * D + (ot + 1) * 512]
                V.waits([t_mm, tok_res0, V.tok()])
                t1 = V.tag(V.eng.tensor_tensor(SCR[:, TMP1:TMP1 + 512], pk[:],
                                               SCR[:, BOUT + ot * 512:BOUT + (ot + 1) * 512],
                                               OP.add))
                V.wait(t1)
                t2 = V.tag(V.eng.tensor_tensor(rsl, rsl, SCR[:, TMP1:TMP1 + 512],
                                               OP.add))
                p.pf_tok[bi] = t2

    tok_res1 = V.tok()

    if stage == 1:
        fin = ctx.enter_context(nc.semaphore("fin"))
        G.wait(tok_res1)
        n = 0
        for tc in range(NTC):
            G.eng.dma_start(out=out[tc * 128:(tc + 1) * 128, :],
                            in_=RES[:, tc * D:(tc + 1) * D]).then_inc(fin, 16)
            n += 16
        G.eng.wait_ge(fin, n)
        ctx.close()
        return nc

    RW1 = [cc_w_out[r:r + 1, O_W1:O_W1 + D * W1R].rearrange(
        "o (kc pp m) -> (o pp) kc m", pp=128, m=W1R) for r in range(8)]
    RW2 = [cc_w_out[r:r + 1, O_W2:O_W2 + I_FFN * W2R].rearrange(
        "o (ic pp m) -> (o pp) ic m", pp=128, m=W2R) for r in range(8)]

    def sf32(c0, w):
        return SCR[:, c0:c0 + 2 * w].bitcast(F32)

    def si32v(c0, w):
        return SCR[:, c0:c0 + 2 * w].bitcast(I32)


    V.wait(TE.tok())
    S.wait(TE.tok())

    for tc in range(NTC):
        rs_ = RES[:, tc * D:(tc + 1) * D]
        t_r = ln_cols(rs_, C_M1, C_R1)
        xn2f = sf32(0, 2048)
        V.wait(t_r)
        t_x = V.tag(V.eng.tensor_scalar(xn2f, rs_, COL[:, C_M1:C_M1 + 1],
                                        COL[:, C_R1:C_R1 + 1], OP.subtract, OP.mult))
        V.wait(t_x)
        quant_chain(xn2f, si32v(4096, 2048), sf32(8192, 2048), sf32(12288, 2048),
                    SCR[:, 16384:18432], 2048)
        if stage == 6 and tc == 0:
            fin = ctx.enter_context(nc.semaphore("fin"))
            G.wait(V.tok())
            G.eng.dma_start(out=out[0:128, :], in_=SCR[:, 16384:18432]).then_inc(fin, 16)
            G.eng.dma_start(out=out[128:256, :], in_=xn2f).then_inc(fin, 16)
            G.eng.dma_start(out=out[256:384, 0:80], in_=COL[:]).then_inc(fin, 16)
            G.eng.wait_ge(fin, 48)
            ctx.close()
            return nc
        for half in range(2):
            bi = p.next_pb()
            TE.waits([V.tok(), p.pb_tok[bi]])
            for j in range(8):
                kc = half * 8 + j
                inst = TE.eng.transpose(p.pb[bi][:, j * 128:(j + 1) * 128],
                                        SCR[:, 16384 + kc * 128:16384 + (kc + 1) * 128],
                                        CID[:])
            t_tr = TE.tag(inst)
            V.wait(t_tr)
            for j in range(8):
                kc = half * 8 + j
                inst = V.eng.tensor_copy(
                    XNF[:, kc * T + tc * 128:kc * T + (tc + 1) * 128],
                    p.pb[bi][:, j * 128:(j + 1) * 128])
            p.pb_tok[bi] = V.tag(inst)
    tok_xqf = V.tok()

    SY.wait(tok_xqf)
    for tc in range(NTC):
        t_sp = dS.tag(SY.eng.dma_start(out=res_spill[tc * 128:(tc + 1) * 128, :],
                                       in_=RES[:, tc * D:(tc + 1) * D]))
    tok_spill = t_sp

    if stage == 5:
        fin = ctx.enter_context(nc.semaphore("fin"))
        G.wait(tok_xqf)
        G.wait(CC.tok())
        n = 0
        G.eng.dma_start(out=out[0:128, :], in_=XNF[:, 0:T]).then_inc(fin, 16)
        n += 16
        t_bl = dG.tag(G.eng.dma_start(
            out=SCR[:, 0:1408],
            in_=cc_w_out[0:1, O_W1:O_W1 + 128 * W1R].rearrange(
                "o (pp m) -> (o pp) m", pp=128)))
        G.wait(t_bl)
        G.eng.dma_start(out=out[128:256, 0:1408], in_=SCR[:, 0:1408]).then_inc(fin, 16)
        n += 16
        t_b2 = dG.tag(G.eng.dma_start(
            out=SCR[:, 2048:2304],
            in_=cc_w_out[3:4, O_W2:O_W2 + 128 * W2R].rearrange(
                "o (pp m) -> (o pp) m", pp=128)))
        G.wait(t_b2)
        G.eng.dma_start(out=out[256:384, 0:256], in_=SCR[:, 2048:2304]).then_inc(fin, 16)
        n += 16
        G.eng.wait_ge(fin, n)
        ctx.close()
        return nc

    HT = RES

    def htf32(c0, w):
        return HT[:, c0:c0 + 2 * w].bitcast(F32)

    def hti32(c0, w):
        return HT[:, c0:c0 + 2 * w].bitcast(I32)

    w1slab = [WSlot(WB[:, 0:2048]), WSlot(WB[:, 2048:4096])]
    slab3 = lambda sl: sl.ap.rearrange("p (kc m) -> p kc m", m=512)

    def mm4_w1(bks, tt, m0, nk=16):
        for b in range(4):
            TE.wait(p.pf_tok[bks[b]])
        last = None
        for g in range(nk // 4):
            sl = w1slab[g % 2]
            dS.wait(sl.war)
            spans = []
            ma = m0
            while ma < m0 + 512:
                r = ma // W1R
                mb = min(m0 + 512, (r + 1) * W1R)
                spans.append((r, ma, mb))
                ma = mb
            toks = []
            for (r, ma, mb) in spans:
                toks.append(p.slot_dma(
                    dS, sl, slab3(sl)[:, :, ma - m0:mb - m0],
                    RW1[r][:, 4 * g:4 * g + 4, ma - r * W1R:mb - r * W1R]))
            TE.waits(toks)
            for j in range(4):
                kc = 4 * g + j
                for b in range(4):
                    tc = tt * 4 + b
                    inst = TE.eng.matmul(
                        p.pf[bks[b]][:],
                        XNF[:, kc * T + tc * 128:kc * T + (tc + 1) * 128],
                        slab3(sl)[:, j, :],
                        start=(kc == 0), stop=(kc == nk - 1))
                last = TE.tag(inst)
            sl.war = last
        return last

    sg_war = [None] * 4
    for tt in range(4):
        for mtp in range(11):
            bks = [p.next_pf() for _ in range(4)]
            t_mm = mm4_w1(bks, tt, mtp * 512)
            S.wait(t_mm)
            for b in range(4):
                S.waits([p.pf_tok[bks[b]], tok_spill, sg_war[b]])
                t_sg = S.tag(S.eng.activation(
                    HT[:, 22528 + b * 512:22528 + (b + 1) * 512],
                    p.pf[bks[b]][:], AF.Silu))
                p.pf_tok[bks[b]] = t_sg
            t_mm = mm4_w1(bks, tt, I_FFN + mtp * 512)
            V.wait(t_mm)
            for b in range(4):
                V.waits([p.pf_tok[bks[b]], tok_spill])
                t_h = V.tag(V.eng.tensor_tensor(
                    HT[:, b * I_FFN + mtp * 512:b * I_FFN + (mtp + 1) * 512],
                    HT[:, 22528 + b * 512:22528 + (b + 1) * 512],
                    p.pf[bks[b]][:], OP.mult))
                p.pf_tok[bks[b]] = t_h
                sg_war[b] = t_h
        V.wait(TE.tok())
        V.wait(V.tok())
        for b in range(4):
            hrow = HT[:, b * I_FFN:(b + 1) * I_FFN]
            t1 = V.tag(V.eng.tensor_reduce(COL[:, 76:77], hrow, AX.X,
                                           OP.max, apply_absolute_value=True))
            V.wait(t1)
            t2 = V.tag(V.eng.tensor_scalar(COL[:, 77:78], COL[:, 76:77],
                                           1.0 / 127.0, 1e-8, OP.mult, OP.max))
            V.wait(t2)
            t3 = V.tag(V.eng.reciprocal(COL[:, 78:79], COL[:, 77:78]))
            t4 = V.tag(V.eng.tensor_scalar(COL[:, 79:80], COL[:, 77:78],
                                           -256.0, None, OP.mult))
            V.waits([t3, t4])
            for pc in range(6):
                w0 = pc * 1024
                w = min(1024, I_FFN - w0)
                hpc = hrow[:, w0:w0 + w]
                qi = hti32(24576, w)
                rf = htf32(26624, w)
                qf = htf32(28672, w)
                dq = HT[:, 30720:30720 + w]
                t5 = V.tag(V.eng.tensor_scalar(qi, hpc, COL[:, 78:79], 256.0,
                                               OP.mult, OP.add))
                V.wait(t5)
                t6 = V.tag(V.eng.scalar_tensor_tensor(rf, hpc, COL[:, 78:79],
                                                      qi, OP.mult, OP.subtract))
                V.wait(t6)
                t7 = V.tag(V.eng.scalar_tensor_tensor(qf, rf, -255.5, qi,
                                                      OP.is_ge, OP.add))
                V.wait(t7)
                t8 = V.tag(V.eng.tensor_scalar(qf, qf, 129.0, 383.0,
                                               OP.max, OP.min))
                V.wait(t8)
                t9 = V.tag(V.eng.tensor_scalar(dq, qf, COL[:, 77:78],
                                               COL[:, 79:80], OP.mult, OP.add))
                nkt = w // 128
                bi = p.next_pb()
                TE.waits([t9, p.pb_tok[bi]])
                for j in range(nkt):
                    inst = TE.eng.transpose(p.pb[bi][:, j * 128:(j + 1) * 128],
                                            dq[:, j * 128:(j + 1) * 128], CID[:])
                t_tr = TE.tag(inst)
                V.wait(t_tr)
                for j in range(nkt):
                    ic = pc * 8 + j
                    inst = V.eng.tensor_copy(
                        SCR[:, ic * 512 + b * 128:ic * 512 + (b + 1) * 128],
                        p.pb[bi][:, j * 128:(j + 1) * 128])
                p.pb_tok[bi] = V.tag(inst)
        tok_hq = V.tok()

        SY.wait(V.tok())
        SY.wait(S.tok())
        x1toks = []
        for b in range(4):
            tc = tt * 4 + b
            x1toks.append(dS.tag(SY.eng.dma_start(
                out=HT[:, 22528 + b * 2048:22528 + (b + 1) * 2048],
                in_=res_spill[tc * 128:(tc + 1) * 128, :])))

        w2slab = w1slab
        for ot in range(4):
            o0 = ot * 512
            bks = [p.next_pf() for _ in range(4)]
            last = None
            for g in range(11):
                sl = w2slab[g % 2]
                dS.wait(sl.war)
                toks = []
                for half in range(2):
                    r = (o0 + half * 256) // W2R
                    toks.append(p.slot_dma(
                        dS, sl, slab3(sl)[:, :, half * 256:(half + 1) * 256],
                        RW2[r][:, 4 * g:4 * g + 4, :]))
                TE.waits(toks)
                TE.wait(tok_hq)
                for b in range(4):
                    TE.wait(p.pf_tok[bks[b]])
                for j in range(4):
                    ic = 4 * g + j
                    for b in range(4):
                        inst = TE.eng.matmul(
                            p.pf[bks[b]][:],
                            SCR[:, ic * 512 + b * 128:ic * 512 + (b + 1) * 128],
                            slab3(sl)[:, j, :],
                            start=(ic == 0), stop=(ic == 43))
                    last = TE.tag(inst)
                sl.war = last
            for b in range(4):
                V.waits([last, x1toks[b]])
                xb_ = HT[:, 22528 + b * 2048 + o0:22528 + b * 2048 + o0 + 512]
                t_e = V.tag(V.eng.tensor_tensor(xb_, xb_, p.pf[bks[b]][:], OP.add))
                p.pf_tok[bks[b]] = t_e
        SY.wait(V.tok())
        for b in range(4):
            tc = tt * 4 + b
            t_x2 = dS.tag(SY.eng.dma_start(
                out=x2_d[tc * 128:(tc + 1) * 128, :],
                in_=HT[:, 22528 + b * 2048:22528 + (b + 1) * 2048]))
    tok_x2 = t_x2


    if stage == 4:
        fin = ctx.enter_context(nc.semaphore("fin"))
        G.wait(tok_x2)
        n = 0
        for tc in range(NTC):
            G.eng.dma_start(out=HT[:, 0:D],
                            in_=x2_d[tc * 128:(tc + 1) * 128, :]).then_inc(fin, 16)
            n += 16
            G.eng.wait_ge(fin, n)
            G.eng.dma_start(out=out[tc * 128:(tc + 1) * 128, :],
                            in_=HT[:, 0:D]).then_inc(fin, 16)
            n += 16
            G.eng.wait_ge(fin, n)
        ctx.close()
        return nc

    RIP = [cc_w_out[r:r + 1, O_IP:O_IP + IPC * MPROJ].rearrange(
        "o (kh pp m) -> (o pp) kh m", pp=128, m=MPROJ) for r in range(8)]
    ROP = [cc_w_out[r:r + 1, O_OP:O_OP + OPC * D].rearrange(
        "o (kh pp m) -> (o pp) kh m", pp=128, m=D) for r in range(8)]
    RW8 = [cc_w_out[r:r + 1, O_W8:O_W8 + W8C * D].rearrange(
        "o (kh pp m) -> (o pp) kh m", pp=128, m=D) for r in range(8)]

    C_EM, C_HS = 68, 69
    t_mk = dS.tag(SY.eng.dma_start(out=COL[0:1, C_EM:C_EM + 2], in_=masks))
    V.wait(t_mk)
    V.eng.tensor_copy(COL[32:33, C_EM:C_EM + 2], COL[0:1, C_EM:C_EM + 2])
    V.eng.tensor_copy(COL[64:65, C_EM:C_EM + 2], COL[0:1, C_EM:C_EM + 2])
    t_c3 = V.tag(V.eng.tensor_copy(COL[96:97, C_EM:C_EM + 2], COL[0:1, C_EM:C_EM + 2]))
    V.wait(t_c3)
    tok_mk = V.tag(V.eng.stream_shuffle(COL[:, C_EM:C_EM + 2],
                                        COL[:, C_EM:C_EM + 2], [0] * 32))

    V.wait(TE.tok())
    for tc in range(NTC):
        t_ld = dS.tag(SY.eng.dma_start(out=SCR[:, 0:2048],
                                       in_=x2_d[tc * 128:(tc + 1) * 128, :]))
        V.wait(t_ld)
        t_r = ln_cols(SCR[:, 0:2048], C_M1, C_R1)
        V.wait(t_r)
        t_x = V.tag(V.eng.tensor_scalar(SCR[:, 2048:4096], SCR[:, 0:2048],
                                        COL[:, C_M1:C_M1 + 1], COL[:, C_R1:C_R1 + 1],
                                        OP.subtract, OP.mult))
        for half in range(2):
            bi = p.next_pb()
            TE.waits([t_x, p.pb_tok[bi]])
            for j in range(8):
                kc = half * 8 + j
                inst = TE.eng.transpose(p.pb[bi][:, j * 128:(j + 1) * 128],
                                        SCR[:, 2048 + kc * 128:2048 + (kc + 1) * 128],
                                        CID[:])
            t_tr = TE.tag(inst)
            V.wait(t_tr)
            for j in range(8):
                kc = half * 8 + j
                inst = V.eng.tensor_copy(
                    XNF[:, kc * T + tc * 128:kc * T + (tc + 1) * 128],
                    p.pb[bi][:, j * 128:(j + 1) * 128])
            p.pb_tok[bi] = V.tag(inst)
        dS.wait(V.tok())
    tok_xn3 = V.tok()

    ipslab = [WSlot(WB[:, 0:512]), WSlot(WB[:, 512:1024])]

    def ipt_slab(g, m0, w):
        sl = ipslab[g % 2]
        dS.wait(sl.war)
        toks = []
        for rr in range(2):
            kc0 = 4 * g + 2 * rr
            r = kc0 // 2
            toks.append(p.slot_dma(
                dS, sl,
                sl.ap[:, 0:4 * w].rearrange("p (kc m) -> p kc m", m=w)[
                    :, 2 * rr:2 * rr + 2, :],
                RIP[r][:, :, m0:m0 + w]))
        return sl, toks

    HSTG = 1024
    for mc in range(35):
        rows = 128 if mc < 34 else 64
        m0 = DIN + mc * 128
        bks = [p.next_pf() for _ in range(4)]
        for b in range(4):
            TE.wait(p.pf_tok[bks[b]])
        last = None
        for g in range(4):
            sl, toks = ipt_slab(g, m0, rows)
            TE.waits(toks + [tok_xn3])
            for j in range(4):
                kc = 4 * g + j
                for b in range(4):
                    inst = TE.eng.matmul(
                        p.pf[bks[b]][0:rows, :],
                        sl.ap[:, 0:4 * rows].rearrange("p (kc m) -> p kc m",
                                                       m=rows)[:, j, :],
                        XNF[:, kc * T + b * 512:kc * T + (b + 1) * 512],
                        start=(kc == 0), stop=(kc == 15))
                last = TE.tag(inst)
            sl.war = last
        if mc < 34:
            for b in range(4):
                V.waits([last, p.pf_tok[bks[b]]])
                t_e = V.tag(V.eng.tensor_copy(SCR[:, 0:512], p.pf[bks[b]][:]))
                if b == 3:
                    V.wait(t_e)
                    t_h = V.tag(V.eng.tensor_copy(
                        SCR[:, HSTG + mc * 3:HSTG + (mc + 1) * 3],
                        SCR[:, 509:512]))
                    t_e = t_h
                SY.wait(t_e)
                t_st = dS.tag(SY.eng.dma_start(
                    out=xbcraw_d[mc * 128:(mc + 1) * 128, b * 512:(b + 1) * 512],
                    in_=SCR[:, 0:512]))
                dS.wait(V.tok())
                p.pf_tok[bks[b]] = t_e
                V.wait(t_st)
        else:
            for b in range(4):
                V.waits([last, p.pf_tok[bks[b]]])
                t_e = V.tag(V.eng.tensor_copy(
                    SC2[0:64, b * 512:(b + 1) * 512], p.pf[bks[b]][0:64, :]))
                p.pf_tok[bks[b]] = t_e
    tok_xbc = dS.tok()

    V.waits([tok_mk])
    t_hm = V.tag(V.eng.tensor_scalar(SC2[0:128, 2048:2150], SCR[:, HSTG:HSTG + 102],
                                     COL[:, C_EM:C_EM + 1], None, OP.mult))
    SY.wait(t_hm)
    t_hw = dS.tag(SY.eng.dma_start(out=cc_h_in, in_=SC2[:, 2048:2150]))
    G.wait(t_hw)
    t_hc = CC.tag(G.eng.collective_compute(
        "AllReduce", OP.add, replica_groups=[[0, 1], [2, 3], [4, 5], [6, 7]],
        ins=[cc_h_in.opt()], outs=[cc_h_out.opt()]))
    G.wait(t_hc)
    t_hl = dG.tag(G.eng.dma_start(out=SC2[:, 2048:2150], in_=cc_h_out))
    V.wait(t_hl)
    t_hs = V.tag(V.eng.tensor_scalar(SCR[:, HSTG:HSTG + 102], SC2[:, 2048:2150],
                                     COL[:, C_HS:C_HS + 1], None, OP.mult))

    if stage == 61:
        fin = ctx.enter_context(nc.semaphore("fin"))
        G.waits([tok_xbc, V.tok()])
        n = 0
        for i in range(16):
            G.eng.dma_start(out=SCR[:, 0:2048],
                            in_=xbcraw_d[i * 128:(i + 1) * 128, 0:2048]).then_inc(fin, 16)
            n += 16
            G.eng.wait_ge(fin, n)
            G.eng.dma_start(out=out[i * 128:(i + 1) * 128, 0:2048],
                            in_=SCR[:, 0:2048]).then_inc(fin, 16)
            n += 16
            G.eng.wait_ge(fin, n)
        G.eng.dma_start(out=out[0:64, 0:2048],
                        in_=SC2[0:64, 0:2048]).then_inc(fin, 16)
        n += 16
        G.eng.wait_ge(fin, n)
        ctx.close()
        return nc

    CPAD = 1280
    CST = 14336
    BCM, CCM = 16384, 18432
    for cc34 in range(34):
        dS.wait(V.tok())
        t_ci = dS.tag(SY.eng.dma_start(
            out=SCR[:, CPAD + 3:CPAD + 3 + 2048],
            in_=xbcraw_d[cc34 * 128:(cc34 + 1) * 128, :]))
        dA.wait(S.tok())
        dA.wait(V.tok())
        t_cwc = dA.tag(S.eng.dma_start(out=COL[:, 12:16],
                                       in_=conv_w[cc34 * 128:(cc34 + 1) * 128, :]))
        t_cbc = dA.tag(S.eng.dma_start(
            out=COL[:, 16:17],
            in_=conv_b[0:1, cc34 * 128:(cc34 + 1) * 128].rearrange("o p -> p o"),
            allow_slow_non_contiguous=True))
        V.waits([t_ci, t_cwc, t_hs])
        V.wait(S.tok())
        t_hc2 = V.tag(V.eng.tensor_copy(SCR[:, CPAD:CPAD + 3],
                                        SCR[:, HSTG + cc34 * 3:HSTG + (cc34 + 1) * 3]))
        V.wait(t_hc2)
        ACC = SC2[:, 2560:4608]
        t_a0 = V.tag(V.eng.tensor_scalar(ACC, SCR[:, CPAD:CPAD + 2048],
                                         COL[:, 12:13], None, OP.mult))
        V.wait(t_a0)
        for i in (1, 2, 3):
            t_ai = V.tag(V.eng.scalar_tensor_tensor(
                ACC, SCR[:, CPAD + i:CPAD + i + 2048], COL[:, 12 + i:13 + i], ACC,
                OP.mult, OP.add))
            V.wait(t_ai)
        S.waits([t_ai, t_cbc])
        if cc34 < 32:
            t_sl = S.tag(S.eng.activation(SCR[:, CST:CST + 2048], ACC, AF.Silu,
                                          bias=COL[:, 16:17]))
            SY.wait(t_sl)
            t_so = dS.tag(SY.eng.dma_start(
                out=xconv_d[cc34 * 128:(cc34 + 1) * 128, :], in_=SCR[:, CST:CST + 2048]))
            S.wait(t_so)
        elif cc34 == 32:
            t_sl = S.tag(S.eng.activation(SCR[:, BCM:BCM + 2048], ACC, AF.Silu,
                                          bias=COL[:, 16:17]))
        else:
            t_sl = S.tag(S.eng.activation(SCR[:, CCM:CCM + 2048], ACC, AF.Silu,
                                          bias=COL[:, 16:17]))
    tok_conv = S.tok()

    if stage == 615:
        fin = ctx.enter_context(nc.semaphore("fin"))
        G.waits([V.tok(), S.tok(), TE.tok(), dS.tok(), dA.tok(), dG.tok()])
        G.eng.dma_start(out=out[0:128, 0:128], in_=CID[:]).then_inc(fin, 16)
        G.eng.wait_ge(fin, 16)
        ctx.close()
        return nc

    t_al2 = dA.tag(S.eng.dma_start(out=COL[0:64, 5:6],
                                   in_=a_log.rearrange("o p -> p o"),
                                   allow_slow_non_contiguous=True))
    t_db = dA.tag(S.eng.dma_start(out=COL[0:64, 6:7],
                                  in_=dt_bias.rearrange("o p -> p o"),
                                  allow_slow_non_contiguous=True))
    t_dv = dA.tag(S.eng.dma_start(out=COL[0:64, 7:8],
                                  in_=d_vec.rearrange("o p -> p o"),
                                  allow_slow_non_contiguous=True))
    S.waits([t_al2, t_db, t_dv, V.tok()])
    t_am = S.tag(S.eng.activation(COL[0:64, 5:6], COL[0:64, 5:6], AF.Exp))
    V.wait(t_am)
    t_an = V.tag(V.eng.tensor_scalar(COL[0:64, 5:6], COL[0:64, 5:6], -1.0, None,
                                     OP.mult))
    DTF = 2560
    dtr = SC2[0:64, 0:2048]
    S.wait(V.tok())
    t_s1 = S.tag(S.eng.activation(SC2[0:64, DTF:DTF + 2048], dtr, AF.Abs,
                                  bias=COL[0:64, 6:7]))
    S.wait(t_s1)
    t_s2 = S.tag(S.eng.activation(SC2[0:64, DTF:DTF + 2048],
                                  SC2[0:64, DTF:DTF + 2048], AF.Exp, scale=-1.0))
    S.wait(t_s2)
    t_s3 = S.tag(S.eng.activation(SC2[0:64, DTF:DTF + 2048],
                                  SC2[0:64, DTF:DTF + 2048], AF.Ln,
                                  bias=COL[0:64, C_ONE:C_ONE + 1]))
    t_s4 = S.tag(S.eng.activation(SC2[0:64, 0:2048], dtr, AF.Relu,
                                  bias=COL[0:64, 6:7]))
    V.waits([t_s3, t_s4])
    t_dt = V.tag(V.eng.tensor_tensor(SC2[0:64, DTF:DTF + 2048],
                                     SC2[0:64, DTF:DTF + 2048],
                                     SC2[0:64, 0:2048], OP.add))
    ADTC = 4096
    V.waits([t_dt, t_an])
    t_adt = V.tag(V.eng.tensor_scalar(SCR[0:64, ADTC:ADTC + 2048],
                                      SC2[0:64, DTF:DTF + 2048],
                                      COL[0:64, 5:6], None, OP.mult))
    V.wait(t_adt)
    for c in range(NCH):
        t_sc = V.tag(V.eng.tensor_tensor_scan(
            SC2[0:64, c * 256:(c + 1) * 256],
            SCR[0:64, ADTC + c * 256:ADTC + (c + 1) * 256],
            SCR[0:64, ADTC + c * 256:ADTC + (c + 1) * 256],
            0.0, OP.add, OP.bypass))
    tok_acscm = V.tok()

    if stage == 616:
        fin = ctx.enter_context(nc.semaphore("fin"))
        G.waits([V.tok(), S.tok(), TE.tok(), dS.tok(), dA.tok(), dG.tok()])
        G.eng.dma_start(out=out[0:128, 0:128], in_=CID[:]).then_inc(fin, 16)
        G.eng.wait_ge(fin, 16)
        ctx.close()
        return nc

    ADTT = 6144
    DTTM = 0
    DTB = 8192
    S.wait(V.tok())
    t_dtb = S.tag(S.eng.activation(SCR[0:64, DTB:DTB + 2048],
                                   SC2[0:64, DTF:DTF + 2048], AF.Identity))
    V.wait(t_dtb)
    for tc in range(NTC):
        bi = p.next_pb()
        TE.waits([V.tok(), t_dtb, p.pb_tok[bi]])
        i1 = TE.eng.transpose(p.pb[bi][:, 0:64],
                              SCR[0:64, ADTC + tc * 128:ADTC + (tc + 1) * 128],
                              CID[0:64, 0:64])
        i2 = TE.eng.transpose(p.pb[bi][:, 64:128],
                              SCR[0:64, DTB + tc * 128:DTB + (tc + 1) * 128],
                              CID[0:64, 0:64])
        t_tr = TE.tag(i2)
        V.wait(t_tr)
        V.eng.tensor_copy(SCR[:, ADTT + tc * 64:ADTT + (tc + 1) * 64],
                          p.pb[bi][:, 0:64])
        inst = V.eng.tensor_copy(SCR[:, DTTM + tc * 64:DTTM + (tc + 1) * 64],
                                 p.pb[bi][:, 64:128])
        p.pb_tok[bi] = V.tag(inst)
    tok_dtt = V.tok()

    if stage == 617:
        fin = ctx.enter_context(nc.semaphore("fin"))
        G.waits([V.tok(), S.tok(), TE.tok(), dS.tok(), dA.tok(), dG.tok()])
        G.eng.dma_start(out=out[0:128, 0:128], in_=CID[:]).then_inc(fin, 16)
        G.eng.wait_ge(fin, 16)
        ctx.close()
        return nc

    if stage == 621:
        fin = ctx.enter_context(nc.semaphore("fin"))
        G.waits([V.tok(), S.tok(), TE.tok(), dS.tok(), dG.tok(), dA.tok()])
        G.eng.dma_start(out=out[0:128, 0:1024],
                        in_=SCR[:, 0:1024]).then_inc(fin, 16)
        G.eng.wait_ge(fin, 16)
        ctx.close()
        return nc

    ACSTM = 2048
    EXPA, DECAYC = 20480, 3072
    TRI_I = CTRI[:, 0:128]
    TRI_S = CTRI[:, 128:256]
    TRI_O = CTRI[:, 256:384]
    V.wait(tok_dtt)
    t_z2 = V.tag(V.eng.memset(DEX[:, 576:832], 0.0))
    for c in range(NCH):
        adt0 = SCR[:, ADTT + (2 * c) * 64:ADTT + (2 * c + 1) * 64]
        adt1 = SCR[:, ADTT + (2 * c + 1) * 64:ADTT + (2 * c + 2) * 64]
        for lh in range(2):
            tcb = 2 * c + lh
            bi = p.next_pf()
            pk = p.pf[bi]
            TE.waits([tok_dtt, p.pf_tok[bi]])
            if lh == 0:
                i1 = TE.eng.matmul(pk[:, 0:64], TRI_I, adt0, start=True, stop=True)
            else:
                TE.eng.matmul(pk[:, 0:64], TRI_O, adt0, start=True, stop=False)
                i1 = TE.eng.matmul(pk[:, 0:64], TRI_I, adt1, start=False, stop=True)
            if lh == 0:
                TE.eng.matmul(pk[:, 64:128], TRI_S, adt0, start=True, stop=False)
                i2 = TE.eng.matmul(pk[:, 64:128], TRI_O, adt1, start=False, stop=True)
            else:
                i2 = TE.eng.matmul(pk[:, 64:128], TRI_S, adt1, start=True, stop=True)
            t_mm = TE.tag(i2)
            V.wait(t_mm)
            t_ac = V.tag(V.eng.tensor_copy(
                SC2[:, ACSTM + tcb * 64:ACSTM + (tcb + 1) * 64], pk[:, 0:64]))
            S.waits([t_mm, t_ac])
            t_e1 = S.tag(S.eng.activation(
                SCR[:, EXPA + tcb * 64:EXPA + (tcb + 1) * 64], pk[:, 0:64], AF.Exp))
            t_e2 = S.tag(S.eng.activation(
                SCR[:, DECAYC + tcb * 64:DECAYC + (tcb + 1) * 64], pk[:, 64:128],
                AF.Exp))
            p.pf_tok[bi] = t_e2
            V.waits([t_e1, t_e2])
        if stage == 6171 and c == 0:
            fin = ctx.enter_context(nc.semaphore("fin"))
            G.waits([V.tok(), S.tok(), TE.tok(), dS.tok(), dA.tok(), dG.tok()])
            G.eng.dma_start(out=out[0:128, 0:128], in_=CID[:]).then_inc(fin, 16)
            G.eng.wait_ge(fin, 16)
            ctx.close()
            return nc
        bi = p.next_pf()
        pk = p.pf[bi]
        TE.wait(p.pf_tok[bi])
        TE.eng.matmul(pk[0:1, 0:64], TRI_O[:, 0:1], adt0, start=True, stop=False)
        i3 = TE.eng.matmul(pk[0:1, 0:64], TRI_O[:, 0:1], adt1, start=False, stop=True)
        t_mm = TE.tag(i3)
        S.wait(t_mm)
        t_db = S.tag(S.eng.activation(DEX[0:1, c * 64:(c + 1) * 64], pk[0:1, 0:64],
                                      AF.Exp))
        p.pf_tok[bi] = t_db
        V.wait(t_db)
        p.bcast_row(DEX[:, c * 64:(c + 1) * 64], 64)
        if stage == 6172 and c == 1:
            fin = ctx.enter_context(nc.semaphore("fin"))
            G.waits([V.tok(), S.tok(), TE.tok(), dS.tok(), dA.tok(), dG.tok()])
            G.eng.dma_start(out=out[0:128, 0:640], in_=DEX[:, 0:640]).then_inc(fin, 16)
            G.eng.dma_start(out=out[128:256, 0:1024],
                            in_=SC2[:, ACSTM:ACSTM + 1024]).then_inc(fin, 16)
            G.eng.wait_ge(fin, 32)
            ctx.close()
            return nc

    if stage == 618:
        fin = ctx.enter_context(nc.semaphore("fin"))
        G.waits([V.tok(), S.tok(), TE.tok(), dS.tok(), dG.tok(), dA.tok()])
        G.eng.dma_start(out=out[0:128, 0:640], in_=DEX[:, 0:640]).then_inc(fin, 16)
        G.eng.dma_start(out=out[128:256, 0:1024],
                        in_=SC2[:, ACSTM:ACSTM + 1024]).then_inc(fin, 16)
        G.eng.dma_start(out=out[256:384, 0:1024],
                        in_=SCR[:, DECAYC:DECAYC + 1024]).then_inc(fin, 16)
        G.eng.dma_start(out=out[384:512, 0:1024],
                        in_=SCR[:, EXPA:EXPA + 1024]).then_inc(fin, 16)
        G.eng.dma_start(out=out[512:640, 0:2048],
                        in_=SCR[:, ADTT:ADTT + 2048]).then_inc(fin, 16)
        G.eng.wait_ge(fin, 80)
        ctx.close()
        return nc

    if stage == 622:
        fin = ctx.enter_context(nc.semaphore("fin"))
        G.waits([V.tok(), S.tok(), TE.tok(), dS.tok(), dG.tok(), dA.tok()])
        G.eng.dma_start(out=out[0:128, 0:1024],
                        in_=SCR[:, 0:1024]).then_inc(fin, 16)
        G.eng.wait_ge(fin, 16)
        ctx.close()
        return nc

    t_dr = dG.tag(G.eng.dma_start(out=DEX[0:1, 512:576], in_=d_vec))
    V.wait(t_dr)
    tok_db = p.bcast_row(DEX[:, 512:576], 64)

    if stage == 62:
        fin = ctx.enter_context(nc.semaphore("fin"))
        G.waits([V.tok(), S.tok(), dS.tok(), dG.tok(), dA.tok(), TE.tok()])
        n = 0
        for i in range(8):
            G.eng.dma_start(out=SCR[:, 0:2048],
                            in_=xconv_d[i * 128:(i + 1) * 128, :]).then_inc(fin, 16)
            n += 16
            G.eng.wait_ge(fin, n)
            G.eng.dma_start(out=out[i * 128:(i + 1) * 128, :],
                            in_=SCR[:, 0:2048]).then_inc(fin, 16)
            n += 16
            G.eng.wait_ge(fin, n)
        G.eng.dma_start(out=out[10 * 128:11 * 128, :],
                        in_=SCR[:, BCM:BCM + 2048]).then_inc(fin, 16)
        G.eng.dma_start(out=out[11 * 128:12 * 128, :],
                        in_=SCR[:, CCM:CCM + 2048]).then_inc(fin, 16)
        G.eng.dma_start(out=out[12 * 128:13 * 128, 0:1024],
                        in_=SC2[:, ACSTM:ACSTM + 1024]).then_inc(fin, 16)
        G.eng.dma_start(out=out[13 * 128:14 * 128, 0:1024],
                        in_=SCR[:, EXPA:EXPA + 1024]).then_inc(fin, 16)
        G.eng.dma_start(out=out[14 * 128:15 * 128, 0:1024],
                        in_=SCR[:, DECAYC:DECAYC + 1024]).then_inc(fin, 16)
        G.eng.dma_start(out=out[15 * 128:16 * 128, 0:640],
                        in_=DEX[:, 0:640]).then_inc(fin, 16)
        G.eng.dma_start(out=out[8 * 128:8 * 128 + 64, 0:2048],
                        in_=SC2[0:64, 0:2048]).then_inc(fin, 16)
        n += 7 * 16
        G.eng.wait_ge(fin, n)
        ctx.close()
        return nc

    XCMA, XTMA, XDTA, XDDA = 0, 8192, 16384, 24576
    BTM = 1024
    CBMA = 1280
    ACSB = 4096
    MPF = 8192
    MB0, MB1 = 12288, 14336
    MXA = 20480

    V.wait(tok_conv)
    V.wait(tok_acscm)
    tok_ydprev = None
    for c in range(NCH):
        V.wait(TE.tok())
        V.wait(tok_ydprev)
        dS.wait(dS.tok())
        dS.waits([V.tok(), TE.tok()])
        t_x = dS.tag(SY.eng.dma_start(
            out=RES[:, XCMA:XCMA + 8192].rearrange("p (xc j) -> p xc j", j=256),
            in_=xconv_d[:, c * 256:(c + 1) * 256].rearrange(
                "(xc pp) j -> pp xc j", pp=128)))
        for xc in range(32):
            bi = p.next_pb()
            TE.waits([t_x, p.pb_tok[bi], ld_id])
            i1 = TE.eng.transpose(p.pb[bi][:, 0:128],
                                  RES[:, XCMA + xc * 256:XCMA + xc * 256 + 128],
                                  CID[:])
            i2 = TE.eng.transpose(p.pb[bi][:, 128:256],
                                  RES[:, XCMA + xc * 256 + 128:XCMA + (xc + 1) * 256],
                                  CID[:])
            t_tr = TE.tag(i2)
            V.wait(t_tr)
            V.eng.tensor_copy(RES[:, XTMA + xc * 128:XTMA + (xc + 1) * 128],
                              p.pb[bi][:, 0:128])
            inst = V.eng.tensor_copy(
                RES[:, XTMA + 4096 + xc * 128:XTMA + 4096 + (xc + 1) * 128],
                p.pb[bi][:, 128:256])
            p.pb_tok[bi] = V.tag(inst)
        V.wait(V.tok())
        for lh in range(2):
            tcb = 2 * c + lh
            dtex = SCR[:, DTTM + tcb * 64:DTTM + (tcb + 1) * 64].unsqueeze(
                2).broadcast_to([128, 64, 64])
            dcex = SCR[:, DECAYC + tcb * 64:DECAYC + (tcb + 1) * 64].unsqueeze(
                2).broadcast_to([128, 64, 64])
            t_1 = V.tag(V.eng.tensor_tensor(
                RES[:, XDTA + lh * 4096:XDTA + (lh + 1) * 4096],
                RES[:, XTMA + lh * 4096:XTMA + (lh + 1) * 4096], dtex, OP.mult))
            V.wait(t_1)
            t_2 = V.tag(V.eng.tensor_tensor(
                RES[:, XDDA + lh * 4096:XDDA + (lh + 1) * 4096],
                RES[:, XDTA + lh * 4096:XDTA + (lh + 1) * 4096], dcex, OP.mult))
            V.wait(t_2)
        bi = p.next_pb()
        TE.waits([V.tok(), p.pb_tok[bi], tok_conv])
        i1 = TE.eng.transpose(p.pb[bi][:, 0:128],
                              SCR[:, BCM + c * 256:BCM + c * 256 + 128], CID[:])
        i2 = TE.eng.transpose(p.pb[bi][:, 128:256],
                              SCR[:, BCM + c * 256 + 128:BCM + (c + 1) * 256], CID[:])
        t_tr = TE.tag(i2)
        V.wait(t_tr)
        inst = V.eng.tensor_copy(SCR[:, BTM:BTM + 256], p.pb[bi][:, 0:256])
        p.pb_tok[bi] = V.tag(inst)
        for sh in range(2):
            bi = p.next_pf()
            pk = p.pf[bi]
            TE.waits([tok_conv, p.pf_tok[bi]])
            t_mm = TE.tag(TE.eng.matmul(
                pk[:, 0:256], SCR[:, BCM + c * 256 + sh * 128:BCM + c * 256 + (sh + 1) * 128],
                SCR[:, CCM + c * 256:CCM + (c + 1) * 256], start=True, stop=True))
            V.waits([t_mm, ld_sm])
            p.pf_tok[bi] = V.tag(V.eng.tensor_tensor(
                SCR[:, CBMA + sh * 256:CBMA + (sh + 1) * 256], pk[:, 0:256],
                CSM[:, sh * 256:(sh + 1) * 256], OP.mult))
        if stage == 635 and c == 1:
            fin = ctx.enter_context(nc.semaphore("fin"))
            G.waits([V.tok(), S.tok(), TE.tok(), dS.tok(), dG.tok()])
            G.eng.dma_start(out=out[0:128, 0:2048],
                            in_=SCR[:, BCM:BCM + 2048]).then_inc(fin, 16)
            G.eng.dma_start(out=out[128:256, 0:2048],
                            in_=SCR[:, CCM:CCM + 2048]).then_inc(fin, 16)
            G.eng.dma_start(out=out[256:384, 0:512],
                            in_=SCR[:, CBMA:CBMA + 512]).then_inc(fin, 16)
            G.eng.dma_start(out=out[384:512, 0:256],
                            in_=SCR[:, BTM:BTM + 256]).then_inc(fin, 16)
            G.eng.wait_ge(fin, 64)
            ctx.close()
            return nc
        for hg in range(8):
            dG.wait(V.tok())
            t_fl = dG.tag(G.eng.dma_start(
                out=SCR[0:1, ACSB:ACSB + 4096].bitcast(F32),
                in_=SC2[hg * 8:hg * 8 + 8, c * 256:(c + 1) * 256]))
            V.wait(t_fl)
            acsb_f = SCR[:, ACSB:ACSB + 4096].bitcast(F32)
            V.eng.tensor_copy(acsb_f[32:33, :], acsb_f[0:1, :])
            V.eng.tensor_copy(acsb_f[64:65, :], acsb_f[0:1, :])
            t_c3 = V.tag(V.eng.tensor_copy(acsb_f[96:97, :], acsb_f[0:1, :]))
            V.wait(t_c3)
            t_bc = V.tag(V.eng.stream_shuffle(acsb_f[:, :], acsb_f[:, :], [0] * 32))
            V.wait(t_bc)
            for sh in range(2):
                tcb = 2 * c + sh
                V.wait(S.tok())
                for h in range(8):
                    t_st2 = V.tag(V.eng.scalar_tensor_tensor(
                        SCR[:, MPF:MPF + 4096].bitcast(F32)[:, h * 256:(h + 1) * 256],
                        acsb_f[:, h * 256:(h + 1) * 256],
                        SC2[:, ACSTM + tcb * 64 + hg * 8 + h:ACSTM + tcb * 64 + hg * 8 + h + 1],
                        DEX[:, 576:832], OP.subtract, OP.min))
                S.waits([t_st2, TE.tok()])
                mb = SCR[:, (MB0 if sh == 0 else MB1):(MB0 if sh == 0 else MB1) + 2048]
                t_ex = S.tag(S.eng.activation(mb,
                                              SCR[:, MPF:MPF + 4096].bitcast(F32),
                                              AF.Exp))
                V.wait(t_ex)
                cbex = SCR[:, CBMA + sh * 256:CBMA + (sh + 1) * 256].unsqueeze(
                    1).broadcast_to([128, 8, 256])
                t_mb = V.tag(V.eng.tensor_tensor(mb, mb, cbex, OP.mult))
                V.wait(t_mb)
            if stage == 634 and c == 1 and hg == 1:
                fin = ctx.enter_context(nc.semaphore("fin"))
                G.waits([V.tok(), S.tok(), TE.tok(), dS.tok(), dG.tok()])
                G.eng.dma_start(out=out[0:128, 0:2048],
                                in_=SCR[0:128, ACSB:ACSB + 4096].bitcast(F32)).then_inc(fin, 16)
                G.eng.dma_start(out=out[128:256, 0:2048],
                                in_=SCR[:, MB0:MB0 + 2048]).then_inc(fin, 16)
                G.eng.dma_start(out=out[256:384, 0:2048],
                                in_=SCR[:, MB1:MB1 + 2048]).then_inc(fin, 16)
                G.eng.dma_start(out=out[384:512, 0:2048],
                                in_=SCR[:, MPF:MPF + 4096].bitcast(F32)).then_inc(fin, 16)
                G.eng.dma_start(out=out[512:640, 0:256],
                                in_=DEX[:, 576:832]).then_inc(fin, 16)
                G.eng.dma_start(out=out[640:768, 0:512],
                                in_=SCR[:, CBMA:CBMA + 512]).then_inc(fin, 16)
                G.eng.dma_start(out=out[768:896, 0:1024],
                                in_=SC2[:, 2048:3072]).then_inc(fin, 16)
                G.eng.wait_ge(fin, 112)
                ctx.close()
                return nc
            if stage == 632 and c == 1 and hg == 0:
                fin = ctx.enter_context(nc.semaphore("fin"))
                G.waits([V.tok(), S.tok(), TE.tok(), dS.tok(), dG.tok()])
                G.eng.dma_start(out=out[0:128, 0:2048],
                                in_=SCR[0:128, ACSB:ACSB + 4096].bitcast(F32)).then_inc(fin, 16)
                G.eng.dma_start(out=out[128:256, 0:2048],
                                in_=SCR[:, MB0:MB0 + 2048]).then_inc(fin, 16)
                G.eng.dma_start(out=out[256:384, 0:2048],
                                in_=SCR[:, MB1:MB1 + 2048]).then_inc(fin, 16)
                G.eng.dma_start(out=out[384:512, 0:2048],
                                in_=RES[:, XDTA:XDTA + 2048]).then_inc(fin, 16)
                G.eng.dma_start(out=out[512:640, 0:2048],
                                in_=RES[:, XTMA:XTMA + 2048]).then_inc(fin, 16)
                G.eng.dma_start(out=out[640:768, 0:2048],
                                in_=SCR[:, MPF:MPF + 4096].bitcast(F32)).then_inc(fin, 16)
                G.eng.wait_ge(fin, 96)
                ctx.close()
                return nc
            if stage == 631 and c == 0 and hg == 0:
                fin = ctx.enter_context(nc.semaphore("fin"))
                G.waits([V.tok(), S.tok(), TE.tok(), dS.tok(), dG.tok()])
                G.eng.dma_start(out=out[0:128, 0:2048],
                                in_=SCR[0:128, ACSB:ACSB + 4096].bitcast(F32)).then_inc(fin, 16)
                G.eng.dma_start(out=out[128:256, 0:2048],
                                in_=SCR[:, MB0:MB0 + 2048]).then_inc(fin, 16)
                G.eng.dma_start(out=out[256:384, 0:2048],
                                in_=SCR[:, MB1:MB1 + 2048]).then_inc(fin, 16)
                G.eng.dma_start(out=out[384:512, 0:512],
                                in_=SCR[:, CBMA:CBMA + 512]).then_inc(fin, 16)
                G.eng.dma_start(out=out[512:640, 0:2048],
                                in_=RES[:, XDTA:XDTA + 2048]).then_inc(fin, 16)
                G.eng.dma_start(out=out[640:768, 0:2048],
                                in_=RES[:, XTMA:XTMA + 2048]).then_inc(fin, 16)
                G.eng.dma_start(out=out[768:896, 0:2048],
                                in_=RES[:, XDTA:XDTA + 2048]).then_inc(fin, 16)
                G.eng.dma_start(out=out[896:1024, 0:1024],
                                in_=SCR[:, DTTM:DTTM + 1024]).then_inc(fin, 16)
                G.eng.dma_start(out=out[1024:1152, 0:2048],
                                in_=RES[:, XDDA:XDDA + 2048]).then_inc(fin, 16)
                G.eng.wait_ge(fin, 144)
                ctx.close()
                return nc
            bi_d0 = p.next_pf()
            bi_d1 = p.next_pf()
            bi_cs = p.next_pf()
            TE.waits([V.tok(), p.pf_tok[bi_d0], p.pf_tok[bi_d1], p.pf_tok[bi_cs]])
            for h in range(8):
                hh = hg * 8 + h
                for lh, bi_d in ((0, bi_d0), (1, bi_d1)):
                    TE.eng.matmul(
                        p.pf[bi_d][:, h * 64:(h + 1) * 64],
                        SCR[:, MB0 + h * 256 + lh * 128:MB0 + h * 256 + (lh + 1) * 128],
                        RES[:, XDTA + hh * 64:XDTA + (hh + 1) * 64],
                        start=True, stop=False)
                    TE.eng.matmul(
                        p.pf[bi_d][:, h * 64:(h + 1) * 64],
                        SCR[:, MB1 + h * 256 + lh * 128:MB1 + h * 256 + (lh + 1) * 128],
                        RES[:, XDTA + 4096 + hh * 64:XDTA + 4096 + (hh + 1) * 64],
                        start=False, stop=True)
                TE.eng.matmul(p.pf[bi_cs][:, h * 64:(h + 1) * 64],
                              SCR[:, BTM:BTM + 128],
                              RES[:, XDDA + hh * 64:XDDA + (hh + 1) * 64],
                              start=True, stop=False)
                i_cs = TE.eng.matmul(p.pf[bi_cs][:, h * 64:(h + 1) * 64],
                                     SCR[:, BTM + 128:BTM + 256],
                                     RES[:, XDDA + 4096 + hh * 64:XDDA + 4096 + (hh + 1) * 64],
                                     start=False, stop=True)
            t_mm = TE.tag(i_cs)
            for lh, bi_d in ((0, bi_d0), (1, bi_d1)):
                dbex = DEX[:, 512 + hg * 8:512 + (hg + 1) * 8].unsqueeze(
                    2).broadcast_to([128, 8, 64])
                ysl = RES[:, XCMA + lh * 4096 + hg * 512:XCMA + lh * 4096 + (hg + 1) * 512]
                V.waits([t_mm, tok_db])
                t_y1 = V.tag(V.eng.tensor_tensor(
                    ysl, RES[:, XTMA + lh * 4096 + hg * 512:XTMA + lh * 4096 + (hg + 1) * 512],
                    dbex, OP.mult))
                V.wait(t_y1)
                t_y2 = V.tag(V.eng.tensor_tensor(ysl, ysl, p.pf[bi_d][:], OP.add))
                p.pf_tok[bi_d] = t_y2
            V.wait(t_mm)
            t_cse = V.tag(V.eng.tensor_copy(DEX[:, 832:1344], p.pf[bi_cs][:]))
            p.pf_tok[bi_cs] = t_cse
            SY.wait(t_cse)
            t_css = dS.tag(SY.eng.dma_start(
                out=cs_d[c * 128:(c + 1) * 128, hg * 512:(hg + 1) * 512],
                in_=DEX[:, 832:1344]))
            V.wait(t_css)
        if stage == 633 and c == 1:
            fin = ctx.enter_context(nc.semaphore("fin"))
            G.waits([V.tok(), S.tok(), TE.tok(), dS.tok(), dG.tok()])
            G.eng.dma_start(out=out[0:128, 0:2048],
                            in_=RES[:, XCMA:XCMA + 2048]).then_inc(fin, 16)
            G.eng.dma_start(out=out[128:256, 0:2048],
                            in_=RES[:, XCMA + 4096:XCMA + 4096 + 2048]).then_inc(fin, 16)
            G.eng.dma_start(out=out[256:384, 0:2048],
                            in_=SCR[:, MB0:MB0 + 2048]).then_inc(fin, 16)
            G.eng.dma_start(out=out[384:512, 0:2048],
                            in_=RES[:, XDTA:XDTA + 2048]).then_inc(fin, 16)
            G.eng.wait_ge(fin, 64)
            ctx.close()
            return nc
        SY.wait(V.tok())
        for lh in range(2):
            t_yd = dS.tag(SY.eng.dma_start(
                out=yd_d[c * 256 + lh * 128:c * 256 + (lh + 1) * 128, :],
                in_=RES[:, XCMA + lh * 4096:XCMA + (lh + 1) * 4096]))
        tok_ydprev = t_yd
    tok_yd = dS.tok()

    if stage == 63:
        fin = ctx.enter_context(nc.semaphore("fin"))
        G.waits([tok_yd, V.tok()])
        n = 0
        for i in range(8):
            G.eng.dma_start(out=SCR[:, 0:4096],
                            in_=yd_d[i * 256:i * 256 + 128, :]).then_inc(fin, 16)
            n += 16
            G.eng.wait_ge(fin, n)
            G.eng.dma_start(out=out[i * 128:(i + 1) * 128, 0:2048],
                            in_=SCR[:, 0:2048].bitcast(F32)[:, 0:2048] if False
                            else SCR[:, 0:4096].bitcast(F32)[:, 0:2048]).then_inc(fin, 16)
            n += 16
            G.eng.wait_ge(fin, n)
        for i in range(8):
            G.eng.dma_start(out=SCR[:, 0:4096],
                            in_=cs_d[i * 128:(i + 1) * 128, :]).then_inc(fin, 16)
            n += 16
            G.eng.wait_ge(fin, n)
            G.eng.dma_start(out=out[(8 + i) * 128:(9 + i) * 128, 0:2048],
                            in_=SCR[:, 0:4096].bitcast(F32)[:, 0:2048]).then_inc(fin, 16)
            n += 16
            G.eng.wait_ge(fin, n)
        G.eng.wait_ge(fin, n)
        ctx.close()
        return nc

    PREV = 4096
    CSL = 12288
    PREVB = 0
    V.wait(dS.tok())
    t_pz = V.tag(V.eng.memset(SCR[:, PREV:PREV + 8192].bitcast(F32), 0.0))
    V.wait(t_pz)
    prevf = SCR[:, PREV:PREV + 8192].bitcast(F32)
    for c in range(NCH):
        t_cl = dS.tag(SY.eng.dma_start(out=SCR[:, CSL:CSL + 4096],
                                       in_=cs_d[c * 128:(c + 1) * 128, :]))
        dcex = DEX[:, c * 64:(c + 1) * 64].unsqueeze(2).broadcast_to([128, 64, 64])
        t_s1 = V.tag(V.eng.tensor_tensor(prevf, prevf, dcex, OP.mult))
        V.waits([t_s1, t_cl])
        t_s2 = V.tag(V.eng.tensor_tensor(prevf, prevf, SCR[:, CSL:CSL + 4096],
                                         OP.add))
        V.wait(t_s2)
        dS.wait(t_s2)
    t_ms = V.tag(V.eng.tensor_scalar(prevf, prevf, COL[:, C_EM:C_EM + 1], None,
                                     OP.mult))
    SY.wait(t_ms)
    t_sw = dS.tag(SY.eng.dma_start(out=cc_s_in, in_=prevf))
    G.wait(t_sw)
    t_sc = CC.tag(G.eng.collective_compute(
        "AllReduce", OP.add, replica_groups=[[0, 1], [2, 3], [4, 5], [6, 7]],
        ins=[cc_s_in.opt()], outs=[cc_s_out.opt()]))
    G.wait(t_sc)
    t_sl2 = dG.tag(G.eng.dma_start(out=prevf, in_=cc_s_out))
    V.wait(t_sl2)
    t_su = V.tag(V.eng.tensor_scalar(prevf, prevf, COL[:, C_HS:C_HS + 1], None,
                                     OP.mult))
    V.wait(t_su)

    if stage == 64:
        fin = ctx.enter_context(nc.semaphore("fin"))
        G.waits([V.tok(), S.tok(), dS.tok(), dG.tok()])
        G.eng.dma_start(out=out[0:128, 0:2048],
                        in_=prevf[:, 0:2048]).then_inc(fin, 16)
        G.eng.dma_start(out=out[128:256, 0:2048],
                        in_=prevf[:, 2048:4096]).then_inc(fin, 16)
        G.eng.dma_start(out=out[256:384, 0:512],
                        in_=DEX[:, 0:512]).then_inc(fin, 16)
        n = 48
        for i in range(8):
            G.eng.dma_start(out=SCR[:, 0:4096],
                            in_=cs_d[i * 128:(i + 1) * 128, :]).then_inc(fin, 16)
            n += 16
            G.eng.wait_ge(fin, n)
            G.eng.dma_start(out=out[(3 + i) * 128:(4 + i) * 128, 0:2048],
                            in_=SCR[:, 0:4096].bitcast(F32)[:, 0:2048]).then_inc(fin, 16)
            n += 16
            G.eng.wait_ge(fin, n)
        G.eng.wait_ge(fin, n)
        ctx.close()
        return nc

    YGA, YNA, YNFA, SZA, YCA, NWA = 0, 8192, 12288, 16384, 24576, 28672
    t_nw = dG.tag(G.eng.dma_start(out=RES[0:1, NWA:NWA + 4096], in_=norm_w))
    V.wait(t_nw)
    tok_nw = p.bcast_row(RES[:, NWA:NWA + 4096], 4096)

    zslab = [WSlot(WB[:, 0:2048]), WSlot(WB[:, 2048:4096])]
    for sl in zslab:
        sl.war = TE.tok()

    for c in range(NCH):
        V.wait(TE.tok())
        t_pb = V.tag(V.eng.tensor_copy(SCR[:, PREVB:PREVB + 4096], prevf))
        V.wait(t_pb)
        t_cl = dS.tag(SY.eng.dma_start(out=SCR[:, CSL:CSL + 4096],
                                       in_=cs_d[c * 128:(c + 1) * 128, :]))
        dcex = DEX[:, c * 64:(c + 1) * 64].unsqueeze(2).broadcast_to([128, 64, 64])
        t_s1 = V.tag(V.eng.tensor_tensor(prevf, prevf, dcex, OP.mult))
        V.waits([t_s1, t_cl])
        t_s2 = V.tag(V.eng.tensor_tensor(prevf, prevf, SCR[:, CSL:CSL + 4096],
                                         OP.add))
        S.wait(V.tok())
        for zm in range(8):
            bks = [p.next_pf() for _ in range(2)]
            for b in range(2):
                TE.wait(p.pf_tok[bks[b]])
            last = None
            for g in range(4):
                sl = zslab[g % 2]
                dS.wait(sl.war)
                toksl = []
                for rr in range(2):
                    kc0 = 4 * g + 2 * rr
                    r = kc0 // 2
                    toksl.append(p.slot_dma(
                        dS, sl,
                        sl.ap.rearrange("p (kc m) -> p kc m", m=512)[
                            :, 2 * rr:2 * rr + 2, :],
                        RIP[r][:, :, zm * 512:(zm + 1) * 512]))
                TE.waits(toksl + [tok_xn3])
                for j in range(4):
                    kc = 4 * g + j
                    for lh in range(2):
                        tcb = 2 * c + lh
                        inst = TE.eng.matmul(
                            p.pf[bks[lh]][:],
                            XNF[:, kc * T + tcb * 128:kc * T + (tcb + 1) * 128],
                            sl.ap.rearrange("p (kc m) -> p kc m", m=512)[:, j, :],
                            start=(kc == 0), stop=(kc == 15))
                    last = TE.tag(inst)
                sl.war = last
            for lh in range(2):
                S.waits([last, p.pf_tok[bks[lh]]])
                t_sz = S.tag(S.eng.activation(
                    RES[:, SZA + lh * 4096 + zm * 512:SZA + lh * 4096 + (zm + 1) * 512],
                    p.pf[bks[lh]][:], AF.Silu))
                p.pf_tok[bks[lh]] = t_sz
        for lh in range(2):
            tcb = 2 * c + lh
            ysl = RES[:, YCA:YCA + 4096]
            dS.wait(S.tok())
            t_ydl = dS.tag(SY.eng.dma_start(
                out=ysl, in_=yd_d[c * 256 + lh * 128:c * 256 + (lh + 1) * 128, :]))
            for hg in range(8):
                bi = p.next_pf()
                pk = p.pf[bi]
                TE.waits([V.tok(), p.pf_tok[bi]])
                for h in range(8):
                    hh = hg * 8 + h
                    i_yo = TE.eng.matmul(
                        pk[:, h * 64:(h + 1) * 64],
                        SCR[:, CCM + c * 256 + lh * 128:CCM + c * 256 + (lh + 1) * 128],
                        SCR[:, PREVB + hh * 64:PREVB + (hh + 1) * 64],
                        start=True, stop=True)
                t_mm = TE.tag(i_yo)
                eaex = SCR[:, EXPA + tcb * 64 + hg * 8:EXPA + tcb * 64 + (hg + 1) * 8].\
                    unsqueeze(2).broadcast_to([128, 8, 64])
                V.waits([t_mm, t_ydl])
                t_e1 = V.tag(V.eng.tensor_tensor(DEX[:, 1344:1856], pk[:], eaex,
                                                 OP.mult))
                V.wait(t_e1)
                t_e2 = V.tag(V.eng.tensor_tensor(
                    ysl[:, hg * 512:(hg + 1) * 512], ysl[:, hg * 512:(hg + 1) * 512],
                    DEX[:, 1344:1856], OP.add))
                p.pf_tok[bi] = t_e2
            if stage == 65 and c == 0 and lh == 0:
                fin = ctx.enter_context(nc.semaphore("fin"))
                G.waits([V.tok(), S.tok(), TE.tok(), dS.tok(), dG.tok()])
                G.eng.dma_start(out=out[0:128, 0:2048],
                                in_=ysl[:, 0:2048]).then_inc(fin, 16)
                G.eng.dma_start(out=out[128:256, 0:2048],
                                in_=ysl[:, 2048:4096]).then_inc(fin, 16)
                G.eng.dma_start(out=out[256:384, 0:2048],
                                in_=RES[:, SZA:SZA + 2048]).then_inc(fin, 16)
                G.eng.dma_start(out=out[384:512, 0:2048],
                                in_=RES[:, SZA + 2048:SZA + 4096]).then_inc(fin, 16)
                G.eng.dma_start(out=out[512:640, 0:1024],
                                in_=SCR[:, EXPA:EXPA + 1024]).then_inc(fin, 16)
                G.eng.dma_start(out=out[640:768, 0:2048],
                                in_=SCR[:, PREVB:PREVB + 2048]).then_inc(fin, 16)
                G.eng.wait_ge(fin, 96)
                ctx.close()
                return nc
            ygf = RES[:, YGA:YGA + 8192].bitcast(F32)
            V.wait(S.tok())
            V.wait(V.tok())
            t_yg = V.tag(V.eng.tensor_tensor(
                ygf, ysl, RES[:, SZA + lh * 4096:SZA + (lh + 1) * 4096], OP.mult))
            S.wait(t_yg)
            t_sq = S.tag(S.eng.activation(ysl, ygf, AF.Square,
                                          accum_out=COL[:, 10:11]))
            S.wait(t_sq)
            t_l = S.tag(S.eng.activation(COL[:, 11:12], COL[:, 10:11], AF.Ln,
                                         bias=COL[:, C_EPS:C_EPS + 1],
                                         scale=1.0 / DIN))
            S.wait(t_l)
            t_r = S.tag(S.eng.activation(COL[:, 11:12], COL[:, 11:12], AF.Exp,
                                         scale=-0.5))
            V.waits([t_r, tok_nw])
            t_n1 = V.tag(V.eng.tensor_scalar(ygf, ygf, COL[:, 11:12], None,
                                             OP.mult))
            V.wait(t_n1)
            t_n2 = V.tag(V.eng.tensor_tensor(RES[:, YNA:YNA + 4096], ygf,
                                             RES[:, NWA:NWA + 4096], OP.mult))
            for xc in range(16):
                bi = p.next_pb()
                TE.waits([t_n2, p.pb_tok[bi]])
                i1 = TE.eng.transpose(p.pb[bi][:, 0:128],
                                      RES[:, YNA + (2 * xc) * 128:YNA + (2 * xc + 1) * 128],
                                      CID[:])
                i2 = TE.eng.transpose(p.pb[bi][:, 128:256],
                                      RES[:, YNA + (2 * xc + 1) * 128:YNA + (2 * xc + 2) * 128],
                                      CID[:])
                t_tr = TE.tag(i2)
                V.wait(t_tr)
                inst = V.eng.tensor_copy(RES[:, YNFA + xc * 256:YNFA + (xc + 1) * 256],
                                         p.pb[bi][:, 0:256])
                p.pb_tok[bi] = V.tag(inst)
            tok_ynf = V.tok()
            dS.wait(tok_ynf)
            t_x2l = dS.tag(SY.eng.dma_start(
                out=RES[:, YNA:YNA + 2048],
                in_=x2_d[c * 256 + lh * 128:c * 256 + (lh + 1) * 128, :]))
            for ot in range(4):
                bi = p.next_pf()
                pk = p.pf[bi]
                TE.waits([tok_ynf, p.pf_tok[bi]])
                last = None
                for g in range(8):
                    sl = zslab[g % 2]
                    dS.wait(sl.war)
                    r = g
                    t_w = p.slot_dma(
                        dS, sl, sl.ap.rearrange("p (kc m) -> p kc m", m=512),
                        ROP[r][:, :, ot * 512:(ot + 1) * 512])
                    TE.wait(t_w)
                    for j in range(4):
                        ic = 4 * g + j
                        inst = TE.eng.matmul(
                            pk[:], RES[:, YNFA + ic * 128:YNFA + (ic + 1) * 128],
                            sl.ap.rearrange("p (kc m) -> p kc m", m=512)[:, j, :],
                            start=(ic == 0), stop=(ic == 31))
                    last = TE.tag(inst)
                    sl.war = last
                V.waits([last, t_x2l])
                t_ev = V.tag(V.eng.tensor_tensor(
                    RES[:, YNA + ot * 512:YNA + (ot + 1) * 512],
                    RES[:, YNA + ot * 512:YNA + (ot + 1) * 512], pk[:], OP.add))
                p.pf_tok[bi] = t_ev
            SY.wait(V.tok())
            t_x3 = dS.tag(SY.eng.dma_start(
                out=x3_d[c * 256 + lh * 128:c * 256 + (lh + 1) * 128, :],
                in_=RES[:, YNA:YNA + 2048]))
            V.wait(t_x3)
    tok_x3 = dS.tok()

    if stage == 7:
        fin = ctx.enter_context(nc.semaphore("fin"))
        G.wait(tok_x3)
        n = 0
        for tc in range(NTC):
            G.eng.dma_start(out=SCR[:, 0:D],
                            in_=x3_d[tc * 128:(tc + 1) * 128, :]).then_inc(fin, 16)
            n += 16
            G.eng.wait_ge(fin, n)
            G.eng.dma_start(out=out[tc * 128:(tc + 1) * 128, :],
                            in_=SCR[:, 0:D]).then_inc(fin, 16)
            n += 16
            G.eng.wait_ge(fin, n)
        ctx.close()
        return nc

    dS.waits([V.tok(), S.tok(), TE.tok()])
    S.wait(V.tok())
    V.wait(S.tok())
    C_RS8, C_SWS = 33, 34
    t_s8 = dS.tag(SY.eng.dma_start(out=COL[0:1, C_RS8:C_RS8 + 2], in_=scal8))
    V.wait(t_s8)
    V.eng.tensor_copy(COL[32:33, C_RS8:C_RS8 + 2], COL[0:1, C_RS8:C_RS8 + 2])
    V.eng.tensor_copy(COL[64:65, C_RS8:C_RS8 + 2], COL[0:1, C_RS8:C_RS8 + 2])
    t_cc3 = V.tag(V.eng.tensor_copy(COL[96:97, C_RS8:C_RS8 + 2],
                                    COL[0:1, C_RS8:C_RS8 + 2]))
    V.wait(t_cc3)
    t_sh8 = V.tag(V.eng.stream_shuffle(COL[:, C_RS8:C_RS8 + 2],
                                       COL[:, C_RS8:C_RS8 + 2], [0] * 32))
    V.wait(t_sh8)
    t_sws = V.tag(V.eng.tensor_tensor(COL[:, C_SWS:C_SWS + 1],
                                      COL[:, C_RS8:C_RS8 + 1],
                                      COL[:, C_SWS:C_SWS + 1], OP.mult))
    V.wait(t_sws)
    t_rs8 = V.tag(V.eng.reciprocal(COL[:, C_RS8:C_RS8 + 1], COL[:, C_RS8:C_RS8 + 1]))
    V.wait(t_rs8)

    V.wait(TE.tok())
    for tc in range(NTC):
        t_ld = dS.tag(SY.eng.dma_start(out=SCR[:, 0:2048],
                                       in_=x3_d[tc * 128:(tc + 1) * 128, :]))
        V.wait(t_ld)
        t_r = ln_cols(SCR[:, 0:2048], C_M1, C_R1)
        V.wait(t_r)
        t_x = V.tag(V.eng.tensor_scalar(sf32(4096, 2048), SCR[:, 0:2048],
                                        COL[:, C_M1:C_M1 + 1], COL[:, C_R1:C_R1 + 1],
                                        OP.subtract, OP.mult))
        V.wait(t_x)
        t_u = V.tag(V.eng.tensor_scalar(sf32(8192, 2048), sf32(4096, 2048),
                                        COL[:, C_RS8:C_RS8 + 1], 240.0,
                                        OP.mult, OP.min))
        V.wait(t_u)
        t_q8 = V.tag(V.eng.tensor_scalar(SCR[:, 12288:13312].bitcast(FP8),
                                         sf32(8192, 2048), -240.0, None, OP.max))
        V.wait(t_q8)
        t_dec = V.tag(V.eng.tensor_copy(SCR[:, 16384:18432],
                                        SCR[:, 12288:13312].bitcast(FP8)))
        for half in range(2):
            bi = p.next_pb()
            TE.waits([t_dec, p.pb_tok[bi]])
            for j in range(8):
                kc = half * 8 + j
                inst = TE.eng.transpose(p.pb[bi][:, j * 128:(j + 1) * 128],
                                        SCR[:, 16384 + kc * 128:16384 + (kc + 1) * 128],
                                        CID[:])
            t_tr = TE.tag(inst)
            V.wait(t_tr)
            for j in range(8):
                kc = half * 8 + j
                inst = V.eng.tensor_copy(
                    XNF[:, kc * T + tc * 128:kc * T + (tc + 1) * 128],
                    p.pb[bi][:, j * 128:(j + 1) * 128])
            p.pb_tok[bi] = V.tag(inst)
        dS.wait(V.tok())
    tok_xq8 = V.tok()

    w8slab = [WSlot(WB[:, 0:2048]), WSlot(WB[:, 2048:4096])]
    for sl in w8slab:
        sl.war = TE.tok()
    for tt in range(4):
        SY.wait(V.tok())
        x3toks = []
        for b in range(4):
            tc = tt * 4 + b
            x3toks.append(dS.tag(SY.eng.dma_start(
                out=SCR[:, b * 2048:(b + 1) * 2048],
                in_=x3_d[tc * 128:(tc + 1) * 128, :])))
        for ot in range(4):
            bks = [p.next_pf() for _ in range(4)]
            for b in range(4):
                TE.wait(p.pf_tok[bks[b]])
            last = None
            for g in range(4):
                sl = w8slab[g % 2]
                dS.wait(sl.war)
                toks = []
                for rr in range(2):
                    kc0 = 4 * g + 2 * rr
                    r = kc0 // 2
                    toks.append(p.slot_dma(
                        dS, sl,
                        sl.ap.rearrange("p (kc m) -> p kc m", m=512)[
                            :, 2 * rr:2 * rr + 2, :],
                        RW8[r][:, :, ot * 512:(ot + 1) * 512]))
                TE.waits(toks + [tok_xq8])
                for j in range(4):
                    ic = 4 * g + j
                    for b in range(4):
                        tc = tt * 4 + b
                        inst = TE.eng.matmul(
                            p.pf[bks[b]][:],
                            XNF[:, ic * T + tc * 128:ic * T + (tc + 1) * 128],
                            sl.ap.rearrange("p (kc m) -> p kc m", m=512)[:, j, :],
                            start=(ic == 0), stop=(ic == 15))
                    last = TE.tag(inst)
                sl.war = last
            for b in range(4):
                tc = tt * 4 + b
                V.waits([last, x3toks[b], p.pf_tok[bks[b]]])
                t_e = V.tag(V.eng.scalar_tensor_tensor(
                    SC2[:, 0:512],
                    p.pf[bks[b]][:], COL[:, C_SWS:C_SWS + 1],
                    SCR[:, b * 2048 + ot * 512:b * 2048 + (ot + 1) * 512],
                    OP.mult, OP.add))
                p.pf_tok[bks[b]] = t_e
                SY.wait(t_e)
                t_st = dS.tag(SY.eng.dma_start(
                    out=out[tc * 128:(tc + 1) * 128, ot * 512:(ot + 1) * 512],
                    in_=SC2[:, 0:512]))
                V.wait(t_st)
    SY.wait(dS.tok())
    G.wait(dS.tok())

    ctx.close()
    return nc



def _rot_tables(tok0):
    rot = D // 2
    inv = 1.0 / (10000.0 ** (np.arange(0, rot, 2, dtype=np.float32) / rot))
    fr = (np.arange(tok0, tok0 + T, dtype=np.float32)[:, None] * inv[None, :])
    fr = np.repeat(fr, 2, axis=-1)
    return np.cos(fr).astype(BFnp), np.sin(fr).astype(BFnp)


def _consts():
    idn = np.eye(128, dtype=np.float32)
    tri = np.zeros((128, 3 * 128), np.float32)
    r = np.arange(128)
    tri[:, 0:128] = (r[:, None] <= r[None, :])
    tri[:, 128:256] = (r[:, None] > r[None, :])
    tri[:, 256:384] = 1.0
    smask = (np.arange(256)[:, None] <= np.arange(256)[None, :]).astype(np.float32)
    return idn.astype(BFnp), tri.astype(BFnp), smask.astype(BFnp)


_NC_CACHE = {}


def kernel(L0_attn_X, L0_attn_W_in, L0_attn_b_in, L0_attn_W_out, L0_attn_b_out,
           L0_attn_gamma, L0_attn_beta, L0_ffn_w_gate_up, L0_ffn_w_down,
           L1_attn_in_proj_w, L1_attn_conv_w, L1_attn_conv_b, L1_attn_dt_bias,
           L1_attn_A_log, L1_attn_D, L1_attn_norm_w, L1_attn_out_proj_w,
           L1_ffn_weight, L1_ffn_weight_scale, L1_ffn_input_scale, _stage=99):
    idn, tri, smask = _consts()
    w8v = np.ascontiguousarray(L1_ffn_weight).view(ml_dtypes.float8_e4m3)
    in_maps = []
    for c in range(8):
        b, h = c // 2, c % 2
        cosb, sinb = _rot_tables(h * T)
        m = {
            "x_in": np.ascontiguousarray(np.asarray(L0_attn_X)[b, h * T:(h + 1) * T, :]),
            "w_in": np.ascontiguousarray(L0_attn_W_in),
            "w_outw": np.ascontiguousarray(L0_attn_W_out),
            "b_in": np.ascontiguousarray(L0_attn_b_in)[None, :],
            "b_out": np.ascontiguousarray(L0_attn_b_out)[None, :],
            "gamma": np.ascontiguousarray(L0_attn_gamma)[None, :],
            "beta": np.ascontiguousarray(L0_attn_beta)[None, :],
            "cosd": cosb, "sind": sinb,
            "idn": idn, "trilm": tri, "smaskd": smask,
            "w1_rows": np.ascontiguousarray(L0_ffn_w_gate_up[c * W1R:(c + 1) * W1R, :]),
            "w2_rows": np.ascontiguousarray(L0_ffn_w_down[c * W2R:(c + 1) * W2R, :]),
            "ip_cols": np.ascontiguousarray(L1_attn_in_proj_w[:, c * IPC:(c + 1) * IPC]),
            "op_cols": np.ascontiguousarray(L1_attn_out_proj_w[:, c * OPC:(c + 1) * OPC]),
            "w8_cols": np.ascontiguousarray(w8v[:, c * W8C:(c + 1) * W8C]),
            "conv_w": np.ascontiguousarray(L1_attn_conv_w),
            "conv_b": np.ascontiguousarray(L1_attn_conv_b)[None, :],
            "dt_bias": np.ascontiguousarray(L1_attn_dt_bias)[None, :],
            "a_log": np.ascontiguousarray(L1_attn_A_log)[None, :],
            "d_vec": np.ascontiguousarray(L1_attn_D)[None, :],
            "norm_w": np.ascontiguousarray(L1_attn_norm_w)[None, :],
            "scal8": np.array([[float(np.asarray(L1_ffn_input_scale,
                                                 dtype=np.float32)),
                                float(np.asarray(L1_ffn_weight_scale))]], np.float32),
            "masks": np.array([[1.0 - h, float(h)]], np.float32),
        }
        in_maps.append(m)
    nc = build_program(_stage)
    res = run_bass_kernel_spmd(nc, in_maps, core_ids=list(range(8)))
    outp = np.zeros((4, 4096, D), np.float32)
    for c in range(8):
        b, h = c // 2, c % 2
        outp[b, h * T:(h + 1) * T, :] = res.results[c]["out"]
    return outp


# version 05
# speedup vs baseline: 1.2834x; 1.0146x over previous
import numpy as np
import ml_dtypes
from contextlib import ExitStack

import concourse.bass as bass
import concourse.mybir as mybir
from concourse.alu_op_type import AluOpType as OP
from concourse.bass_utils import run_bass_kernel_spmd

F32 = mybir.dt.float32
F16 = mybir.dt.float16
BF16 = mybir.dt.bfloat16
I32 = mybir.dt.int32
FP8 = mybir.dt.float8e4
AF = mybir.ActivationFunctionType
AX = mybir.AxisListType
BFnp = ml_dtypes.bfloat16

T = 2048
D = 2048
NTC = T // 128
NKC = D // 128
I_FFN = 5632
M_GUP = 2 * I_FFN
DIN, DSTATE, NH, HDIM, CHUNK = 4096, 128, 64, 64, 256
CONVD = DIN + 2 * DSTATE
NCH = T // CHUNK
MPROJ = 2 * (DIN + DSTATE) + NH
EPS = 1e-5

W1R = M_GUP // 8
W2R = D // 8
IPC = D // 8
OPC = DIN // 8
W8C = D // 8

O_W1 = 0
O_W2 = O_W1 + D * W1R
O_IP = O_W2 + I_FFN * W2R
O_OP = O_IP + IPC * MPROJ
O_W8 = O_OP + OPC * D
RANKB_ELEMS = O_W8 + W8C * D


class Q:

    def __init__(self, eng, sem, step=1, extra=None):
        self.eng = eng
        self.sem = sem
        self.step = step
        self.cnt = 0
        self.seen = {}
        self.extra = list(extra or [])
        self.dma = step == 16

    def wait(self, tok):
        if tok is None:
            return
        sem, val = tok
        key = id(sem)
        if self.seen.get(key, 0) >= val:
            return
        self.eng.wait_ge(sem, val)
        self.seen[key] = val

    def waits(self, toks):
        for t in toks:
            self.wait(t)

    def tag(self, inst):
        if self.cnt + self.step > 60000 and self.extra:
            self.sem = self.extra.pop(0)
            self.cnt = 0
        self.cnt += self.step
        inst.then_inc(self.sem, self.step)
        if self.dma:
            self.eng.wait_ge(self.sem, self.cnt)
            self.seen[id(self.sem)] = self.cnt
        return (self.sem, self.cnt)

    def tok(self):
        return (self.sem, self.cnt)


class WSlot:

    def __init__(self, ap):
        self.ap = ap
        self.war = None
        self.sem = None
        self.cnt = 0


class PB:

    def __init__(self, nc, ctx):
        self.nc = nc
        self.ctx = ctx
        sem = lambda n: ctx.enter_context(nc.semaphore(n))
        self.V = Q(nc.vector, sem("sv"))
        self.S = Q(nc.scalar, sem("ss"))
        self.T = Q(nc.tensor, sem("stn"))
        self.G = Q(nc.gpsimd, sem("sgp"))
        self.SY = Q(nc.sync, sem("ssy"))
        self.dG = Q(nc.gpsimd, sem("dg"), 16,
                    extra=[sem("dg2"), sem("dg3")])
        self.dS = Q(nc.sync, sem("dsy"), 16,
                    extra=[sem("dsy2")])
        self.dA = Q(nc.scalar, sem("dact"), 16)
        self.CC = Q(nc.gpsimd, sem("scc"))
        self.pf = [ctx.enter_context(nc.psum_tensor(f"pf{i}", [128, 512], F32))
                   for i in range(6)]
        self.pb = [ctx.enter_context(nc.psum_tensor(f"pb{i}", [128, 1024], BF16))
                   for i in range(2)]
        self.pf_tok = [None] * 6
        self.pb_tok = [None] * 2
        self.pf_i = 0
        self.pb_i = 0

    def sb(self, name, shape, dtype):
        return self.ctx.enter_context(self.nc.sbuf_tensor(name, shape, dtype))

    def next_pf(self):
        i = self.pf_i
        self.pf_i = (self.pf_i + 1) % 6
        return i

    def next_pb(self):
        i = self.pb_i
        self.pb_i = (self.pb_i + 1) % 2
        return i

    def bcast_row(self, dst, n):
        V = self.V
        V.eng.tensor_copy(dst[32:33, 0:n], dst[0:1, 0:n])
        V.eng.tensor_copy(dst[64:65, 0:n], dst[0:1, 0:n])
        t = V.tag(V.eng.tensor_copy(dst[96:97, 0:n], dst[0:1, 0:n]))
        V.wait(t)
        return V.tag(V.eng.stream_shuffle(dst[:, 0:n], dst[:, 0:n], [0] * 32))

    def slot_dma(self, dq, sl, out, in_, **kw):
        if sl.sem is None:
            self._wsid = getattr(self, "_wsid", 0) + 1
            sl.sem = self.ctx.enter_context(self.nc.semaphore("ws%d" % self._wsid))
        if sl.cnt:
            dq.eng.wait_ge(sl.sem, sl.cnt)
        inst = dq.eng.dma_start(out=out, in_=in_, **kw)
        sl.cnt += 16
        inst.then_inc(sl.sem, 16)
        return (sl.sem, sl.cnt)

    def stream_mm(self, pk_ap, nk, res_fn, src_fn, slots, stream_is_lhsT,
                  pre_toks=(), dq=None):
        TE = self.T
        dq = dq or self.dG
        for tk in pre_toks:
            TE.wait(tk)
        t_mm = None
        for kc in range(nk):
            sl = slots[kc % len(slots)]
            dq.wait(sl.war)
            wt = self.slot_dma(dq, sl, sl.ap, src_fn(kc))
            TE.wait(wt)
            if stream_is_lhsT:
                inst = TE.eng.matmul(pk_ap, sl.ap, res_fn(kc),
                                     start=(kc == 0), stop=(kc == nk - 1))
            else:
                inst = TE.eng.matmul(pk_ap, res_fn(kc), sl.ap,
                                     start=(kc == 0), stop=(kc == nk - 1))
            t_mm = TE.tag(inst)
            sl.war = t_mm
        return t_mm


def build_program(stage):
    nc = bass.Bass("TRN2", target_bir_lowering=False)
    ctx = ExitStack()

    def dram(n, shp, dt, kind=None):
        if kind:
            return nc.dram_tensor(n, shp, dt, kind=kind).ap()
        return nc.dram_tensor(n, shp, dt).ap()

    x_in = dram("x_in", [T, D], F32, "ExternalInput")
    w_in = dram("w_in", [D, 3 * D], F32, "ExternalInput")
    w_outw = dram("w_outw", [D, D], F32, "ExternalInput")
    b_in = dram("b_in", [1, 3 * D], F32, "ExternalInput")
    b_out = dram("b_out", [1, D], F32, "ExternalInput")
    gamma = dram("gamma", [1, D], F32, "ExternalInput")
    beta = dram("beta", [1, D], F32, "ExternalInput")
    cosd = dram("cosd", [T, 1024], BF16, "ExternalInput")
    sind = dram("sind", [T, 1024], BF16, "ExternalInput")
    idn = dram("idn", [128, 128], BF16, "ExternalInput")
    trilm = dram("trilm", [128, 3 * 128], BF16, "ExternalInput")
    smaskd = dram("smaskd", [256, 256], BF16, "ExternalInput")
    w1_rows = dram("w1_rows", [W1R, D], F32, "ExternalInput")
    w2_rows = dram("w2_rows", [W2R, I_FFN], F32, "ExternalInput")
    ip_cols = dram("ip_cols", [MPROJ, IPC], F32, "ExternalInput")
    op_cols = dram("op_cols", [D, OPC], F32, "ExternalInput")
    w8_cols = dram("w8_cols", [D, W8C], FP8, "ExternalInput")
    conv_w = dram("conv_w", [CONVD, 4], F32, "ExternalInput")
    conv_b = dram("conv_b", [1, CONVD], F32, "ExternalInput")
    dt_bias = dram("dt_bias", [1, NH], F32, "ExternalInput")
    a_log = dram("a_log", [1, NH], F32, "ExternalInput")
    d_vec = dram("d_vec", [1, NH], F32, "ExternalInput")
    norm_w = dram("norm_w", [1, DIN], F32, "ExternalInput")
    scal8 = dram("scal8", [1, 2], F32, "ExternalInput")
    masks = dram("masks", [1, 2], F32, "ExternalInput")

    out = dram("out", [T, D], F32, "ExternalOutput")

    cc_a_in = dram("cc_a_in", [1, D], F32)
    cc_a_out = dram("cc_a_out", [1, D], F32)
    cc_h_in = dram("cc_h_in", [128, 34 * 3], F32)
    cc_h_out = dram("cc_h_out", [128, 34 * 3], F32)
    cc_s_in = dram("cc_s_in", [128, NH * HDIM], F32)
    cc_s_out = dram("cc_s_out", [128, NH * HDIM], F32)
    RANKB = RANKB_ELEMS
    cc_w_in = dram("cc_w_in", [1, RANKB], BF16)
    cc_w_out = dram("cc_w_out", [8, RANKB], BF16)
    xbcraw_d = dram("xbcraw_d", [CONVD, T], BF16)
    xconv_d = dram("xconv_d", [DIN, T], BF16)
    res_spill = dram("res_spill", [T, D], BF16)
    x2_d = dram("x2_d", [T, D], BF16)
    x3_d = dram("x3_d", [T, D], BF16)
    cs_d = dram("cs_d", [NCH * 128, NH * HDIM], BF16)
    yd_d = dram("yd_d", [T, NH * HDIM], BF16)

    p = PB(nc, ctx)
    V, S, TE, G, SY = p.V, p.S, p.T, p.G, p.SY
    dG, dS, dA, CC = p.dG, p.dS, p.dA, p.CC

    RES = p.sb("RES", [128, NTC * D], BF16)
    XNF = p.sb("XNF", [128, NKC * T], BF16)
    SCR = p.sb("SCR", [128, 22528], BF16)
    SC2 = p.sb("SC2", [128, 4608], F32)
    WB = p.sb("WB", [128, 4096], BF16)
    COL = p.sb("COL", [128, 80], F32)
    CID = p.sb("CID", [128, 128], BF16)
    CTRI = p.sb("CTRI", [128, 3 * 128], BF16)
    CSM = p.sb("CSM", [128, 2 * 256], BF16)
    DEX = p.sb("DEX", [128, 2176], BF16)

    C_EPS, C_M1, C_R1, C_M2, C_R2 = 0, 1, 3, 4, 6
    C_STAT = 8
    C_SSK, C_SSV, C_RNK, C_RNV, C_RN, C_TMP = 33, 34, 35, 36, 37, 38
    C_ONE = 39
    C_ACOL = 40
    C_BQ = 56

    ws_a = [WSlot(WB[:, 0:512]), WSlot(WB[:, 512:1024])]
    ws_b = [WSlot(WB[:, 1024:1536]), WSlot(WB[:, 1536:2048])]
    ws_c = [WSlot(WB[:, 2048:2560]), WSlot(WB[:, 2560:3072])]
    ws_q = [WSlot(WB[:, 3072:3200]), WSlot(WB[:, 3200:3328])]

    _areas = [(RES, NTC * D), (XNF, NKC * T), (SCR, 22528), (WB, 4096),
              (CID, 128), (CTRI, 3 * 128), (CSM, 2 * 256), (DEX, 2176),
              (SC2, 4608), (COL, 80)]
    _mi = None
    for _arena, _w in _areas:
        for _c0 in range(0, _w, 8192):
            _mi = V.eng.memset(_arena[:, _c0:min(_c0 + 8192, _w)], 0.0)
    tok_init = V.tag(_mi)
    for _q in (V, S, TE, G, SY, dS, dG, dA, CC):
        _q.wait(tok_init)

    ld_id = dS.tag(SY.eng.dma_start(out=CID[:], in_=idn))
    dS.tag(SY.eng.dma_start(out=CTRI[:], in_=trilm))
    dS.tag(SY.eng.dma_start(out=CSM[:, 0:256], in_=smaskd[0:128, :]))
    ld_sm = dS.tag(SY.eng.dma_start(out=CSM[:, 256:512], in_=smaskd[128:256, :]))
    V.eng.memset(COL[:, C_EPS:C_EPS + 1], EPS)
    V.eng.memset(COL[:, C_ONE:C_ONE + 1], 1.0)
    tok_eps = V.tag(V.eng.memset(COL[:, C_TMP:C_TMP + 1], 0.0))
    ld_bq = dS.tag(SY.eng.dma_start(
        out=COL[:, C_BQ:C_BQ + 16],
        in_=b_in.rearrange("o (kc p) -> (o p) kc", p=128)[:, 0:16],
        allow_slow_non_contiguous=True))

    XB = XNF

    def xf32(c0, w):
        return XB[:, c0:c0 + 2 * w].bitcast(F32)

    def xi32(c0, w):
        return XB[:, c0:c0 + 2 * w].bitcast(I32)

    PA0, PA1, PA2, PA3 = 72, 73, 74, 75
    tok_prep_col = V.tag(V.eng.memset(COL[:, 76:77], 0.0))

    def quant_chain(wt_f32, qi_i32, rf_f32, qf_f32, out_bf, n, amax_src=None):
        V.wait(dA.tok())
        """Quantize wt (fp32 [?,n]) -> dequantized bf16 out. Uses COL PA0..3.
        amax_src: if given, per-row amax was precomputed into COL[PA1]=s,
        COL[PA2]=rs, COL[PA3]=-256*s already."""
        if amax_src is None:
            t1 = V.tag(V.eng.tensor_reduce(COL[:, PA0:PA0 + 1], wt_f32, AX.X,
                                           OP.max, apply_absolute_value=True))
            V.wait(t1)
            t2 = V.tag(V.eng.tensor_scalar(COL[:, PA1:PA1 + 1], COL[:, PA0:PA0 + 1],
                                           1.0 / 127.0, 1e-8, OP.mult, OP.max))
            V.wait(t2)
            t3 = V.tag(V.eng.reciprocal(COL[:, PA2:PA2 + 1], COL[:, PA1:PA1 + 1]))
            t4 = V.tag(V.eng.tensor_scalar(COL[:, PA3:PA3 + 1], COL[:, PA1:PA1 + 1],
                                           -256.0, None, OP.mult))
            V.waits([t3, t4])
        t5 = V.tag(V.eng.tensor_scalar(qi_i32, wt_f32, COL[:, PA2:PA2 + 1], 256.0,
                                       OP.mult, OP.add))
        V.wait(t5)
        t6 = V.tag(V.eng.scalar_tensor_tensor(rf_f32, wt_f32, COL[:, PA2:PA2 + 1],
                                              qi_i32, OP.mult, OP.subtract))
        V.wait(t6)
        t7 = V.tag(V.eng.scalar_tensor_tensor(qf_f32, rf_f32, -255.5, qi_i32,
                                              OP.is_ge, OP.add))
        V.wait(t7)
        t8 = V.tag(V.eng.tensor_scalar(qf_f32, qf_f32, 129.0, 383.0,
                                       OP.max, OP.min))
        V.wait(t8)
        t9 = V.tag(V.eng.tensor_scalar(out_bf, qf_f32, COL[:, PA1:PA1 + 1],
                                       COL[:, PA3:PA3 + 1], OP.mult, OP.add))
        return t9

    def prep_transpose_store(src_bf, nkt, stage_buf, stage_c0, dst_view):
        V.wait(dA.tok())
        nleft = nkt
        j0 = 0
        while nleft > 0:
            nb = min(8, nleft)
            bi = p.next_pb()
            TE.waits([V.tok(), ld_id, p.pb_tok[bi]])
            for j in range(nb):
                kc = j0 + j
                inst = TE.eng.transpose(p.pb[bi][:, j * 128:(j + 1) * 128],
                                        src_bf[:, kc * 128:(kc + 1) * 128], CID[:])
            t_tr = TE.tag(inst)
            V.wait(t_tr)
            for j in range(nb):
                kc = j0 + j
                inst = V.eng.tensor_copy(
                    stage_buf[:, stage_c0 + kc * 128:stage_c0 + (kc + 1) * 128],
                    p.pb[bi][:, j * 128:(j + 1) * 128])
            p.pb_tok[bi] = V.tag(inst)
            j0 += nb
            nleft -= nb
        S.wait(V.tok())
        return dA.tag(S.eng.dma_start(
            out=dst_view,
            in_=stage_buf[:, stage_c0:stage_c0 + nkt * 128].rearrange(
                "p (kc j) -> p kc j", j=128)))

    prep_store_toks = []
    rr1 = cc_w_in[0:1, O_W1:O_W1 + D * W1R].rearrange(
        "o (kc pp m) -> (o pp) kc m", pp=128, m=W1R)
    for rc in range(W1R // 128):
        dS.wait(V.tok())
        t_ld = dS.tag(SY.eng.dma_start(out=xf32(0, 2048),
                                       in_=w1_rows[rc * 128:(rc + 1) * 128, :]))
        V.waits([t_ld, tok_prep_col])
        quant_chain(xf32(0, 2048), xi32(4096, 2048), xf32(8192, 2048),
                    xf32(12288, 2048), XB[:, 16384:18432], 2048)
        t_st = prep_transpose_store(XB[:, 16384:18432], 16, XB, 18432,
                                    rr1[:, :, rc * 128:(rc + 1) * 128])
        prep_store_toks.append(t_st)

    rr2 = cc_w_in[0:1, O_W2:O_W2 + I_FFN * W2R].rearrange(
        "o (ic pp m) -> (o pp) ic m", pp=128, m=W2R)
    for rc in range(W2R // 128):
        dS.wait(V.tok())
        t_ld = dS.tag(SY.eng.dma_start(out=xf32(0, 5632),
                                       in_=w2_rows[rc * 128:(rc + 1) * 128, :]))
        V.wait(t_ld)
        t1 = V.tag(V.eng.tensor_reduce(COL[:, PA0:PA0 + 1], xf32(0, 5632), AX.X,
                                       OP.max, apply_absolute_value=True))
        V.wait(t1)
        t2 = V.tag(V.eng.tensor_scalar(COL[:, PA1:PA1 + 1], COL[:, PA0:PA0 + 1],
                                       1.0 / 127.0, 1e-8, OP.mult, OP.max))
        V.wait(t2)
        t3 = V.tag(V.eng.reciprocal(COL[:, PA2:PA2 + 1], COL[:, PA1:PA1 + 1]))
        t4 = V.tag(V.eng.tensor_scalar(COL[:, PA3:PA3 + 1], COL[:, PA1:PA1 + 1],
                                       -256.0, None, OP.mult))
        V.waits([t3, t4])
        for hh in range(2):
            w0 = hh * 2816
            quant_chain(xf32(0, 5632)[:, w0:w0 + 2816],
                        xi32(11264, 2816), xf32(16896, 2816), xf32(22528, 2816),
                        XB[:, 28160:30976], 2816, amax_src=True)
            t_st = prep_transpose_store(
                XB[:, 28160:30976], 22, SCR, 0,
                rr2[:, hh * 22:(hh + 1) * 22, rc * 128:(rc + 1) * 128])
            prep_store_toks.append(t_st)

    rrip = cc_w_in[0:1, O_IP:O_IP + IPC * MPROJ].rearrange(
        "o (kh pp m) -> (o pp) kh m", pp=128, m=MPROJ)
    for mc in range(67):
        rows = 128 if mc < 66 else MPROJ - 66 * 128
        dG.wait(V.tok())
        dG.wait(TE.tok())
        t_ld = dG.tag(G.eng.dma_start(
            out=XB[0:rows, 0:256], in_=ip_cols[mc * 128:mc * 128 + rows, :]))
        V.wait(t_ld)
        bi = p.next_pb()
        TE.waits([t_ld, ld_id, p.pb_tok[bi], V.tok()])
        for kh in range(2):
            inst = TE.eng.transpose(p.pb[bi][0:128, kh * 128:kh * 128 + rows],
                                    XB[0:rows, kh * 128:(kh + 1) * 128],
                                    CID[0:rows, 0:rows])
        t_tr = TE.tag(inst)
        V.wait(t_tr)
        V.wait(dA.tok())
        for kh in range(2):
            inst = V.eng.tensor_copy(XB[:, 512 + kh * 128:512 + kh * 128 + rows],
                                     p.pb[bi][0:128, kh * 128:kh * 128 + rows])
        p.pb_tok[bi] = V.tag(inst)
        S.wait(V.tok())
        t_st = dA.tag(S.eng.dma_start(
            out=rrip[:, :, mc * 128:mc * 128 + rows],
            in_=XB[:, 512:768].rearrange("p (kh j) -> p kh j", j=128)[:, :, 0:rows]))
        prep_store_toks.append(t_st)

    rrop = cc_w_in[0:1, O_OP:O_OP + OPC * D].rearrange(
        "o (kh pp m) -> (o pp) kh m", pp=128, m=D)
    for mc in range(16):
        dG.wait(V.tok())
        dG.wait(TE.tok())
        t_ld = dG.tag(G.eng.dma_start(out=XB[:, 0:512],
                                      in_=op_cols[mc * 128:(mc + 1) * 128, :]))
        V.wait(t_ld)
        bi = p.next_pb()
        TE.waits([t_ld, ld_id, p.pb_tok[bi], V.tok()])
        for kh in range(4):
            inst = TE.eng.transpose(p.pb[bi][:, kh * 128:(kh + 1) * 128],
                                    XB[:, kh * 128:(kh + 1) * 128], CID[:])
        t_tr = TE.tag(inst)
        V.wait(t_tr)
        V.wait(dA.tok())
        for kh in range(4):
            inst = V.eng.tensor_copy(XB[:, 1024 + kh * 128:1024 + (kh + 1) * 128],
                                     p.pb[bi][:, kh * 128:(kh + 1) * 128])
        p.pb_tok[bi] = V.tag(inst)
        S.wait(V.tok())
        t_st = dA.tag(S.eng.dma_start(
            out=rrop[:, :, mc * 128:(mc + 1) * 128],
            in_=XB[:, 1024:1536].rearrange("p (kh j) -> p kh j", j=128)))
        prep_store_toks.append(t_st)

    rr8 = cc_w_in[0:1, O_W8:O_W8 + W8C * D].rearrange(
        "o (kh pp m) -> (o pp) kh m", pp=128, m=D)
    s8t = SCR[:, 8192:8320].bitcast(FP8)
    for mc in range(16):
        dS.wait(V.tok())
        dS.wait(TE.tok())
        t_ld = dS.tag(SY.eng.dma_start(out=s8t[:, 0:256],
                                       in_=w8_cols[mc * 128:(mc + 1) * 128, :]))
        V.wait(t_ld)
        t_cv = V.tag(V.eng.tensor_copy(XB[:, 0:256], s8t[:, 0:256]))
        V.wait(t_cv)
        bi = p.next_pb()
        TE.waits([ld_id, p.pb_tok[bi], V.tok()])
        for kh in range(2):
            inst = TE.eng.transpose(p.pb[bi][:, kh * 128:(kh + 1) * 128],
                                    XB[:, kh * 128:(kh + 1) * 128], CID[:])
        t_tr = TE.tag(inst)
        V.wait(t_tr)
        V.wait(dA.tok())
        for kh in range(2):
            inst = V.eng.tensor_copy(XB[:, 512 + kh * 128:512 + (kh + 1) * 128],
                                     p.pb[bi][:, kh * 128:(kh + 1) * 128])
        p.pb_tok[bi] = V.tag(inst)
        S.wait(V.tok())
        t_st = dA.tag(S.eng.dma_start(
            out=rr8[:, :, mc * 128:(mc + 1) * 128],
            in_=XB[:, 512:768].rearrange("p (kh j) -> p kh j", j=128)))
        prep_store_toks.append(t_st)

    G.waits(prep_store_toks)
    tok_wblob = CC.tag(G.eng.collective_compute(
        "AllGather", OP.bypass, replica_groups=[list(range(8))],
        ins=[cc_w_in.opt()],
        outs=[cc_w_out.opt()]))
    tok_prep_v = V.tok()
    tok_prep_dS = dS.tok()

    GBG, GBB, TMP1, TMP2, BOUT, BKV, P1S = 0, 2048, 4096, 6144, 8192, 10240, 14336

    dG.wait(tok_prep_v)
    dG.wait(tok_prep_dS)
    dS.wait(tok_prep_v)
    V.wait(tok_prep_dS)
    TE.wait(tok_prep_dS)
    t_g = dG.tag(G.eng.dma_start(out=SCR[0:1, GBG:GBG + D], in_=gamma))
    t_b = dG.tag(G.eng.dma_start(out=SCR[0:1, GBB:GBB + D], in_=beta))
    V.waits([t_g, t_b])
    p.bcast_row(SCR[:, GBG:GBG + D], D)
    tok_gb = p.bcast_row(SCR[:, GBB:GBB + D], D)

    def ln_cols(src_ap, c_m, c_r):
        V.wait(tok_eps)
        for j in range(4):
            t1 = V.tag(V.eng.bn_stats(COL[:, C_STAT + 6 * j:C_STAT + 6 * j + 6],
                                      src_ap[:, 512 * j:512 * (j + 1)]))
        V.wait(t1)
        t2 = V.tag(V.eng.bn_aggr(COL[:, c_m:c_m + 2], COL[:, C_STAT:C_STAT + 24]))
        S.wait(t2)
        t3 = S.tag(S.eng.activation(COL[:, c_r:c_r + 1], COL[:, c_m + 1:c_m + 2],
                                    AF.Ln, bias=COL[:, C_EPS:C_EPS + 1], scale=1.0))
        S.wait(t3)
        return S.tag(S.eng.activation(COL[:, c_r:c_r + 1], COL[:, c_r:c_r + 1],
                                      AF.Exp, scale=-0.5))

    for tc in range(NTC):
        rs = RES[:, tc * D:(tc + 1) * D]
        t_ld = dG.tag(G.eng.dma_start(out=rs, in_=x_in[tc * 128:(tc + 1) * 128, :]))
        V.wait(t_ld)
        t_r1 = ln_cols(rs, C_M1, C_R1)
        xn1o = SCR[:, TMP1:TMP1 + D]
        V.wait(t_r1)
        t_x1o = V.tag(V.eng.tensor_scalar(xn1o, rs, COL[:, C_M1:C_M1 + 1],
                                          COL[:, C_R1:C_R1 + 1], OP.subtract, OP.mult))
        V.wait(t_x1o)
        t_r2 = ln_cols(xn1o, C_M2, C_R2)
        xnp = SCR[:, TMP2:TMP2 + D]
        V.wait(t_r2)
        t_xnp = V.tag(V.eng.tensor_scalar(xnp, xn1o, COL[:, C_M2:C_M2 + 1],
                                          COL[:, C_R2:C_R2 + 1], OP.subtract, OP.mult))
        V.waits([t_xnp, tok_gb])
        t_mg = V.tag(V.eng.tensor_tensor(xnp, xnp, SCR[:, GBG:GBG + D], OP.mult))
        V.wait(t_mg)
        t_xn = V.tag(V.eng.tensor_tensor(xnp, xnp, SCR[:, GBB:GBB + D], OP.add))
        t_xp = V.tag(V.eng.tensor_tensor(rs, rs, xn1o, OP.add))
        TE.waits([t_xn, ld_id])
        for half in range(2):
            bi = p.next_pb()
            TE.wait(p.pb_tok[bi])
            for j in range(8):
                kc = half * 8 + j
                inst = TE.eng.transpose(p.pb[bi][:, j * 128:(j + 1) * 128],
                                        xnp[:, kc * 128:(kc + 1) * 128], CID[:])
            t_tr = TE.tag(inst)
            V.wait(t_tr)
            for j in range(8):
                kc = half * 8 + j
                inst = V.eng.tensor_copy(
                    XNF[:, kc * T + tc * 128: kc * T + (tc + 1) * 128],
                    p.pb[bi][:, j * 128:(j + 1) * 128])
            p.pb_tok[bi] = V.tag(inst)

    tok_xnf = V.tok()
    tok_res0 = V.tok()

    if stage < 1:
        tok_res1 = tok_res0
        fin = ctx.enter_context(nc.semaphore("fin"))
        G.wait(tok_res1)
        n = 0
        for tc in range(NTC):
            G.eng.dma_start(out=out[tc * 128:(tc + 1) * 128, :],
                            in_=RES[:, tc * D:(tc + 1) * D]).then_inc(fin, 16)
            n += 16
        G.eng.wait_ge(fin, n)
        ctx.close()
        return nc

    t_bo = dG.tag(G.eng.dma_start(out=SCR[0:1, BOUT:BOUT + D], in_=b_out))
    V.wait(t_bo)
    p.bcast_row(SCR[:, BOUT:BOUT + D], D)
    t_bkv = dG.tag(G.eng.dma_start(out=SCR[0:1, BKV:BKV + 4096],
                                    in_=b_in[0:1, D:3 * D]))
    V.wait(t_bkv)
    tok_bias = p.bcast_row(SCR[:, BKV:BKV + 4096], 4096)

    KST, VST = P1S, P1S + 2048
    ROTC, ROTS = P1S + 4096, P1S + 5120
    t_z = V.tag(V.eng.memset(SC2[:, 0:D], 0.0))
    V.wait(t_z)

    for tc in range(NTC):
        for half, base in ((0, KST), (1, VST)):
            for mt in range(4):
                bi = p.next_pf()
                pk = p.pf[bi]
                col0 = (1 + half) * D + mt * 512
                t_mm = p.stream_mm(
                    pk[:], NKC,
                    lambda kc: XNF[:, kc * T + tc * 128:kc * T + (tc + 1) * 128],
                    lambda kc, c0=col0: w_in[kc * 128:(kc + 1) * 128, c0:c0 + 512],
                    ws_a, stream_is_lhsT=False,
                    pre_toks=[tok_xnf, p.pf_tok[bi]])
                V.waits([t_mm, tok_bias])
                p.pf_tok[bi] = V.tag(V.eng.tensor_tensor(
                    SCR[:, base + mt * 512:base + (mt + 1) * 512], pk[:],
                    SCR[:, BKV + half * D + mt * 512:BKV + half * D + (mt + 1) * 512],
                    OP.add))
        dS.wait(V.tok())
        t_c = dS.tag(SY.eng.dma_start(out=SCR[:, ROTC:ROTC + 1024],
                                      in_=cosd[tc * 128:(tc + 1) * 128, :]))
        t_s = dS.tag(SY.eng.dma_start(out=SCR[:, ROTS:ROTS + 1024],
                                      in_=sind[tc * 128:(tc + 1) * 128, :]))
        for base, c_ss in ((KST, C_SSK), (VST, C_SSV)):
            kv = SCR[:, base:base + D]
            rot = SCR[:, base:base + 1024]
            S.wait(V.tok())
            t_sq = S.tag(S.eng.activation(SCR[:, TMP1:TMP1 + D], kv, AF.Square,
                                          accum_out=COL[:, c_ss:c_ss + 1]))
            V.waits([t_c, t_s, t_sq])
            tmp = SCR[:, TMP1:TMP1 + 1024]
            cosb = SCR[:, ROTC:ROTC + 1024]
            sinb = SCR[:, ROTS:ROTS + 1024]
            ev_o = SCR[:, TMP2:TMP2 + 1024]
            V.tag(V.eng.tensor_tensor(tmp, rot, cosb, OP.mult))
            V.tag(V.eng.tensor_tensor(ev_o[:, 0:1024:2], rot[:, 1:1024:2],
                                      sinb[:, 0:1024:2], OP.mult))
            t_w = V.tag(V.eng.tensor_tensor(ev_o[:, 1:1024:2], rot[:, 0:1024:2],
                                            sinb[:, 1:1024:2], OP.mult))
            V.wait(t_w)
            V.tag(V.eng.tensor_tensor(tmp[:, 0:1024:2], tmp[:, 0:1024:2],
                                      ev_o[:, 0:1024:2], OP.subtract))
            t_a = V.tag(V.eng.tensor_tensor(tmp[:, 1:1024:2], tmp[:, 1:1024:2],
                                            ev_o[:, 1:1024:2], OP.add))
            V.wait(t_a)
            V.wait(V.tok())
            t_cp = V.tag(V.eng.tensor_copy(rot, tmp))
            V.wait(t_cp)
        S.wait(V.tok())
        S.tag(S.eng.activation(COL[:, C_RNK:C_RNK + 1], COL[:, C_SSK:C_SSK + 1], AF.Ln))
        t_lv = S.tag(S.eng.activation(COL[:, C_RNV:C_RNV + 1], COL[:, C_SSV:C_SSV + 1],
                                      AF.Ln))
        S.wait(t_lv)
        S.tag(S.eng.activation(COL[:, C_RNK:C_RNK + 1], COL[:, C_RNK:C_RNK + 1],
                               AF.Exp, scale=-0.5))
        t_ev = S.tag(S.eng.activation(COL[:, C_RNV:C_RNV + 1], COL[:, C_RNV:C_RNV + 1],
                                      AF.Exp, scale=-0.5))
        V.wait(t_ev)
        t_rn = V.tag(V.eng.tensor_tensor(COL[:, C_RN:C_RN + 1], COL[:, C_RNK:C_RNK + 1],
                                         COL[:, C_RNV:C_RNV + 1], OP.mult))
        V.wait(t_rn)
        t_p = V.tag(V.eng.tensor_tensor(SCR[:, TMP1:TMP1 + D], SCR[:, KST:KST + D],
                                        SCR[:, VST:VST + D], OP.mult))
        V.wait(t_p)
        V.tag(V.eng.scalar_tensor_tensor(SC2[:, 0:D], SCR[:, TMP1:TMP1 + D],
                                         COL[:, C_RN:C_RN + 1], SC2[:, 0:D],
                                         OP.mult, OP.add))
        if stage == 3 and tc == 0:
            fin = ctx.enter_context(nc.semaphore("fin"))
            G.wait(V.tok())
            n = 0
            G.eng.dma_start(out=out[0:128, :], in_=SCR[:, KST:KST + D]).then_inc(fin, 16)
            G.eng.dma_start(out=out[128:256, :], in_=SCR[:, VST:VST + D]).then_inc(fin, 16)
            G.eng.dma_start(out=out[256:384, 0:8], in_=COL[:, C_SSK:C_SSK + 8]).then_inc(fin, 16)
            n = 48
            G.eng.wait_ge(fin, n)
            ctx.close()
            return nc

    tok_apart = V.tok()
    for j in range(4):
        bi = p.next_pf()
        pk = p.pf[bi]
        TE.waits([tok_apart, p.pf_tok[bi], tok_eps])
        t_mm = TE.tag(TE.eng.matmul(pk[0:1, 0:512], COL[:, C_ONE:C_ONE + 1],
                                    SC2[:, j * 512:(j + 1) * 512],
                                    start=True, stop=True))
        V.wait(t_mm)
        p.pf_tok[bi] = V.tag(V.eng.tensor_copy(
            SC2[0:1, 2048 + j * 512:2048 + (j + 1) * 512], pk[0:1, 0:512]))

    t_asum = V.tok()
    SY.wait(t_asum)
    t_aw = dS.tag(SY.eng.dma_start(out=cc_a_in, in_=SC2[0:1, 2048:2048 + D]))
    G.wait(t_aw)
    t_cc = CC.tag(G.eng.collective_compute(
        "AllReduce", OP.add, replica_groups=[[0, 1], [2, 3], [4, 5], [6, 7]],
        ins=[cc_a_in.opt()], outs=[cc_a_out.opt()]))
    G.wait(t_cc)
    t_al = dG.tag(G.eng.dma_start(out=SC2[0:1, 0:D], in_=cc_a_out))
    V.wait(t_al)
    t_af = V.tag(V.eng.tensor_scalar(SC2[0:1, 0:D], SC2[0:1, 0:D], 0.5, 0.5,
                                     OP.mult, OP.add))
    SY.wait(t_af)
    t_aw2 = dS.tag(SY.eng.dma_start(out=cc_a_in, in_=SC2[0:1, 0:D]))
    dS.wait(t_aw2)
    t_ac = dS.tag(SY.eng.dma_start(
        out=COL[:, C_ACOL:C_ACOL + 16],
        in_=cc_a_in.rearrange("o (kc p) -> (o p) kc", p=128),
        allow_slow_non_contiguous=True))

    if stage == 2:
        fin = ctx.enter_context(nc.semaphore("fin"))
        SY.wait(t_ac)
        SY.wait((t_af[0], t_af[1]))
        n = 0
        SY.eng.dma_start(out=out[0:1, :], in_=SC2[0:1, 0:D]).then_inc(fin, 16)
        n += 16
        SY.eng.dma_start(out=out[4:132, 0:16], in_=COL[:, C_ACOL:C_ACOL + 16]).then_inc(fin, 16)
        n += 16
        SY.eng.dma_start(out=out[256:257, :], in_=SC2[0:1, 2048:2048 + D]).then_inc(fin, 16)
        n += 16
        SY.eng.wait_ge(fin, n)
        ctx.close()
        return nc

    AQF = P1S
    for tt_i in range(4):
        t0 = tt_i * 512
        for mc in range(NKC):
            bi = p.next_pf()
            pk = p.pf[bi]
            t_mm = p.stream_mm(
                pk[:, 0:512], NKC,
                lambda kc: XNF[:, kc * T + t0:kc * T + t0 + 512],
                lambda kc, m=mc: w_in[kc * 128:(kc + 1) * 128, m * 128:(m + 1) * 128],
                ws_q, stream_is_lhsT=True,
                pre_toks=[tok_xnf, p.pf_tok[bi]])
            V.waits([t_mm, t_ac, ld_bq])
            p.pf_tok[bi] = V.tag(V.eng.tensor_scalar(
                SCR[:, AQF + mc * 512:AQF + (mc + 1) * 512], pk[:, 0:512],
                COL[:, C_BQ + mc:C_BQ + mc + 1], COL[:, C_ACOL + mc:C_ACOL + mc + 1],
                OP.add, OP.mult))
        tok_aq = V.tok()
        for stc in range(4):
            tc = tt_i * 4 + stc
            for ot in range(4):
                bi = p.next_pf()
                pk = p.pf[bi]
                t_mm = p.stream_mm(
                    pk[:], NKC,
                    lambda kc, s=stc: SCR[:, AQF + kc * 512 + s * 128:
                                          AQF + kc * 512 + (s + 1) * 128],
                    lambda kc, o=ot: w_outw[kc * 128:(kc + 1) * 128,
                                            o * 512:(o + 1) * 512],
                    ws_b, stream_is_lhsT=False,
                    pre_toks=[tok_aq, p.pf_tok[bi]])
                rsl = RES[:, tc * D + ot * 512:tc * D + (ot + 1) * 512]
                V.waits([t_mm, tok_res0, V.tok()])
                t1 = V.tag(V.eng.tensor_tensor(SCR[:, TMP1:TMP1 + 512], pk[:],
                                               SCR[:, BOUT + ot * 512:BOUT + (ot + 1) * 512],
                                               OP.add))
                V.wait(t1)
                t2 = V.tag(V.eng.tensor_tensor(rsl, rsl, SCR[:, TMP1:TMP1 + 512],
                                               OP.add))
                p.pf_tok[bi] = t2

    tok_res1 = V.tok()

    if stage == 1:
        fin = ctx.enter_context(nc.semaphore("fin"))
        G.wait(tok_res1)
        n = 0
        for tc in range(NTC):
            G.eng.dma_start(out=out[tc * 128:(tc + 1) * 128, :],
                            in_=RES[:, tc * D:(tc + 1) * D]).then_inc(fin, 16)
            n += 16
        G.eng.wait_ge(fin, n)
        ctx.close()
        return nc

    RW1 = [cc_w_out[r:r + 1, O_W1:O_W1 + D * W1R].rearrange(
        "o (kc pp m) -> (o pp) kc m", pp=128, m=W1R) for r in range(8)]
    RW2 = [cc_w_out[r:r + 1, O_W2:O_W2 + I_FFN * W2R].rearrange(
        "o (ic pp m) -> (o pp) ic m", pp=128, m=W2R) for r in range(8)]

    def sf32(c0, w):
        return SCR[:, c0:c0 + 2 * w].bitcast(F32)

    def si32v(c0, w):
        return SCR[:, c0:c0 + 2 * w].bitcast(I32)


    V.wait(TE.tok())
    S.wait(TE.tok())

    for tc in range(NTC):
        rs_ = RES[:, tc * D:(tc + 1) * D]
        t_r = ln_cols(rs_, C_M1, C_R1)
        xn2f = sf32(0, 2048)
        V.wait(t_r)
        t_x = V.tag(V.eng.tensor_scalar(xn2f, rs_, COL[:, C_M1:C_M1 + 1],
                                        COL[:, C_R1:C_R1 + 1], OP.subtract, OP.mult))
        V.wait(t_x)
        quant_chain(xn2f, si32v(4096, 2048), sf32(8192, 2048), sf32(12288, 2048),
                    SCR[:, 16384:18432], 2048)
        if stage == 6 and tc == 0:
            fin = ctx.enter_context(nc.semaphore("fin"))
            G.wait(V.tok())
            G.eng.dma_start(out=out[0:128, :], in_=SCR[:, 16384:18432]).then_inc(fin, 16)
            G.eng.dma_start(out=out[128:256, :], in_=xn2f).then_inc(fin, 16)
            G.eng.dma_start(out=out[256:384, 0:80], in_=COL[:]).then_inc(fin, 16)
            G.eng.wait_ge(fin, 48)
            ctx.close()
            return nc
        for half in range(2):
            bi = p.next_pb()
            TE.waits([V.tok(), p.pb_tok[bi]])
            for j in range(8):
                kc = half * 8 + j
                inst = TE.eng.transpose(p.pb[bi][:, j * 128:(j + 1) * 128],
                                        SCR[:, 16384 + kc * 128:16384 + (kc + 1) * 128],
                                        CID[:])
            t_tr = TE.tag(inst)
            V.wait(t_tr)
            for j in range(8):
                kc = half * 8 + j
                inst = V.eng.tensor_copy(
                    XNF[:, kc * T + tc * 128:kc * T + (tc + 1) * 128],
                    p.pb[bi][:, j * 128:(j + 1) * 128])
            p.pb_tok[bi] = V.tag(inst)
    tok_xqf = V.tok()

    SY.wait(tok_xqf)
    for tc in range(NTC):
        t_sp = dS.tag(SY.eng.dma_start(out=res_spill[tc * 128:(tc + 1) * 128, :],
                                       in_=RES[:, tc * D:(tc + 1) * D]))
    tok_spill = t_sp

    if stage == 5:
        fin = ctx.enter_context(nc.semaphore("fin"))
        G.wait(tok_xqf)
        G.wait(CC.tok())
        n = 0
        G.eng.dma_start(out=out[0:128, :], in_=XNF[:, 0:T]).then_inc(fin, 16)
        n += 16
        t_bl = dG.tag(G.eng.dma_start(
            out=SCR[:, 0:1408],
            in_=cc_w_out[0:1, O_W1:O_W1 + 128 * W1R].rearrange(
                "o (pp m) -> (o pp) m", pp=128)))
        G.wait(t_bl)
        G.eng.dma_start(out=out[128:256, 0:1408], in_=SCR[:, 0:1408]).then_inc(fin, 16)
        n += 16
        t_b2 = dG.tag(G.eng.dma_start(
            out=SCR[:, 2048:2304],
            in_=cc_w_out[3:4, O_W2:O_W2 + 128 * W2R].rearrange(
                "o (pp m) -> (o pp) m", pp=128)))
        G.wait(t_b2)
        G.eng.dma_start(out=out[256:384, 0:256], in_=SCR[:, 2048:2304]).then_inc(fin, 16)
        n += 16
        G.eng.wait_ge(fin, n)
        ctx.close()
        return nc

    HT = RES

    def htf32(c0, w):
        return HT[:, c0:c0 + 2 * w].bitcast(F32)

    def hti32(c0, w):
        return HT[:, c0:c0 + 2 * w].bitcast(I32)

    w1slab = [WSlot(WB[:, 0:2048]), WSlot(WB[:, 2048:4096])]
    slab3 = lambda sl: sl.ap.rearrange("p (kc m) -> p kc m", m=512)

    def mm4_w1(bks, tt, m0, nk=16):
        for b in range(4):
            TE.wait(p.pf_tok[bks[b]])
        last = None
        for g in range(nk // 4):
            sl = w1slab[g % 2]
            dS.wait(sl.war)
            spans = []
            ma = m0
            while ma < m0 + 512:
                r = ma // W1R
                mb = min(m0 + 512, (r + 1) * W1R)
                spans.append((r, ma, mb))
                ma = mb
            toks = []
            for (r, ma, mb) in spans:
                toks.append(p.slot_dma(
                    dS, sl, slab3(sl)[:, :, ma - m0:mb - m0],
                    RW1[r][:, 4 * g:4 * g + 4, ma - r * W1R:mb - r * W1R]))
            TE.waits(toks)
            for j in range(4):
                kc = 4 * g + j
                for b in range(4):
                    tc = tt * 4 + b
                    inst = TE.eng.matmul(
                        p.pf[bks[b]][:],
                        XNF[:, kc * T + tc * 128:kc * T + (tc + 1) * 128],
                        slab3(sl)[:, j, :],
                        start=(kc == 0), stop=(kc == nk - 1))
                last = TE.tag(inst)
            sl.war = last
        return last

    sg_war = [None] * 4
    for tt in range(4):
        for mtp in range(11):
            bks = [p.next_pf() for _ in range(4)]
            t_mm = mm4_w1(bks, tt, mtp * 512)
            S.wait(t_mm)
            for b in range(4):
                S.waits([p.pf_tok[bks[b]], tok_spill, sg_war[b]])
                t_sg = S.tag(S.eng.activation(
                    HT[:, 22528 + b * 512:22528 + (b + 1) * 512],
                    p.pf[bks[b]][:], AF.Silu))
                p.pf_tok[bks[b]] = t_sg
            t_mm = mm4_w1(bks, tt, I_FFN + mtp * 512)
            V.wait(t_mm)
            for b in range(4):
                V.waits([p.pf_tok[bks[b]], tok_spill])
                t_h = V.tag(V.eng.tensor_tensor(
                    HT[:, b * I_FFN + mtp * 512:b * I_FFN + (mtp + 1) * 512],
                    HT[:, 22528 + b * 512:22528 + (b + 1) * 512],
                    p.pf[bks[b]][:], OP.mult))
                p.pf_tok[bks[b]] = t_h
                sg_war[b] = t_h
        V.wait(TE.tok())
        V.wait(V.tok())
        for b in range(4):
            hrow = HT[:, b * I_FFN:(b + 1) * I_FFN]
            t1 = V.tag(V.eng.tensor_reduce(COL[:, 76:77], hrow, AX.X,
                                           OP.max, apply_absolute_value=True))
            V.wait(t1)
            t2 = V.tag(V.eng.tensor_scalar(COL[:, 77:78], COL[:, 76:77],
                                           1.0 / 127.0, 1e-8, OP.mult, OP.max))
            V.wait(t2)
            t3 = V.tag(V.eng.reciprocal(COL[:, 78:79], COL[:, 77:78]))
            t4 = V.tag(V.eng.tensor_scalar(COL[:, 79:80], COL[:, 77:78],
                                           -256.0, None, OP.mult))
            V.waits([t3, t4])
            for pc in range(6):
                w0 = pc * 1024
                w = min(1024, I_FFN - w0)
                hpc = hrow[:, w0:w0 + w]
                qi = hti32(24576, w)
                rf = htf32(26624, w)
                qf = htf32(28672, w)
                dq = HT[:, 30720:30720 + w]
                t5 = V.tag(V.eng.tensor_scalar(qi, hpc, COL[:, 78:79], 256.0,
                                               OP.mult, OP.add))
                V.wait(t5)
                t6 = V.tag(V.eng.scalar_tensor_tensor(rf, hpc, COL[:, 78:79],
                                                      qi, OP.mult, OP.subtract))
                V.wait(t6)
                t7 = V.tag(V.eng.scalar_tensor_tensor(qf, rf, -255.5, qi,
                                                      OP.is_ge, OP.add))
                V.wait(t7)
                t8 = V.tag(V.eng.tensor_scalar(qf, qf, 129.0, 383.0,
                                               OP.max, OP.min))
                V.wait(t8)
                t9 = V.tag(V.eng.tensor_scalar(dq, qf, COL[:, 77:78],
                                               COL[:, 79:80], OP.mult, OP.add))
                nkt = w // 128
                bi = p.next_pb()
                TE.waits([t9, p.pb_tok[bi]])
                for j in range(nkt):
                    inst = TE.eng.transpose(p.pb[bi][:, j * 128:(j + 1) * 128],
                                            dq[:, j * 128:(j + 1) * 128], CID[:])
                t_tr = TE.tag(inst)
                V.wait(t_tr)
                for j in range(nkt):
                    ic = pc * 8 + j
                    inst = V.eng.tensor_copy(
                        SCR[:, ic * 512 + b * 128:ic * 512 + (b + 1) * 128],
                        p.pb[bi][:, j * 128:(j + 1) * 128])
                p.pb_tok[bi] = V.tag(inst)
        tok_hq = V.tok()

        SY.wait(V.tok())
        SY.wait(S.tok())
        x1toks = []
        for b in range(4):
            tc = tt * 4 + b
            x1toks.append(dS.tag(SY.eng.dma_start(
                out=HT[:, 22528 + b * 2048:22528 + (b + 1) * 2048],
                in_=res_spill[tc * 128:(tc + 1) * 128, :])))

        w2slab = w1slab
        for ot in range(4):
            o0 = ot * 512
            bks = [p.next_pf() for _ in range(4)]
            last = None
            for g in range(11):
                sl = w2slab[g % 2]
                dS.wait(sl.war)
                toks = []
                for half in range(2):
                    r = (o0 + half * 256) // W2R
                    toks.append(p.slot_dma(
                        dS, sl, slab3(sl)[:, :, half * 256:(half + 1) * 256],
                        RW2[r][:, 4 * g:4 * g + 4, :]))
                TE.waits(toks)
                TE.wait(tok_hq)
                for b in range(4):
                    TE.wait(p.pf_tok[bks[b]])
                for j in range(4):
                    ic = 4 * g + j
                    for b in range(4):
                        inst = TE.eng.matmul(
                            p.pf[bks[b]][:],
                            SCR[:, ic * 512 + b * 128:ic * 512 + (b + 1) * 128],
                            slab3(sl)[:, j, :],
                            start=(ic == 0), stop=(ic == 43))
                    last = TE.tag(inst)
                sl.war = last
            for b in range(4):
                V.waits([last, x1toks[b]])
                xb_ = HT[:, 22528 + b * 2048 + o0:22528 + b * 2048 + o0 + 512]
                t_e = V.tag(V.eng.tensor_tensor(xb_, xb_, p.pf[bks[b]][:], OP.add))
                p.pf_tok[bks[b]] = t_e
        SY.wait(V.tok())
        for b in range(4):
            tc = tt * 4 + b
            t_x2 = dS.tag(SY.eng.dma_start(
                out=x2_d[tc * 128:(tc + 1) * 128, :],
                in_=HT[:, 22528 + b * 2048:22528 + (b + 1) * 2048]))
    tok_x2 = t_x2


    if stage == 4:
        fin = ctx.enter_context(nc.semaphore("fin"))
        G.wait(tok_x2)
        n = 0
        for tc in range(NTC):
            G.eng.dma_start(out=HT[:, 0:D],
                            in_=x2_d[tc * 128:(tc + 1) * 128, :]).then_inc(fin, 16)
            n += 16
            G.eng.wait_ge(fin, n)
            G.eng.dma_start(out=out[tc * 128:(tc + 1) * 128, :],
                            in_=HT[:, 0:D]).then_inc(fin, 16)
            n += 16
            G.eng.wait_ge(fin, n)
        ctx.close()
        return nc

    RIP = [cc_w_out[r:r + 1, O_IP:O_IP + IPC * MPROJ].rearrange(
        "o (kh pp m) -> (o pp) kh m", pp=128, m=MPROJ) for r in range(8)]
    ROP = [cc_w_out[r:r + 1, O_OP:O_OP + OPC * D].rearrange(
        "o (kh pp m) -> (o pp) kh m", pp=128, m=D) for r in range(8)]
    RW8 = [cc_w_out[r:r + 1, O_W8:O_W8 + W8C * D].rearrange(
        "o (kh pp m) -> (o pp) kh m", pp=128, m=D) for r in range(8)]

    C_EM, C_HS = 68, 69
    t_mk = dS.tag(SY.eng.dma_start(out=COL[0:1, C_EM:C_EM + 2], in_=masks))
    V.wait(t_mk)
    V.eng.tensor_copy(COL[32:33, C_EM:C_EM + 2], COL[0:1, C_EM:C_EM + 2])
    V.eng.tensor_copy(COL[64:65, C_EM:C_EM + 2], COL[0:1, C_EM:C_EM + 2])
    t_c3 = V.tag(V.eng.tensor_copy(COL[96:97, C_EM:C_EM + 2], COL[0:1, C_EM:C_EM + 2]))
    V.wait(t_c3)
    tok_mk = V.tag(V.eng.stream_shuffle(COL[:, C_EM:C_EM + 2],
                                        COL[:, C_EM:C_EM + 2], [0] * 32))

    V.wait(TE.tok())
    for tc in range(NTC):
        t_ld = dS.tag(SY.eng.dma_start(out=SCR[:, 0:2048],
                                       in_=x2_d[tc * 128:(tc + 1) * 128, :]))
        V.wait(t_ld)
        t_r = ln_cols(SCR[:, 0:2048], C_M1, C_R1)
        V.wait(t_r)
        t_x = V.tag(V.eng.tensor_scalar(SCR[:, 2048:4096], SCR[:, 0:2048],
                                        COL[:, C_M1:C_M1 + 1], COL[:, C_R1:C_R1 + 1],
                                        OP.subtract, OP.mult))
        for half in range(2):
            bi = p.next_pb()
            TE.waits([t_x, p.pb_tok[bi]])
            for j in range(8):
                kc = half * 8 + j
                inst = TE.eng.transpose(p.pb[bi][:, j * 128:(j + 1) * 128],
                                        SCR[:, 2048 + kc * 128:2048 + (kc + 1) * 128],
                                        CID[:])
            t_tr = TE.tag(inst)
            V.wait(t_tr)
            for j in range(8):
                kc = half * 8 + j
                inst = V.eng.tensor_copy(
                    XNF[:, kc * T + tc * 128:kc * T + (tc + 1) * 128],
                    p.pb[bi][:, j * 128:(j + 1) * 128])
            p.pb_tok[bi] = V.tag(inst)
        dS.wait(V.tok())
    tok_xn3 = V.tok()

    ipslab = [WSlot(WB[:, 0:512]), WSlot(WB[:, 512:1024])]

    def ipt_slab(g, m0, w):
        sl = ipslab[g % 2]
        dS.wait(sl.war)
        toks = []
        for rr in range(2):
            kc0 = 4 * g + 2 * rr
            r = kc0 // 2
            toks.append(p.slot_dma(
                dS, sl,
                sl.ap[:, 0:4 * w].rearrange("p (kc m) -> p kc m", m=w)[
                    :, 2 * rr:2 * rr + 2, :],
                RIP[r][:, :, m0:m0 + w]))
        return sl, toks

    HSTG = 1024
    for mc in range(35):
        rows = 128 if mc < 34 else 64
        m0 = DIN + mc * 128
        bks = [p.next_pf() for _ in range(4)]
        for b in range(4):
            TE.wait(p.pf_tok[bks[b]])
        last = None
        for g in range(4):
            sl, toks = ipt_slab(g, m0, rows)
            TE.waits(toks + [tok_xn3])
            for j in range(4):
                kc = 4 * g + j
                for b in range(4):
                    inst = TE.eng.matmul(
                        p.pf[bks[b]][0:rows, :],
                        sl.ap[:, 0:4 * rows].rearrange("p (kc m) -> p kc m",
                                                       m=rows)[:, j, :],
                        XNF[:, kc * T + b * 512:kc * T + (b + 1) * 512],
                        start=(kc == 0), stop=(kc == 15))
                last = TE.tag(inst)
            sl.war = last
        if mc < 34:
            for b in range(4):
                V.waits([last, p.pf_tok[bks[b]]])
                t_e = V.tag(V.eng.tensor_copy(SCR[:, 0:512], p.pf[bks[b]][:]))
                if b == 3:
                    V.wait(t_e)
                    t_h = V.tag(V.eng.tensor_copy(
                        SCR[:, HSTG + mc * 3:HSTG + (mc + 1) * 3],
                        SCR[:, 509:512]))
                    t_e = t_h
                SY.wait(t_e)
                t_st = dS.tag(SY.eng.dma_start(
                    out=xbcraw_d[mc * 128:(mc + 1) * 128, b * 512:(b + 1) * 512],
                    in_=SCR[:, 0:512]))
                dS.wait(V.tok())
                p.pf_tok[bks[b]] = t_e
                V.wait(t_st)
        else:
            for b in range(4):
                V.waits([last, p.pf_tok[bks[b]]])
                t_e = V.tag(V.eng.tensor_copy(
                    SC2[0:64, b * 512:(b + 1) * 512], p.pf[bks[b]][0:64, :]))
                p.pf_tok[bks[b]] = t_e
    tok_xbc = dS.tok()

    V.waits([tok_mk])
    t_hm = V.tag(V.eng.tensor_scalar(SC2[0:128, 2048:2150], SCR[:, HSTG:HSTG + 102],
                                     COL[:, C_EM:C_EM + 1], None, OP.mult))
    SY.wait(t_hm)
    t_hw = dS.tag(SY.eng.dma_start(out=cc_h_in, in_=SC2[:, 2048:2150]))
    G.wait(t_hw)
    t_hc = CC.tag(G.eng.collective_compute(
        "AllReduce", OP.add, replica_groups=[[0, 1], [2, 3], [4, 5], [6, 7]],
        ins=[cc_h_in.opt()], outs=[cc_h_out.opt()]))
    G.wait(t_hc)
    t_hl = dG.tag(G.eng.dma_start(out=SC2[:, 2048:2150], in_=cc_h_out))
    V.wait(t_hl)
    t_hs = V.tag(V.eng.tensor_scalar(SCR[:, HSTG:HSTG + 102], SC2[:, 2048:2150],
                                     COL[:, C_HS:C_HS + 1], None, OP.mult))

    if stage == 61:
        fin = ctx.enter_context(nc.semaphore("fin"))
        G.waits([tok_xbc, V.tok()])
        n = 0
        for i in range(16):
            G.eng.dma_start(out=SCR[:, 0:2048],
                            in_=xbcraw_d[i * 128:(i + 1) * 128, 0:2048]).then_inc(fin, 16)
            n += 16
            G.eng.wait_ge(fin, n)
            G.eng.dma_start(out=out[i * 128:(i + 1) * 128, 0:2048],
                            in_=SCR[:, 0:2048]).then_inc(fin, 16)
            n += 16
            G.eng.wait_ge(fin, n)
        G.eng.dma_start(out=out[0:64, 0:2048],
                        in_=SC2[0:64, 0:2048]).then_inc(fin, 16)
        n += 16
        G.eng.wait_ge(fin, n)
        ctx.close()
        return nc

    CPAD = 1280
    CST = 14336
    BCM, CCM = 16384, 18432
    for cc34 in range(34):
        dS.wait(V.tok())
        t_ci = dS.tag(SY.eng.dma_start(
            out=SCR[:, CPAD + 3:CPAD + 3 + 2048],
            in_=xbcraw_d[cc34 * 128:(cc34 + 1) * 128, :]))
        dA.wait(S.tok())
        dA.wait(V.tok())
        t_cwc = dA.tag(S.eng.dma_start(out=COL[:, 12:16],
                                       in_=conv_w[cc34 * 128:(cc34 + 1) * 128, :]))
        t_cbc = dA.tag(S.eng.dma_start(
            out=COL[:, 16:17],
            in_=conv_b[0:1, cc34 * 128:(cc34 + 1) * 128].rearrange("o p -> p o"),
            allow_slow_non_contiguous=True))
        V.waits([t_ci, t_cwc, t_hs])
        V.wait(S.tok())
        t_hc2 = V.tag(V.eng.tensor_copy(SCR[:, CPAD:CPAD + 3],
                                        SCR[:, HSTG + cc34 * 3:HSTG + (cc34 + 1) * 3]))
        V.wait(t_hc2)
        ACC = SC2[:, 2560:4608]
        t_a0 = V.tag(V.eng.tensor_scalar(ACC, SCR[:, CPAD:CPAD + 2048],
                                         COL[:, 12:13], None, OP.mult))
        V.wait(t_a0)
        for i in (1, 2, 3):
            t_ai = V.tag(V.eng.scalar_tensor_tensor(
                ACC, SCR[:, CPAD + i:CPAD + i + 2048], COL[:, 12 + i:13 + i], ACC,
                OP.mult, OP.add))
            V.wait(t_ai)
        S.waits([t_ai, t_cbc])
        if cc34 < 32:
            t_sl = S.tag(S.eng.activation(SCR[:, CST:CST + 2048], ACC, AF.Silu,
                                          bias=COL[:, 16:17]))
            SY.wait(t_sl)
            t_so = dS.tag(SY.eng.dma_start(
                out=xconv_d[cc34 * 128:(cc34 + 1) * 128, :], in_=SCR[:, CST:CST + 2048]))
            S.wait(t_so)
        elif cc34 == 32:
            t_sl = S.tag(S.eng.activation(SCR[:, BCM:BCM + 2048], ACC, AF.Silu,
                                          bias=COL[:, 16:17]))
        else:
            t_sl = S.tag(S.eng.activation(SCR[:, CCM:CCM + 2048], ACC, AF.Silu,
                                          bias=COL[:, 16:17]))
    tok_conv = S.tok()

    if stage == 615:
        fin = ctx.enter_context(nc.semaphore("fin"))
        G.waits([V.tok(), S.tok(), TE.tok(), dS.tok(), dA.tok(), dG.tok()])
        G.eng.dma_start(out=out[0:128, 0:128], in_=CID[:]).then_inc(fin, 16)
        G.eng.wait_ge(fin, 16)
        ctx.close()
        return nc

    t_al2 = dA.tag(S.eng.dma_start(out=COL[0:64, 5:6],
                                   in_=a_log.rearrange("o p -> p o"),
                                   allow_slow_non_contiguous=True))
    t_db = dA.tag(S.eng.dma_start(out=COL[0:64, 6:7],
                                  in_=dt_bias.rearrange("o p -> p o"),
                                  allow_slow_non_contiguous=True))
    t_dv = dA.tag(S.eng.dma_start(out=COL[0:64, 7:8],
                                  in_=d_vec.rearrange("o p -> p o"),
                                  allow_slow_non_contiguous=True))
    S.waits([t_al2, t_db, t_dv, V.tok()])
    t_am = S.tag(S.eng.activation(COL[0:64, 5:6], COL[0:64, 5:6], AF.Exp))
    V.wait(t_am)
    t_an = V.tag(V.eng.tensor_scalar(COL[0:64, 5:6], COL[0:64, 5:6], -1.0, None,
                                     OP.mult))
    DTF = 2560
    dtr = SC2[0:64, 0:2048]
    S.wait(V.tok())
    t_s1 = S.tag(S.eng.activation(SC2[0:64, DTF:DTF + 2048], dtr, AF.Abs,
                                  bias=COL[0:64, 6:7]))
    S.wait(t_s1)
    t_s2 = S.tag(S.eng.activation(SC2[0:64, DTF:DTF + 2048],
                                  SC2[0:64, DTF:DTF + 2048], AF.Exp, scale=-1.0))
    S.wait(t_s2)
    t_s3 = S.tag(S.eng.activation(SC2[0:64, DTF:DTF + 2048],
                                  SC2[0:64, DTF:DTF + 2048], AF.Ln,
                                  bias=COL[0:64, C_ONE:C_ONE + 1]))
    t_s4 = S.tag(S.eng.activation(SC2[0:64, 0:2048], dtr, AF.Relu,
                                  bias=COL[0:64, 6:7]))
    V.waits([t_s3, t_s4])
    t_dt = V.tag(V.eng.tensor_tensor(SC2[0:64, DTF:DTF + 2048],
                                     SC2[0:64, DTF:DTF + 2048],
                                     SC2[0:64, 0:2048], OP.add))
    ADTC = 4096
    V.waits([t_dt, t_an])
    t_adt = V.tag(V.eng.tensor_scalar(SCR[0:64, ADTC:ADTC + 2048],
                                      SC2[0:64, DTF:DTF + 2048],
                                      COL[0:64, 5:6], None, OP.mult))
    V.wait(t_adt)
    for c in range(NCH):
        t_sc = V.tag(V.eng.tensor_tensor_scan(
            SC2[0:64, c * 256:(c + 1) * 256],
            SCR[0:64, ADTC + c * 256:ADTC + (c + 1) * 256],
            SCR[0:64, ADTC + c * 256:ADTC + (c + 1) * 256],
            0.0, OP.add, OP.bypass))
    tok_acscm = V.tok()

    if stage == 616:
        fin = ctx.enter_context(nc.semaphore("fin"))
        G.waits([V.tok(), S.tok(), TE.tok(), dS.tok(), dA.tok(), dG.tok()])
        G.eng.dma_start(out=out[0:128, 0:128], in_=CID[:]).then_inc(fin, 16)
        G.eng.wait_ge(fin, 16)
        ctx.close()
        return nc

    ADTT = 6144
    DTTM = 0
    DTB = 8192
    S.wait(V.tok())
    t_dtb = S.tag(S.eng.activation(SCR[0:64, DTB:DTB + 2048],
                                   SC2[0:64, DTF:DTF + 2048], AF.Identity))
    V.wait(t_dtb)
    for tc in range(NTC):
        bi = p.next_pb()
        TE.waits([V.tok(), t_dtb, p.pb_tok[bi]])
        i1 = TE.eng.transpose(p.pb[bi][:, 0:64],
                              SCR[0:64, ADTC + tc * 128:ADTC + (tc + 1) * 128],
                              CID[0:64, 0:64])
        i2 = TE.eng.transpose(p.pb[bi][:, 64:128],
                              SCR[0:64, DTB + tc * 128:DTB + (tc + 1) * 128],
                              CID[0:64, 0:64])
        t_tr = TE.tag(i2)
        V.wait(t_tr)
        V.eng.tensor_copy(SCR[:, ADTT + tc * 64:ADTT + (tc + 1) * 64],
                          p.pb[bi][:, 0:64])
        inst = V.eng.tensor_copy(SCR[:, DTTM + tc * 64:DTTM + (tc + 1) * 64],
                                 p.pb[bi][:, 64:128])
        p.pb_tok[bi] = V.tag(inst)
    tok_dtt = V.tok()

    if stage == 617:
        fin = ctx.enter_context(nc.semaphore("fin"))
        G.waits([V.tok(), S.tok(), TE.tok(), dS.tok(), dA.tok(), dG.tok()])
        G.eng.dma_start(out=out[0:128, 0:128], in_=CID[:]).then_inc(fin, 16)
        G.eng.wait_ge(fin, 16)
        ctx.close()
        return nc

    if stage == 621:
        fin = ctx.enter_context(nc.semaphore("fin"))
        G.waits([V.tok(), S.tok(), TE.tok(), dS.tok(), dG.tok(), dA.tok()])
        G.eng.dma_start(out=out[0:128, 0:1024],
                        in_=SCR[:, 0:1024]).then_inc(fin, 16)
        G.eng.wait_ge(fin, 16)
        ctx.close()
        return nc

    ACSTM = 2048
    EXPA, DECAYC = 20480, 3072
    TRI_I = CTRI[:, 0:128]
    TRI_S = CTRI[:, 128:256]
    TRI_O = CTRI[:, 256:384]
    V.wait(tok_dtt)
    t_z2 = V.tag(V.eng.memset(DEX[:, 576:832], 0.0))
    for c in range(NCH):
        adt0 = SCR[:, ADTT + (2 * c) * 64:ADTT + (2 * c + 1) * 64]
        adt1 = SCR[:, ADTT + (2 * c + 1) * 64:ADTT + (2 * c + 2) * 64]
        for lh in range(2):
            tcb = 2 * c + lh
            bi = p.next_pf()
            pk = p.pf[bi]
            TE.waits([tok_dtt, p.pf_tok[bi]])
            if lh == 0:
                i1 = TE.eng.matmul(pk[:, 0:64], TRI_I, adt0, start=True, stop=True)
            else:
                TE.eng.matmul(pk[:, 0:64], TRI_O, adt0, start=True, stop=False)
                i1 = TE.eng.matmul(pk[:, 0:64], TRI_I, adt1, start=False, stop=True)
            if lh == 0:
                TE.eng.matmul(pk[:, 64:128], TRI_S, adt0, start=True, stop=False)
                i2 = TE.eng.matmul(pk[:, 64:128], TRI_O, adt1, start=False, stop=True)
            else:
                i2 = TE.eng.matmul(pk[:, 64:128], TRI_S, adt1, start=True, stop=True)
            t_mm = TE.tag(i2)
            V.wait(t_mm)
            t_ac = V.tag(V.eng.tensor_copy(
                SC2[:, ACSTM + tcb * 64:ACSTM + (tcb + 1) * 64], pk[:, 0:64]))
            S.waits([t_mm, t_ac])
            t_e1 = S.tag(S.eng.activation(
                SCR[:, EXPA + tcb * 64:EXPA + (tcb + 1) * 64], pk[:, 0:64], AF.Exp))
            t_e2 = S.tag(S.eng.activation(
                SCR[:, DECAYC + tcb * 64:DECAYC + (tcb + 1) * 64], pk[:, 64:128],
                AF.Exp))
            p.pf_tok[bi] = t_e2
            V.waits([t_e1, t_e2])
        if stage == 6171 and c == 0:
            fin = ctx.enter_context(nc.semaphore("fin"))
            G.waits([V.tok(), S.tok(), TE.tok(), dS.tok(), dA.tok(), dG.tok()])
            G.eng.dma_start(out=out[0:128, 0:128], in_=CID[:]).then_inc(fin, 16)
            G.eng.wait_ge(fin, 16)
            ctx.close()
            return nc
        bi = p.next_pf()
        pk = p.pf[bi]
        TE.wait(p.pf_tok[bi])
        TE.eng.matmul(pk[0:1, 0:64], TRI_O[:, 0:1], adt0, start=True, stop=False)
        i3 = TE.eng.matmul(pk[0:1, 0:64], TRI_O[:, 0:1], adt1, start=False, stop=True)
        t_mm = TE.tag(i3)
        S.wait(t_mm)
        t_db = S.tag(S.eng.activation(DEX[0:1, c * 64:(c + 1) * 64], pk[0:1, 0:64],
                                      AF.Exp))
        p.pf_tok[bi] = t_db
        V.wait(t_db)
        p.bcast_row(DEX[:, c * 64:(c + 1) * 64], 64)
        if stage == 6172 and c == 1:
            fin = ctx.enter_context(nc.semaphore("fin"))
            G.waits([V.tok(), S.tok(), TE.tok(), dS.tok(), dA.tok(), dG.tok()])
            G.eng.dma_start(out=out[0:128, 0:640], in_=DEX[:, 0:640]).then_inc(fin, 16)
            G.eng.dma_start(out=out[128:256, 0:1024],
                            in_=SC2[:, ACSTM:ACSTM + 1024]).then_inc(fin, 16)
            G.eng.wait_ge(fin, 32)
            ctx.close()
            return nc

    if stage == 618:
        fin = ctx.enter_context(nc.semaphore("fin"))
        G.waits([V.tok(), S.tok(), TE.tok(), dS.tok(), dG.tok(), dA.tok()])
        G.eng.dma_start(out=out[0:128, 0:640], in_=DEX[:, 0:640]).then_inc(fin, 16)
        G.eng.dma_start(out=out[128:256, 0:1024],
                        in_=SC2[:, ACSTM:ACSTM + 1024]).then_inc(fin, 16)
        G.eng.dma_start(out=out[256:384, 0:1024],
                        in_=SCR[:, DECAYC:DECAYC + 1024]).then_inc(fin, 16)
        G.eng.dma_start(out=out[384:512, 0:1024],
                        in_=SCR[:, EXPA:EXPA + 1024]).then_inc(fin, 16)
        G.eng.dma_start(out=out[512:640, 0:2048],
                        in_=SCR[:, ADTT:ADTT + 2048]).then_inc(fin, 16)
        G.eng.wait_ge(fin, 80)
        ctx.close()
        return nc

    if stage == 622:
        fin = ctx.enter_context(nc.semaphore("fin"))
        G.waits([V.tok(), S.tok(), TE.tok(), dS.tok(), dG.tok(), dA.tok()])
        G.eng.dma_start(out=out[0:128, 0:1024],
                        in_=SCR[:, 0:1024]).then_inc(fin, 16)
        G.eng.wait_ge(fin, 16)
        ctx.close()
        return nc

    t_dr = dG.tag(G.eng.dma_start(out=DEX[0:1, 512:576], in_=d_vec))
    V.wait(t_dr)
    tok_db = p.bcast_row(DEX[:, 512:576], 64)

    if stage == 62:
        fin = ctx.enter_context(nc.semaphore("fin"))
        G.waits([V.tok(), S.tok(), dS.tok(), dG.tok(), dA.tok(), TE.tok()])
        n = 0
        for i in range(8):
            G.eng.dma_start(out=SCR[:, 0:2048],
                            in_=xconv_d[i * 128:(i + 1) * 128, :]).then_inc(fin, 16)
            n += 16
            G.eng.wait_ge(fin, n)
            G.eng.dma_start(out=out[i * 128:(i + 1) * 128, :],
                            in_=SCR[:, 0:2048]).then_inc(fin, 16)
            n += 16
            G.eng.wait_ge(fin, n)
        G.eng.dma_start(out=out[10 * 128:11 * 128, :],
                        in_=SCR[:, BCM:BCM + 2048]).then_inc(fin, 16)
        G.eng.dma_start(out=out[11 * 128:12 * 128, :],
                        in_=SCR[:, CCM:CCM + 2048]).then_inc(fin, 16)
        G.eng.dma_start(out=out[12 * 128:13 * 128, 0:1024],
                        in_=SC2[:, ACSTM:ACSTM + 1024]).then_inc(fin, 16)
        G.eng.dma_start(out=out[13 * 128:14 * 128, 0:1024],
                        in_=SCR[:, EXPA:EXPA + 1024]).then_inc(fin, 16)
        G.eng.dma_start(out=out[14 * 128:15 * 128, 0:1024],
                        in_=SCR[:, DECAYC:DECAYC + 1024]).then_inc(fin, 16)
        G.eng.dma_start(out=out[15 * 128:16 * 128, 0:640],
                        in_=DEX[:, 0:640]).then_inc(fin, 16)
        G.eng.dma_start(out=out[8 * 128:8 * 128 + 64, 0:2048],
                        in_=SC2[0:64, 0:2048]).then_inc(fin, 16)
        n += 7 * 16
        G.eng.wait_ge(fin, n)
        ctx.close()
        return nc

    XCMA, XTMA, XDTA, XDDA = 0, 8192, 16384, 24576
    BTM = 1024
    CBMA = 1280
    ACSB = 4096
    MPF = 8192
    MB0, MB1 = 12288, 14336
    MXA = 20480

    V.wait(tok_conv)
    V.wait(tok_acscm)
    tok_ydprev = None
    for c in range(NCH):
        V.wait(TE.tok())
        V.wait(tok_ydprev)
        dS.wait(dA.tok())
        dS.waits([V.tok(), TE.tok()])
        t_x = dS.tag(SY.eng.dma_start(
            out=RES[:, XCMA:XCMA + 8192].rearrange("p (xc j) -> p xc j", j=256),
            in_=xconv_d[:, c * 256:(c + 1) * 256].rearrange(
                "(xc pp) j -> pp xc j", pp=128)))
        for xc in range(32):
            bi = p.next_pb()
            TE.waits([t_x, p.pb_tok[bi], ld_id])
            i1 = TE.eng.transpose(p.pb[bi][:, 0:128],
                                  RES[:, XCMA + xc * 256:XCMA + xc * 256 + 128],
                                  CID[:])
            i2 = TE.eng.transpose(p.pb[bi][:, 128:256],
                                  RES[:, XCMA + xc * 256 + 128:XCMA + (xc + 1) * 256],
                                  CID[:])
            t_tr = TE.tag(i2)
            V.wait(t_tr)
            V.eng.tensor_copy(RES[:, XTMA + xc * 128:XTMA + (xc + 1) * 128],
                              p.pb[bi][:, 0:128])
            inst = V.eng.tensor_copy(
                RES[:, XTMA + 4096 + xc * 128:XTMA + 4096 + (xc + 1) * 128],
                p.pb[bi][:, 128:256])
            p.pb_tok[bi] = V.tag(inst)
        V.wait(V.tok())
        for lh in range(2):
            tcb = 2 * c + lh
            dtex = SCR[:, DTTM + tcb * 64:DTTM + (tcb + 1) * 64].unsqueeze(
                2).broadcast_to([128, 64, 64])
            dcex = SCR[:, DECAYC + tcb * 64:DECAYC + (tcb + 1) * 64].unsqueeze(
                2).broadcast_to([128, 64, 64])
            t_1 = V.tag(V.eng.tensor_tensor(
                RES[:, XDTA + lh * 4096:XDTA + (lh + 1) * 4096],
                RES[:, XTMA + lh * 4096:XTMA + (lh + 1) * 4096], dtex, OP.mult))
            V.wait(t_1)
            t_2 = V.tag(V.eng.tensor_tensor(
                RES[:, XDDA + lh * 4096:XDDA + (lh + 1) * 4096],
                RES[:, XDTA + lh * 4096:XDTA + (lh + 1) * 4096], dcex, OP.mult))
            V.wait(t_2)
        bi = p.next_pb()
        TE.waits([V.tok(), p.pb_tok[bi], tok_conv])
        i1 = TE.eng.transpose(p.pb[bi][:, 0:128],
                              SCR[:, BCM + c * 256:BCM + c * 256 + 128], CID[:])
        i2 = TE.eng.transpose(p.pb[bi][:, 128:256],
                              SCR[:, BCM + c * 256 + 128:BCM + (c + 1) * 256], CID[:])
        t_tr = TE.tag(i2)
        V.wait(t_tr)
        inst = V.eng.tensor_copy(SCR[:, BTM:BTM + 256], p.pb[bi][:, 0:256])
        p.pb_tok[bi] = V.tag(inst)
        for sh in range(2):
            bi = p.next_pf()
            pk = p.pf[bi]
            TE.waits([tok_conv, p.pf_tok[bi]])
            t_mm = TE.tag(TE.eng.matmul(
                pk[:, 0:256], SCR[:, BCM + c * 256 + sh * 128:BCM + c * 256 + (sh + 1) * 128],
                SCR[:, CCM + c * 256:CCM + (c + 1) * 256], start=True, stop=True))
            V.waits([t_mm, ld_sm])
            p.pf_tok[bi] = V.tag(V.eng.tensor_tensor(
                SCR[:, CBMA + sh * 256:CBMA + (sh + 1) * 256], pk[:, 0:256],
                CSM[:, sh * 256:(sh + 1) * 256], OP.mult))
        if stage == 635 and c == 1:
            fin = ctx.enter_context(nc.semaphore("fin"))
            G.waits([V.tok(), S.tok(), TE.tok(), dS.tok(), dG.tok()])
            G.eng.dma_start(out=out[0:128, 0:2048],
                            in_=SCR[:, BCM:BCM + 2048]).then_inc(fin, 16)
            G.eng.dma_start(out=out[128:256, 0:2048],
                            in_=SCR[:, CCM:CCM + 2048]).then_inc(fin, 16)
            G.eng.dma_start(out=out[256:384, 0:512],
                            in_=SCR[:, CBMA:CBMA + 512]).then_inc(fin, 16)
            G.eng.dma_start(out=out[384:512, 0:256],
                            in_=SCR[:, BTM:BTM + 256]).then_inc(fin, 16)
            G.eng.wait_ge(fin, 64)
            ctx.close()
            return nc
        for hg in range(8):
            dG.wait(V.tok())
            t_fl = dG.tag(G.eng.dma_start(
                out=SCR[0:1, ACSB:ACSB + 4096].bitcast(F32),
                in_=SC2[hg * 8:hg * 8 + 8, c * 256:(c + 1) * 256]))
            V.wait(t_fl)
            acsb_f = SCR[:, ACSB:ACSB + 4096].bitcast(F32)
            V.eng.tensor_copy(acsb_f[32:33, :], acsb_f[0:1, :])
            V.eng.tensor_copy(acsb_f[64:65, :], acsb_f[0:1, :])
            t_c3 = V.tag(V.eng.tensor_copy(acsb_f[96:97, :], acsb_f[0:1, :]))
            V.wait(t_c3)
            t_bc = V.tag(V.eng.stream_shuffle(acsb_f[:, :], acsb_f[:, :], [0] * 32))
            V.wait(t_bc)
            for sh in range(2):
                tcb = 2 * c + sh
                V.wait(S.tok())
                for h in range(8):
                    t_st2 = V.tag(V.eng.scalar_tensor_tensor(
                        SCR[:, MPF:MPF + 4096].bitcast(F32)[:, h * 256:(h + 1) * 256],
                        acsb_f[:, h * 256:(h + 1) * 256],
                        SC2[:, ACSTM + tcb * 64 + hg * 8 + h:ACSTM + tcb * 64 + hg * 8 + h + 1],
                        DEX[:, 576:832], OP.subtract, OP.min))
                S.waits([t_st2, TE.tok()])
                mb = SCR[:, (MB0 if sh == 0 else MB1):(MB0 if sh == 0 else MB1) + 2048]
                t_ex = S.tag(S.eng.activation(mb,
                                              SCR[:, MPF:MPF + 4096].bitcast(F32),
                                              AF.Exp))
                V.wait(t_ex)
                cbex = SCR[:, CBMA + sh * 256:CBMA + (sh + 1) * 256].unsqueeze(
                    1).broadcast_to([128, 8, 256])
                t_mb = V.tag(V.eng.tensor_tensor(mb, mb, cbex, OP.mult))
                V.wait(t_mb)
            if stage == 634 and c == 1 and hg == 1:
                fin = ctx.enter_context(nc.semaphore("fin"))
                G.waits([V.tok(), S.tok(), TE.tok(), dS.tok(), dG.tok()])
                G.eng.dma_start(out=out[0:128, 0:2048],
                                in_=SCR[0:128, ACSB:ACSB + 4096].bitcast(F32)).then_inc(fin, 16)
                G.eng.dma_start(out=out[128:256, 0:2048],
                                in_=SCR[:, MB0:MB0 + 2048]).then_inc(fin, 16)
                G.eng.dma_start(out=out[256:384, 0:2048],
                                in_=SCR[:, MB1:MB1 + 2048]).then_inc(fin, 16)
                G.eng.dma_start(out=out[384:512, 0:2048],
                                in_=SCR[:, MPF:MPF + 4096].bitcast(F32)).then_inc(fin, 16)
                G.eng.dma_start(out=out[512:640, 0:256],
                                in_=DEX[:, 576:832]).then_inc(fin, 16)
                G.eng.dma_start(out=out[640:768, 0:512],
                                in_=SCR[:, CBMA:CBMA + 512]).then_inc(fin, 16)
                G.eng.dma_start(out=out[768:896, 0:1024],
                                in_=SC2[:, 2048:3072]).then_inc(fin, 16)
                G.eng.wait_ge(fin, 112)
                ctx.close()
                return nc
            if stage == 632 and c == 1 and hg == 0:
                fin = ctx.enter_context(nc.semaphore("fin"))
                G.waits([V.tok(), S.tok(), TE.tok(), dS.tok(), dG.tok()])
                G.eng.dma_start(out=out[0:128, 0:2048],
                                in_=SCR[0:128, ACSB:ACSB + 4096].bitcast(F32)).then_inc(fin, 16)
                G.eng.dma_start(out=out[128:256, 0:2048],
                                in_=SCR[:, MB0:MB0 + 2048]).then_inc(fin, 16)
                G.eng.dma_start(out=out[256:384, 0:2048],
                                in_=SCR[:, MB1:MB1 + 2048]).then_inc(fin, 16)
                G.eng.dma_start(out=out[384:512, 0:2048],
                                in_=RES[:, XDTA:XDTA + 2048]).then_inc(fin, 16)
                G.eng.dma_start(out=out[512:640, 0:2048],
                                in_=RES[:, XTMA:XTMA + 2048]).then_inc(fin, 16)
                G.eng.dma_start(out=out[640:768, 0:2048],
                                in_=SCR[:, MPF:MPF + 4096].bitcast(F32)).then_inc(fin, 16)
                G.eng.wait_ge(fin, 96)
                ctx.close()
                return nc
            if stage == 631 and c == 0 and hg == 0:
                fin = ctx.enter_context(nc.semaphore("fin"))
                G.waits([V.tok(), S.tok(), TE.tok(), dS.tok(), dG.tok()])
                G.eng.dma_start(out=out[0:128, 0:2048],
                                in_=SCR[0:128, ACSB:ACSB + 4096].bitcast(F32)).then_inc(fin, 16)
                G.eng.dma_start(out=out[128:256, 0:2048],
                                in_=SCR[:, MB0:MB0 + 2048]).then_inc(fin, 16)
                G.eng.dma_start(out=out[256:384, 0:2048],
                                in_=SCR[:, MB1:MB1 + 2048]).then_inc(fin, 16)
                G.eng.dma_start(out=out[384:512, 0:512],
                                in_=SCR[:, CBMA:CBMA + 512]).then_inc(fin, 16)
                G.eng.dma_start(out=out[512:640, 0:2048],
                                in_=RES[:, XDTA:XDTA + 2048]).then_inc(fin, 16)
                G.eng.dma_start(out=out[640:768, 0:2048],
                                in_=RES[:, XTMA:XTMA + 2048]).then_inc(fin, 16)
                G.eng.dma_start(out=out[768:896, 0:2048],
                                in_=RES[:, XDTA:XDTA + 2048]).then_inc(fin, 16)
                G.eng.dma_start(out=out[896:1024, 0:1024],
                                in_=SCR[:, DTTM:DTTM + 1024]).then_inc(fin, 16)
                G.eng.dma_start(out=out[1024:1152, 0:2048],
                                in_=RES[:, XDDA:XDDA + 2048]).then_inc(fin, 16)
                G.eng.wait_ge(fin, 144)
                ctx.close()
                return nc
            bi_d0 = p.next_pf()
            bi_d1 = p.next_pf()
            bi_cs = p.next_pf()
            TE.waits([V.tok(), p.pf_tok[bi_d0], p.pf_tok[bi_d1], p.pf_tok[bi_cs]])
            for h in range(8):
                hh = hg * 8 + h
                for lh, bi_d in ((0, bi_d0), (1, bi_d1)):
                    TE.eng.matmul(
                        p.pf[bi_d][:, h * 64:(h + 1) * 64],
                        SCR[:, MB0 + h * 256 + lh * 128:MB0 + h * 256 + (lh + 1) * 128],
                        RES[:, XDTA + hh * 64:XDTA + (hh + 1) * 64],
                        start=True, stop=False)
                    TE.eng.matmul(
                        p.pf[bi_d][:, h * 64:(h + 1) * 64],
                        SCR[:, MB1 + h * 256 + lh * 128:MB1 + h * 256 + (lh + 1) * 128],
                        RES[:, XDTA + 4096 + hh * 64:XDTA + 4096 + (hh + 1) * 64],
                        start=False, stop=True)
                TE.eng.matmul(p.pf[bi_cs][:, h * 64:(h + 1) * 64],
                              SCR[:, BTM:BTM + 128],
                              RES[:, XDDA + hh * 64:XDDA + (hh + 1) * 64],
                              start=True, stop=False)
                i_cs = TE.eng.matmul(p.pf[bi_cs][:, h * 64:(h + 1) * 64],
                                     SCR[:, BTM + 128:BTM + 256],
                                     RES[:, XDDA + 4096 + hh * 64:XDDA + 4096 + (hh + 1) * 64],
                                     start=False, stop=True)
            t_mm = TE.tag(i_cs)
            for lh, bi_d in ((0, bi_d0), (1, bi_d1)):
                dbex = DEX[:, 512 + hg * 8:512 + (hg + 1) * 8].unsqueeze(
                    2).broadcast_to([128, 8, 64])
                ysl = RES[:, XCMA + lh * 4096 + hg * 512:XCMA + lh * 4096 + (hg + 1) * 512]
                V.waits([t_mm, tok_db])
                t_y1 = V.tag(V.eng.tensor_tensor(
                    ysl, RES[:, XTMA + lh * 4096 + hg * 512:XTMA + lh * 4096 + (hg + 1) * 512],
                    dbex, OP.mult))
                V.wait(t_y1)
                t_y2 = V.tag(V.eng.tensor_tensor(ysl, ysl, p.pf[bi_d][:], OP.add))
                p.pf_tok[bi_d] = t_y2
            V.wait(t_mm)
            t_cse = V.tag(V.eng.tensor_copy(DEX[:, 832:1344], p.pf[bi_cs][:]))
            p.pf_tok[bi_cs] = t_cse
            SY.wait(t_cse)
            t_css = dS.tag(SY.eng.dma_start(
                out=cs_d[c * 128:(c + 1) * 128, hg * 512:(hg + 1) * 512],
                in_=DEX[:, 832:1344]))
            V.wait(t_css)
        if stage == 633 and c == 1:
            fin = ctx.enter_context(nc.semaphore("fin"))
            G.waits([V.tok(), S.tok(), TE.tok(), dS.tok(), dG.tok()])
            G.eng.dma_start(out=out[0:128, 0:2048],
                            in_=RES[:, XCMA:XCMA + 2048]).then_inc(fin, 16)
            G.eng.dma_start(out=out[128:256, 0:2048],
                            in_=RES[:, XCMA + 4096:XCMA + 4096 + 2048]).then_inc(fin, 16)
            G.eng.dma_start(out=out[256:384, 0:2048],
                            in_=SCR[:, MB0:MB0 + 2048]).then_inc(fin, 16)
            G.eng.dma_start(out=out[384:512, 0:2048],
                            in_=RES[:, XDTA:XDTA + 2048]).then_inc(fin, 16)
            G.eng.wait_ge(fin, 64)
            ctx.close()
            return nc
        S.wait(V.tok())
        for lh in range(2):
            t_yd = dA.tag(S.eng.dma_start(
                out=yd_d[c * 256 + lh * 128:c * 256 + (lh + 1) * 128, :],
                in_=RES[:, XCMA + lh * 4096:XCMA + (lh + 1) * 4096]))
        tok_ydprev = t_yd
    tok_yd = dS.tok()

    if stage == 63:
        fin = ctx.enter_context(nc.semaphore("fin"))
        G.waits([tok_yd, V.tok()])
        n = 0
        for i in range(8):
            G.eng.dma_start(out=SCR[:, 0:4096],
                            in_=yd_d[i * 256:i * 256 + 128, :]).then_inc(fin, 16)
            n += 16
            G.eng.wait_ge(fin, n)
            G.eng.dma_start(out=out[i * 128:(i + 1) * 128, 0:2048],
                            in_=SCR[:, 0:2048].bitcast(F32)[:, 0:2048] if False
                            else SCR[:, 0:4096].bitcast(F32)[:, 0:2048]).then_inc(fin, 16)
            n += 16
            G.eng.wait_ge(fin, n)
        for i in range(8):
            G.eng.dma_start(out=SCR[:, 0:4096],
                            in_=cs_d[i * 128:(i + 1) * 128, :]).then_inc(fin, 16)
            n += 16
            G.eng.wait_ge(fin, n)
            G.eng.dma_start(out=out[(8 + i) * 128:(9 + i) * 128, 0:2048],
                            in_=SCR[:, 0:4096].bitcast(F32)[:, 0:2048]).then_inc(fin, 16)
            n += 16
            G.eng.wait_ge(fin, n)
        G.eng.wait_ge(fin, n)
        ctx.close()
        return nc

    PREV = 4096
    CSL = 12288
    PREVB = 0
    V.wait(dS.tok())
    t_pz = V.tag(V.eng.memset(SCR[:, PREV:PREV + 8192].bitcast(F32), 0.0))
    V.wait(t_pz)
    prevf = SCR[:, PREV:PREV + 8192].bitcast(F32)
    for c in range(NCH):
        t_cl = dS.tag(SY.eng.dma_start(out=SCR[:, CSL:CSL + 4096],
                                       in_=cs_d[c * 128:(c + 1) * 128, :]))
        dcex = DEX[:, c * 64:(c + 1) * 64].unsqueeze(2).broadcast_to([128, 64, 64])
        t_s1 = V.tag(V.eng.tensor_tensor(prevf, prevf, dcex, OP.mult))
        V.waits([t_s1, t_cl])
        t_s2 = V.tag(V.eng.tensor_tensor(prevf, prevf, SCR[:, CSL:CSL + 4096],
                                         OP.add))
        V.wait(t_s2)
        dS.wait(t_s2)
    t_ms = V.tag(V.eng.tensor_scalar(prevf, prevf, COL[:, C_EM:C_EM + 1], None,
                                     OP.mult))
    SY.wait(t_ms)
    t_sw = dS.tag(SY.eng.dma_start(out=cc_s_in, in_=prevf))
    G.wait(t_sw)
    t_sc = CC.tag(G.eng.collective_compute(
        "AllReduce", OP.add, replica_groups=[[0, 1], [2, 3], [4, 5], [6, 7]],
        ins=[cc_s_in.opt()], outs=[cc_s_out.opt()]))
    G.wait(t_sc)
    t_sl2 = dG.tag(G.eng.dma_start(out=prevf, in_=cc_s_out))
    V.wait(t_sl2)
    t_su = V.tag(V.eng.tensor_scalar(prevf, prevf, COL[:, C_HS:C_HS + 1], None,
                                     OP.mult))
    V.wait(t_su)

    if stage == 64:
        fin = ctx.enter_context(nc.semaphore("fin"))
        G.waits([V.tok(), S.tok(), dS.tok(), dG.tok()])
        G.eng.dma_start(out=out[0:128, 0:2048],
                        in_=prevf[:, 0:2048]).then_inc(fin, 16)
        G.eng.dma_start(out=out[128:256, 0:2048],
                        in_=prevf[:, 2048:4096]).then_inc(fin, 16)
        G.eng.dma_start(out=out[256:384, 0:512],
                        in_=DEX[:, 0:512]).then_inc(fin, 16)
        n = 48
        for i in range(8):
            G.eng.dma_start(out=SCR[:, 0:4096],
                            in_=cs_d[i * 128:(i + 1) * 128, :]).then_inc(fin, 16)
            n += 16
            G.eng.wait_ge(fin, n)
            G.eng.dma_start(out=out[(3 + i) * 128:(4 + i) * 128, 0:2048],
                            in_=SCR[:, 0:4096].bitcast(F32)[:, 0:2048]).then_inc(fin, 16)
            n += 16
            G.eng.wait_ge(fin, n)
        G.eng.wait_ge(fin, n)
        ctx.close()
        return nc

    YGA, YNA, YNFA, SZA, YCA, NWA = 0, 8192, 12288, 16384, 24576, 28672
    t_nw = dG.tag(G.eng.dma_start(out=RES[0:1, NWA:NWA + 4096], in_=norm_w))
    V.wait(t_nw)
    tok_nw = p.bcast_row(RES[:, NWA:NWA + 4096], 4096)

    zslab = [WSlot(WB[:, 0:2048]), WSlot(WB[:, 2048:4096])]
    for sl in zslab:
        sl.war = TE.tok()

    for c in range(NCH):
        V.wait(TE.tok())
        t_pb = V.tag(V.eng.tensor_copy(SCR[:, PREVB:PREVB + 4096], prevf))
        V.wait(t_pb)
        t_cl = dS.tag(SY.eng.dma_start(out=SCR[:, CSL:CSL + 4096],
                                       in_=cs_d[c * 128:(c + 1) * 128, :]))
        dcex = DEX[:, c * 64:(c + 1) * 64].unsqueeze(2).broadcast_to([128, 64, 64])
        t_s1 = V.tag(V.eng.tensor_tensor(prevf, prevf, dcex, OP.mult))
        V.waits([t_s1, t_cl])
        t_s2 = V.tag(V.eng.tensor_tensor(prevf, prevf, SCR[:, CSL:CSL + 4096],
                                         OP.add))
        S.wait(V.tok())
        for zm in range(8):
            bks = [p.next_pf() for _ in range(2)]
            for b in range(2):
                TE.wait(p.pf_tok[bks[b]])
            last = None
            for g in range(4):
                sl = zslab[g % 2]
                dS.wait(sl.war)
                toksl = []
                for rr in range(2):
                    kc0 = 4 * g + 2 * rr
                    r = kc0 // 2
                    toksl.append(p.slot_dma(
                        dS, sl,
                        sl.ap.rearrange("p (kc m) -> p kc m", m=512)[
                            :, 2 * rr:2 * rr + 2, :],
                        RIP[r][:, :, zm * 512:(zm + 1) * 512]))
                TE.waits(toksl + [tok_xn3])
                for j in range(4):
                    kc = 4 * g + j
                    for lh in range(2):
                        tcb = 2 * c + lh
                        inst = TE.eng.matmul(
                            p.pf[bks[lh]][:],
                            XNF[:, kc * T + tcb * 128:kc * T + (tcb + 1) * 128],
                            sl.ap.rearrange("p (kc m) -> p kc m", m=512)[:, j, :],
                            start=(kc == 0), stop=(kc == 15))
                    last = TE.tag(inst)
                sl.war = last
            for lh in range(2):
                S.waits([last, p.pf_tok[bks[lh]]])
                t_sz = S.tag(S.eng.activation(
                    RES[:, SZA + lh * 4096 + zm * 512:SZA + lh * 4096 + (zm + 1) * 512],
                    p.pf[bks[lh]][:], AF.Silu))
                p.pf_tok[bks[lh]] = t_sz
        for lh in range(2):
            tcb = 2 * c + lh
            ysl = RES[:, YCA:YCA + 4096]
            dS.wait(S.tok())
            t_ydl = dS.tag(SY.eng.dma_start(
                out=ysl, in_=yd_d[c * 256 + lh * 128:c * 256 + (lh + 1) * 128, :]))
            for hg in range(8):
                bi = p.next_pf()
                pk = p.pf[bi]
                TE.waits([V.tok(), p.pf_tok[bi]])
                for h in range(8):
                    hh = hg * 8 + h
                    i_yo = TE.eng.matmul(
                        pk[:, h * 64:(h + 1) * 64],
                        SCR[:, CCM + c * 256 + lh * 128:CCM + c * 256 + (lh + 1) * 128],
                        SCR[:, PREVB + hh * 64:PREVB + (hh + 1) * 64],
                        start=True, stop=True)
                t_mm = TE.tag(i_yo)
                eaex = SCR[:, EXPA + tcb * 64 + hg * 8:EXPA + tcb * 64 + (hg + 1) * 8].\
                    unsqueeze(2).broadcast_to([128, 8, 64])
                V.waits([t_mm, t_ydl])
                t_e1 = V.tag(V.eng.tensor_tensor(DEX[:, 1344:1856], pk[:], eaex,
                                                 OP.mult))
                V.wait(t_e1)
                t_e2 = V.tag(V.eng.tensor_tensor(
                    ysl[:, hg * 512:(hg + 1) * 512], ysl[:, hg * 512:(hg + 1) * 512],
                    DEX[:, 1344:1856], OP.add))
                p.pf_tok[bi] = t_e2
            if stage == 65 and c == 0 and lh == 0:
                fin = ctx.enter_context(nc.semaphore("fin"))
                G.waits([V.tok(), S.tok(), TE.tok(), dS.tok(), dG.tok()])
                G.eng.dma_start(out=out[0:128, 0:2048],
                                in_=ysl[:, 0:2048]).then_inc(fin, 16)
                G.eng.dma_start(out=out[128:256, 0:2048],
                                in_=ysl[:, 2048:4096]).then_inc(fin, 16)
                G.eng.dma_start(out=out[256:384, 0:2048],
                                in_=RES[:, SZA:SZA + 2048]).then_inc(fin, 16)
                G.eng.dma_start(out=out[384:512, 0:2048],
                                in_=RES[:, SZA + 2048:SZA + 4096]).then_inc(fin, 16)
                G.eng.dma_start(out=out[512:640, 0:1024],
                                in_=SCR[:, EXPA:EXPA + 1024]).then_inc(fin, 16)
                G.eng.dma_start(out=out[640:768, 0:2048],
                                in_=SCR[:, PREVB:PREVB + 2048]).then_inc(fin, 16)
                G.eng.wait_ge(fin, 96)
                ctx.close()
                return nc
            ygf = RES[:, YGA:YGA + 8192].bitcast(F32)
            V.wait(S.tok())
            V.wait(V.tok())
            t_yg = V.tag(V.eng.tensor_tensor(
                ygf, ysl, RES[:, SZA + lh * 4096:SZA + (lh + 1) * 4096], OP.mult))
            S.wait(t_yg)
            t_sq = S.tag(S.eng.activation(ysl, ygf, AF.Square,
                                          accum_out=COL[:, 10:11]))
            S.wait(t_sq)
            t_l = S.tag(S.eng.activation(COL[:, 11:12], COL[:, 10:11], AF.Ln,
                                         bias=COL[:, C_EPS:C_EPS + 1],
                                         scale=1.0 / DIN))
            S.wait(t_l)
            t_r = S.tag(S.eng.activation(COL[:, 11:12], COL[:, 11:12], AF.Exp,
                                         scale=-0.5))
            V.waits([t_r, tok_nw])
            t_n1 = V.tag(V.eng.tensor_scalar(ygf, ygf, COL[:, 11:12], None,
                                             OP.mult))
            V.wait(t_n1)
            t_n2 = V.tag(V.eng.tensor_tensor(RES[:, YNA:YNA + 4096], ygf,
                                             RES[:, NWA:NWA + 4096], OP.mult))
            for xc in range(16):
                bi = p.next_pb()
                TE.waits([t_n2, p.pb_tok[bi]])
                i1 = TE.eng.transpose(p.pb[bi][:, 0:128],
                                      RES[:, YNA + (2 * xc) * 128:YNA + (2 * xc + 1) * 128],
                                      CID[:])
                i2 = TE.eng.transpose(p.pb[bi][:, 128:256],
                                      RES[:, YNA + (2 * xc + 1) * 128:YNA + (2 * xc + 2) * 128],
                                      CID[:])
                t_tr = TE.tag(i2)
                V.wait(t_tr)
                inst = V.eng.tensor_copy(RES[:, YNFA + xc * 256:YNFA + (xc + 1) * 256],
                                         p.pb[bi][:, 0:256])
                p.pb_tok[bi] = V.tag(inst)
            tok_ynf = V.tok()
            dS.wait(tok_ynf)
            t_x2l = dS.tag(SY.eng.dma_start(
                out=RES[:, YNA:YNA + 2048],
                in_=x2_d[c * 256 + lh * 128:c * 256 + (lh + 1) * 128, :]))
            for ot in range(4):
                bi = p.next_pf()
                pk = p.pf[bi]
                TE.waits([tok_ynf, p.pf_tok[bi]])
                last = None
                for g in range(8):
                    sl = zslab[g % 2]
                    dS.wait(sl.war)
                    r = g
                    t_w = p.slot_dma(
                        dS, sl, sl.ap.rearrange("p (kc m) -> p kc m", m=512),
                        ROP[r][:, :, ot * 512:(ot + 1) * 512])
                    TE.wait(t_w)
                    for j in range(4):
                        ic = 4 * g + j
                        inst = TE.eng.matmul(
                            pk[:], RES[:, YNFA + ic * 128:YNFA + (ic + 1) * 128],
                            sl.ap.rearrange("p (kc m) -> p kc m", m=512)[:, j, :],
                            start=(ic == 0), stop=(ic == 31))
                    last = TE.tag(inst)
                    sl.war = last
                V.waits([last, t_x2l])
                t_ev = V.tag(V.eng.tensor_tensor(
                    RES[:, YNA + ot * 512:YNA + (ot + 1) * 512],
                    RES[:, YNA + ot * 512:YNA + (ot + 1) * 512], pk[:], OP.add))
                p.pf_tok[bi] = t_ev
            SY.wait(V.tok())
            t_x3 = dS.tag(SY.eng.dma_start(
                out=x3_d[c * 256 + lh * 128:c * 256 + (lh + 1) * 128, :],
                in_=RES[:, YNA:YNA + 2048]))
            V.wait(t_x3)
    tok_x3 = dS.tok()

    if stage == 7:
        fin = ctx.enter_context(nc.semaphore("fin"))
        G.wait(tok_x3)
        n = 0
        for tc in range(NTC):
            G.eng.dma_start(out=SCR[:, 0:D],
                            in_=x3_d[tc * 128:(tc + 1) * 128, :]).then_inc(fin, 16)
            n += 16
            G.eng.wait_ge(fin, n)
            G.eng.dma_start(out=out[tc * 128:(tc + 1) * 128, :],
                            in_=SCR[:, 0:D]).then_inc(fin, 16)
            n += 16
            G.eng.wait_ge(fin, n)
        ctx.close()
        return nc

    dS.waits([V.tok(), S.tok(), TE.tok()])
    S.wait(V.tok())
    V.wait(S.tok())
    C_RS8, C_SWS = 33, 34
    t_s8 = dS.tag(SY.eng.dma_start(out=COL[0:1, C_RS8:C_RS8 + 2], in_=scal8))
    V.wait(t_s8)
    V.eng.tensor_copy(COL[32:33, C_RS8:C_RS8 + 2], COL[0:1, C_RS8:C_RS8 + 2])
    V.eng.tensor_copy(COL[64:65, C_RS8:C_RS8 + 2], COL[0:1, C_RS8:C_RS8 + 2])
    t_cc3 = V.tag(V.eng.tensor_copy(COL[96:97, C_RS8:C_RS8 + 2],
                                    COL[0:1, C_RS8:C_RS8 + 2]))
    V.wait(t_cc3)
    t_sh8 = V.tag(V.eng.stream_shuffle(COL[:, C_RS8:C_RS8 + 2],
                                       COL[:, C_RS8:C_RS8 + 2], [0] * 32))
    V.wait(t_sh8)
    t_sws = V.tag(V.eng.tensor_tensor(COL[:, C_SWS:C_SWS + 1],
                                      COL[:, C_RS8:C_RS8 + 1],
                                      COL[:, C_SWS:C_SWS + 1], OP.mult))
    V.wait(t_sws)
    t_rs8 = V.tag(V.eng.reciprocal(COL[:, C_RS8:C_RS8 + 1], COL[:, C_RS8:C_RS8 + 1]))
    V.wait(t_rs8)

    V.wait(TE.tok())
    for tc in range(NTC):
        t_ld = dS.tag(SY.eng.dma_start(out=SCR[:, 0:2048],
                                       in_=x3_d[tc * 128:(tc + 1) * 128, :]))
        V.wait(t_ld)
        t_r = ln_cols(SCR[:, 0:2048], C_M1, C_R1)
        V.wait(t_r)
        t_x = V.tag(V.eng.tensor_scalar(sf32(4096, 2048), SCR[:, 0:2048],
                                        COL[:, C_M1:C_M1 + 1], COL[:, C_R1:C_R1 + 1],
                                        OP.subtract, OP.mult))
        V.wait(t_x)
        t_u = V.tag(V.eng.tensor_scalar(sf32(8192, 2048), sf32(4096, 2048),
                                        COL[:, C_RS8:C_RS8 + 1], 240.0,
                                        OP.mult, OP.min))
        V.wait(t_u)
        t_q8 = V.tag(V.eng.tensor_scalar(SCR[:, 12288:13312].bitcast(FP8),
                                         sf32(8192, 2048), -240.0, None, OP.max))
        V.wait(t_q8)
        t_dec = V.tag(V.eng.tensor_copy(SCR[:, 16384:18432],
                                        SCR[:, 12288:13312].bitcast(FP8)))
        for half in range(2):
            bi = p.next_pb()
            TE.waits([t_dec, p.pb_tok[bi]])
            for j in range(8):
                kc = half * 8 + j
                inst = TE.eng.transpose(p.pb[bi][:, j * 128:(j + 1) * 128],
                                        SCR[:, 16384 + kc * 128:16384 + (kc + 1) * 128],
                                        CID[:])
            t_tr = TE.tag(inst)
            V.wait(t_tr)
            for j in range(8):
                kc = half * 8 + j
                inst = V.eng.tensor_copy(
                    XNF[:, kc * T + tc * 128:kc * T + (tc + 1) * 128],
                    p.pb[bi][:, j * 128:(j + 1) * 128])
            p.pb_tok[bi] = V.tag(inst)
        dS.wait(V.tok())
    tok_xq8 = V.tok()

    w8slab = [WSlot(WB[:, 0:2048]), WSlot(WB[:, 2048:4096])]
    for sl in w8slab:
        sl.war = TE.tok()
    for tt in range(4):
        S.wait(V.tok())
        x3toks = []
        for b in range(4):
            tc = tt * 4 + b
            x3toks.append(dA.tag(S.eng.dma_start(
                out=SCR[:, b * 2048:(b + 1) * 2048],
                in_=x3_d[tc * 128:(tc + 1) * 128, :])))
        for ot in range(4):
            bks = [p.next_pf() for _ in range(4)]
            for b in range(4):
                TE.wait(p.pf_tok[bks[b]])
            last = None
            for g in range(4):
                sl = w8slab[g % 2]
                dS.wait(sl.war)
                toks = []
                for rr in range(2):
                    kc0 = 4 * g + 2 * rr
                    r = kc0 // 2
                    toks.append(p.slot_dma(
                        dS, sl,
                        sl.ap.rearrange("p (kc m) -> p kc m", m=512)[
                            :, 2 * rr:2 * rr + 2, :],
                        RW8[r][:, :, ot * 512:(ot + 1) * 512]))
                TE.waits(toks + [tok_xq8])
                for j in range(4):
                    ic = 4 * g + j
                    for b in range(4):
                        tc = tt * 4 + b
                        inst = TE.eng.matmul(
                            p.pf[bks[b]][:],
                            XNF[:, ic * T + tc * 128:ic * T + (tc + 1) * 128],
                            sl.ap.rearrange("p (kc m) -> p kc m", m=512)[:, j, :],
                            start=(ic == 0), stop=(ic == 15))
                    last = TE.tag(inst)
                sl.war = last
            for b in range(4):
                tc = tt * 4 + b
                V.waits([last, x3toks[b], p.pf_tok[bks[b]]])
                t_e = V.tag(V.eng.scalar_tensor_tensor(
                    SC2[:, 0:512],
                    p.pf[bks[b]][:], COL[:, C_SWS:C_SWS + 1],
                    SCR[:, b * 2048 + ot * 512:b * 2048 + (ot + 1) * 512],
                    OP.mult, OP.add))
                p.pf_tok[bks[b]] = t_e
                SY.wait(t_e)
                t_st = dS.tag(SY.eng.dma_start(
                    out=out[tc * 128:(tc + 1) * 128, ot * 512:(ot + 1) * 512],
                    in_=SC2[:, 0:512]))
                V.wait(t_st)
    SY.wait(dS.tok())
    G.wait(dS.tok())

    ctx.close()
    return nc



def _rot_tables(tok0):
    rot = D // 2
    inv = 1.0 / (10000.0 ** (np.arange(0, rot, 2, dtype=np.float32) / rot))
    fr = (np.arange(tok0, tok0 + T, dtype=np.float32)[:, None] * inv[None, :])
    fr = np.repeat(fr, 2, axis=-1)
    return np.cos(fr).astype(BFnp), np.sin(fr).astype(BFnp)


def _consts():
    idn = np.eye(128, dtype=np.float32)
    tri = np.zeros((128, 3 * 128), np.float32)
    r = np.arange(128)
    tri[:, 0:128] = (r[:, None] <= r[None, :])
    tri[:, 128:256] = (r[:, None] > r[None, :])
    tri[:, 256:384] = 1.0
    smask = (np.arange(256)[:, None] <= np.arange(256)[None, :]).astype(np.float32)
    return idn.astype(BFnp), tri.astype(BFnp), smask.astype(BFnp)


_NC_CACHE = {}


def kernel(L0_attn_X, L0_attn_W_in, L0_attn_b_in, L0_attn_W_out, L0_attn_b_out,
           L0_attn_gamma, L0_attn_beta, L0_ffn_w_gate_up, L0_ffn_w_down,
           L1_attn_in_proj_w, L1_attn_conv_w, L1_attn_conv_b, L1_attn_dt_bias,
           L1_attn_A_log, L1_attn_D, L1_attn_norm_w, L1_attn_out_proj_w,
           L1_ffn_weight, L1_ffn_weight_scale, L1_ffn_input_scale, _stage=99):
    idn, tri, smask = _consts()
    w8v = np.ascontiguousarray(L1_ffn_weight).view(ml_dtypes.float8_e4m3)
    in_maps = []
    for c in range(8):
        b, h = c // 2, c % 2
        cosb, sinb = _rot_tables(h * T)
        m = {
            "x_in": np.ascontiguousarray(np.asarray(L0_attn_X)[b, h * T:(h + 1) * T, :]),
            "w_in": np.ascontiguousarray(L0_attn_W_in),
            "w_outw": np.ascontiguousarray(L0_attn_W_out),
            "b_in": np.ascontiguousarray(L0_attn_b_in)[None, :],
            "b_out": np.ascontiguousarray(L0_attn_b_out)[None, :],
            "gamma": np.ascontiguousarray(L0_attn_gamma)[None, :],
            "beta": np.ascontiguousarray(L0_attn_beta)[None, :],
            "cosd": cosb, "sind": sinb,
            "idn": idn, "trilm": tri, "smaskd": smask,
            "w1_rows": np.ascontiguousarray(L0_ffn_w_gate_up[c * W1R:(c + 1) * W1R, :]),
            "w2_rows": np.ascontiguousarray(L0_ffn_w_down[c * W2R:(c + 1) * W2R, :]),
            "ip_cols": np.ascontiguousarray(L1_attn_in_proj_w[:, c * IPC:(c + 1) * IPC]),
            "op_cols": np.ascontiguousarray(L1_attn_out_proj_w[:, c * OPC:(c + 1) * OPC]),
            "w8_cols": np.ascontiguousarray(w8v[:, c * W8C:(c + 1) * W8C]),
            "conv_w": np.ascontiguousarray(L1_attn_conv_w),
            "conv_b": np.ascontiguousarray(L1_attn_conv_b)[None, :],
            "dt_bias": np.ascontiguousarray(L1_attn_dt_bias)[None, :],
            "a_log": np.ascontiguousarray(L1_attn_A_log)[None, :],
            "d_vec": np.ascontiguousarray(L1_attn_D)[None, :],
            "norm_w": np.ascontiguousarray(L1_attn_norm_w)[None, :],
            "scal8": np.array([[float(np.asarray(L1_ffn_input_scale,
                                                 dtype=np.float32)),
                                float(np.asarray(L1_ffn_weight_scale))]], np.float32),
            "masks": np.array([[1.0 - h, float(h)]], np.float32),
        }
        in_maps.append(m)
    nc = build_program(_stage)
    res = run_bass_kernel_spmd(nc, in_maps, core_ids=list(range(8)))
    outp = np.zeros((4, 4096, D), np.float32)
    for c in range(8):
        b, h = c // 2, c % 2
        outp[b, h * T:(h + 1) * T, :] = res.results[c]["out"]
    return outp
